# version 10
# baseline (speedup 1.0000x reference)
import contextlib
import numpy as np
import ml_dtypes
import concourse.bass as bass
import concourse.mybir as mybir
from concourse.ap import AP
from concourse.bass_core_rust import IndirectOffsetOnAxis
from concourse.bass_utils import run_bass_kernel_spmd

F32 = mybir.dt.float32
BF16 = mybir.dt.bfloat16
I32 = mybir.dt.int32
AF = mybir.ActivationFunctionType
ALU = mybir.AluOpType
AX = mybir.AxisListType.X

P = 128
D = 1024
T = 2048
NTT = T // P
NCH = D // P
H = 16
DH = 64
DFF = 3072
NEXP = 8
ECAP = 640
NSLOT = NEXP * ECAP
CAPACITY = 4096
SKW = 2176
RPN = 4096
QFW = 2064

_PHASES = 4
_TAPS = ()
_C_STAGES = 3
_C_HEADS = H
_C_SC = 3


def _host_consts():
    ident = np.eye(P, dtype=np.float64)
    lstrict = (np.arange(P)[:, None] < np.arange(P)[None, :]).astype(np.float64)
    tokids = (np.arange(NTT)[None, :] * P + np.arange(P)[:, None]).astype(np.int32)
    inv_freq = 1.0 / (10000.0 ** (np.arange(0, D, 2, dtype=np.float64) / D))
    pos = np.arange(-(T - 1), T, dtype=np.float64)
    ang = pos[:, None] * inv_freq[None, :]
    emb = np.zeros((2 * T - 1, D), np.float64)
    emb[:, 0::2] = np.sin(ang)
    emb[:, 1::2] = np.cos(ang)
    remb_t = np.zeros((D, RPN), np.float32)
    remb_t[:, : 2 * T - 1] = emb.T.astype(np.float32)
    return {
        "c_identb": ident.astype(ml_dtypes.bfloat16),
        "c_identf": ident.astype(np.float32),
        "c_lstrict": lstrict.astype(np.float32),
        "c_ones128": np.ones((P, P), np.float32),
        "c_tokids": tokids,
        "c_remb_t": remb_t,
    }


class K:

    def __init__(self, nc, stack):
        self.nc = nc
        self.stack = stack
        self.sem = {}
        self.cnt = {}

    def mksem(self, *names):
        for n in names:
            self.sem[n] = self.stack.enter_context(self.nc.semaphore(n))
            self.cnt[n] = 0

    def inc(self, instr, name, n=1):
        instr.then_inc(self.sem[name], n)
        self.cnt[name] += n
        return instr

    def dinc(self, instr, name):
        return self.inc(instr, name, 16)

    def wait(self, eng, name):
        if self.cnt[name] > 0:
            eng.wait_ge(self.sem[name], self.cnt[name])

    def wait_at(self, eng, name, val):
        if val > 0:
            eng.wait_ge(self.sem[name], val)


def build_program(phases=None):
    if phases is None:
        phases = _PHASES
    nc = bass.Bass("TRN2", target_bir_lowering=False,
                   detect_race_conditions=False)

    def din(name, shape, dtype=F32):
        return nc.dram_tensor(name, shape, dtype, kind="ExternalInput")

    def dint(name, shape, dtype=F32):
        return nc.dram_tensor(name, shape, dtype, kind="Internal")

    x_in = din("x_in", [T, D])
    w_in = din("w_in", [D, DFF])
    lam_log = din("lam_log", [P, NCH])
    w_norm_c = din("w_norm_c", [P, NCH])
    w_out = din("w_out", [D, D])
    wq_d = din("wq", [D, D])
    wk_d = din("wk", [D, D])
    wv_d = din("wv", [D, D])
    wr_d = din("wr", [D, D])
    u_bias_t = din("u_bias_t", [DH, H])
    v_bias_t = din("v_bias_t", [DH, H])
    gate_w = din("gate_w", [D, NEXP])
    c_fc = din("c_fc", [NEXP, D, DFF])
    c_proj = din("c_proj", [NEXP, DFF // 2, D])
    prev_mask = din("prev_mask", [NEXP, 1])
    c_identb = din("c_identb", [P, P], BF16)
    c_identf = din("c_identf", [P, P])
    c_lstrict = din("c_lstrict", [P, P])
    c_ones128 = din("c_ones128", [P, P])
    c_tokids = din("c_tokids", [P, NTT], I32)
    c_remb_t = din("c_remb_t", [D, RPN])

    out_d = nc.dram_tensor("out", [T, D], F32, kind="ExternalOutput")

    xtok_h = dint("xtok_h", [T, D], BF16)
    inv_h = dint("inv_h", [NSLOT, 1], I32)
    slotout_h = dint("slotout_h", [NSLOT, D], F32)
    skw_h = dint("skw_h", [H * NTT * P, SKW], BF16)
    rp_hd = dint("rp_h", [H * (DH + 1), RPN], BF16)
    vt_hd = dint("vt_h", [T, H * (DH + 1)], BF16)
    cc_in = dint("cc_in", [1, NEXP])
    cc_out = dint("cc_out", [NEXP, NEXP])
    base_h = dint("base_h", [1, NEXP])
    mh_hd = dint("mh_h", [1, H])

    with contextlib.ExitStack() as ST:
        k = K(nc, ST)
        k.mksem("ld", "wld", "wld2", "mm", "ve", "se", "st", "pk", "sk",
                "cc", "ev2", "xk", "so", "tr", "pz", "cc2", "hl")
        block = ST.enter_context(nc.Block())

        sb = lambda n, s, d=F32: ST.enter_context(nc.sbuf_tensor(n, s, d))

        X = sb("X", [P, NTT, D])
        IDB = sb("IDB", [P, P], BF16)
        IDF = sb("IDF", [P, P])
        LST = sb("LST", [P, P])
        ON128 = sb("ON128", [P, P])
        TOK = sb("TOK", [P, NTT], I32)
        LAM = sb("LAM", [P, NCH])
        LAMH = sb("LAMH", [P, NCH])
        LAMLOG = sb("LAMLOG", [P, NCH])
        WNC = sb("WNC", [P, NCH])
        UBT = sb("UBT", [DH, H])
        VBT = sb("VBT", [DH, H])
        PMASK = sb("PMASK", [NEXP, 1])
        GW = sb("GW", [P, NCH, NEXP], BF16)
        LNS = sb("LNS", [P, NTT, 12])
        EPS5 = sb("EPS5", [P, 1])
        EPS11 = sb("EPS11", [P, 1])
        SQ = sb("SQ", [P, 2, D])

        @block.sync
        def _(sync):
            src = AP(x_in, 0, [[D, P], [P * D, NTT], [1, D]])
            k.dinc(sync.dma_start(out=X[:, :, :], in_=src), "ld")
            k.dinc(sync.dma_start(out=IDB[:, :], in_=c_identb[:, :]), "ld")
            k.dinc(sync.dma_start(out=IDF[:, :], in_=c_identf[:, :]), "ld")
            k.dinc(sync.dma_start(out=LST[:, :], in_=c_lstrict[:, :]), "ld")
            k.dinc(sync.dma_start(out=ON128[:, :], in_=c_ones128[:, :]), "ld")
            k.dinc(sync.dma_start(out=TOK[:, :], in_=c_tokids[:, :]), "ld")
            k.dinc(sync.dma_start(out=WNC[:, :], in_=w_norm_c[:, :]), "ld")
            k.dinc(sync.dma_start(out=UBT[:, :], in_=u_bias_t[:, :]), "ld")
            k.dinc(sync.dma_start(out=VBT[:, :], in_=v_bias_t[:, :]), "ld")
            k.dinc(sync.dma_start(out=PMASK[:, :], in_=prev_mask[:, :]), "ld")
            k.dinc(sync.dma_start(out=LAMLOG[:, :], in_=lam_log[:, :]), "ld")

        @block.gpsimd
        def _(gpsimd):
            src = AP(gate_w, 0, [[NEXP, P], [P * NEXP, NCH], [1, NEXP]])
            k.dinc(gpsimd.dma_start(out=GW[:, :, :], in_=src), "wld")

        @block.vector
        def _(vector):
            vector.memset(EPS5[:, :], 1e-5)
            k.inc(vector.memset(EPS11[:, :], 1e-11), "ve")

        @block.scalar
        def _(scalar):
            k.wait(scalar, "ld")
            k.inc(scalar.activation(LAM[:, :], LAMLOG[:, :], AF.Sigmoid), "se")

        @block.vector
        def _(vector):
            k.wait(vector, "se")
            k.inc(vector.tensor_scalar(LAMH[:, :], LAM[:, :], 0.5, None,
                                       ALU.mult), "ve")

        def emit_ln(XLT, rms):
            se0 = k.cnt["se"]
            ve0 = k.cnt["ve"]

            @block.scalar
            def _(scalar):
                k.wait(scalar, "ld")
                k.wait(scalar, "ve")
                k.wait(scalar, "mm")
                for tt in range(NTT):
                    if tt >= 2:
                        k.wait_at(scalar, "ve", ve0 + tt - 1)
                    k.inc(scalar.activation(SQ[:, tt % 2, :], X[:, tt, :],
                                            AF.Square), "se")

            @block.vector
            def _(vector):
                k.wait(vector, "ld")
                for tt in range(NTT):
                    k.wait_at(vector, "se", se0 + tt + 1)
                    vector.reduce_sum(LNS[:, tt, 5:6], X[:, tt, :], AX)
                    k.inc(vector.reduce_sum(LNS[:, tt, 6:7], SQ[:, tt % 2, :],
                                            AX), "ve")
                for tt in range(NTT):
                    vector.tensor_scalar(LNS[:, tt, 0:1], LNS[:, tt, 5:6],
                                         -1.0 / D, None, ALU.mult)
                for tt in range(NTT):
                    vector.tensor_scalar(LNS[:, tt, 7:8], LNS[:, tt, 5:6],
                                         1.0 / D, None, ALU.mult)
                for tt in range(NTT):
                    vector.tensor_tensor(LNS[:, tt, 8:9], LNS[:, tt, 7:8],
                                         LNS[:, tt, 7:8], ALU.mult)
                for tt in range(NTT):
                    vector.tensor_scalar(LNS[:, tt, 9:10], LNS[:, tt, 6:7],
                                         1.0 / D, None, ALU.mult)
                for tt in range(NTT):
                    k.inc(vector.tensor_tensor(LNS[:, tt, 10:11],
                                               LNS[:, tt, 9:10],
                                               LNS[:, tt, 8:9], ALU.subtract),
                          "ve")

            @block.scalar
            def _(scalar):
                k.wait(scalar, "ve")
                for tt in range(NTT):
                    k.inc(scalar.activation(LNS[:, tt, 11:12], LNS[:, tt, 10:11],
                                            AF.Sqrt, bias=EPS5[:, :]), "se")
                if rms:
                    for tt in range(NTT):
                        k.inc(scalar.activation(LNS[:, tt, 5:6],
                                                LNS[:, tt, 10:11], AF.Sqrt,
                                                bias=EPS11[:, :],
                                                scale=1.000001), "se")

            @block.vector
            def _(vector):
                k.wait(vector, "se")
                for tt in range(NTT):
                    k.inc(vector.reciprocal(LNS[:, tt, 1:2], LNS[:, tt, 11:12]),
                          "ve")
                if rms:
                    for tt in range(NTT):
                        k.inc(vector.reciprocal(LNS[:, tt, 2:3], LNS[:, tt, 5:6]),
                              "ve")
                for tt in range(NTT):
                    vector.tensor_scalar(LNS[:, tt, 3:4], LNS[:, tt, 1:2], 1.0,
                                         None, ALU.add)
                for tt in range(NTT):
                    k.inc(vector.tensor_tensor(LNS[:, tt, 4:5], LNS[:, tt, 0:1],
                                               LNS[:, tt, 1:2], ALU.mult), "ve")
                sc = 2 if rms else 1
                for tt in range(NTT):
                    k.inc(vector.tensor_scalar(XLT[:, tt, :], X[:, tt, :],
                                               LNS[:, tt, 0:1],
                                               LNS[:, tt, sc:sc + 1],
                                               ALU.add, ALU.mult), "ve")

        def tap(name, buf):
            if name not in _TAPS:
                return
            dt_ = nc.dram_tensor("dbg_" + name,
                                 [buf.shape[0], int(np.prod(buf.shape[1:]))],
                                 buf.ap().dtype, kind="ExternalOutput")

            @block.sync
            def _(sync):
                k.wait(sync, "ve")
                k.wait(sync, "se")
                k.wait(sync, "mm")
                src = buf.ap().rearrange("p ... -> p (...)")
                k.dinc(sync.dma_start(out=dt_[:, :], in_=src), "st")

        def emit_transposes(XLT, XF, PSA, scale_wnc):
            mm0 = k.cnt["mm"]
            ev0 = k.cnt["ev2"]
            ntile = NTT * NCH

            @block.tensor
            def _(tensor):
                k.wait(tensor, "ve")
                for n in range(ntile):
                    tt, c = divmod(n, NCH)
                    if n >= 16:
                        k.wait_at(tensor, "ev2", ev0 + (n - 16) // 8 + 1)
                    slot = n % 16
                    k.inc(tensor.transpose(
                        PSA[:, slot * P:(slot + 1) * P],
                        XLT[:, tt, c * P:(c + 1) * P], IDB[:, :]), "mm")

            @block.vector
            def _(vector):
                for g in range(ntile // 8):
                    k.wait_at(vector, "mm", mm0 + (g + 1) * 8)
                    for j in range(8):
                        n = g * 8 + j
                        tt, c = divmod(n, NCH)
                        slot = n % 16
                        src = PSA[:, slot * P:(slot + 1) * P]
                        dst = XF[:, c, tt * P:(tt + 1) * P]
                        if scale_wnc:
                            vector.tensor_scalar(dst, src, WNC[:, c:c + 1],
                                                 None, ALU.mult)
                        else:
                            vector.tensor_copy(dst, src)
                    k.inc(vector.engine_nop(), "ev2")

        with contextlib.ExitStack() as PA:
            asb = lambda n, s, d=F32: PA.enter_context(nc.sbuf_tensor(n, s, d))

            XF = asb("aXF", [P, NCH, T], BF16)
            YB = asb("aYB", [P, NCH, T], BF16)

            with contextlib.ExitStack() as PT:
                XLT = PT.enter_context(nc.sbuf_tensor("aXLT", [P, NTT, D], BF16))
                PSA = PT.enter_context(nc.psum_tensor("aPSA", [P, 2048], BF16))
                emit_ln(XLT, rms=True)
                tap("xlt", XLT)
                emit_transposes(XLT, XF, PSA, scale_wnc=True)
                tap("xf", XF)
            mm_xlt_done = k.cnt["mm"]

            WIT = asb("aWIT", [P, 2, NCH, 3, P], BF16)
            WOB = asb("aWOB", [P, NCH, D], BF16)
            AG = asb("aAG", [P, T])
            UG = asb("aUG", [P, T])
            TA2 = asb("aTA2", [P, 2, 512])
            TR = asb("aTR", [P, 2, 512])
            TV = asb("aTV", [P, 2, 512])
            TZV = asb("aTZV", [P, 2, 512])
            T1 = asb("aT1", [P, 2, 512])

            with contextlib.ExitStack() as PP:
                PSP = PP.enter_context(nc.psum_tensor("aPSP", [P, 6, 512], F32))

                @block.gpsimd
                def _(gpsimd):
                    k.wait_at(gpsimd, "mm", mm_xlt_done)
                    if _TAPS:
                        k.wait(gpsimd, "st")
                    src = AP(w_out, 0, [[D, P], [P * D, NCH], [1, D]])
                    k.dinc(gpsimd.dma_start(out=WOB[:, :, :], in_=src), "wld")

                mm_proj = []
                SE_BASE = k.cnt["se"]
                VE_BASE = k.cnt["ve"]
                se_pred = lambda idx: SE_BASE + idx + 1
                ve_pred = lambda idx: VE_BASE + (idx // 4) * 5 + (idx % 4) + 1
                scan_pred = lambda g: VE_BASE + (g + 1) * 5

                for g in range(NCH):
                    @block.gpsimd
                    def _(gpsimd, g=g):
                        if g >= 2:
                            k.wait_at(gpsimd, "mm", mm_proj[(g - 2) * 4 + 3])
                        for kc in range(NCH):
                            src = AP(w_in, kc * P * DFF + g * P,
                                     [[DFF, P], [D, 3], [1, P]])
                            k.dinc(gpsimd.dma_start(
                                out=WIT[:, g % 2, kc, :, :], in_=src), "wld2")

                    @block.tensor
                    def _(tensor, g=g):
                        k.wait_at(tensor, "wld2", 16 * NCH * (g + 1))
                        if g == 0:
                            k.wait(tensor, "ev2")
                        for f in range(4):
                            par = f % 2
                            if g * 4 + f >= 2:
                                k.wait_at(tensor, "se",
                                          se_pred(g * 4 + f - 2))
                            for kc in range(NCH):
                                for s in range(3):
                                    k.inc(tensor.matmul(
                                        PSP[:, par * 3 + s, :],
                                        WIT[:, g % 2, kc, s, :],
                                        XF[:, kc, f * 512:(f + 1) * 512],
                                        start=(kc == 0), stop=(kc == NCH - 1)),
                                        "mm")
                            mm_proj.append(k.cnt["mm"])

                    for f in range(4):
                        idx = g * 4 + f

                        @block.scalar
                        def _(scalar, g=g, f=f, idx=idx):
                            par = f % 2
                            k.wait_at(scalar, "mm", mm_proj[idx])
                            if f == 0 and g >= 1:
                                k.wait_at(scalar, "ve", scan_pred(g - 1))
                            if idx >= 2:
                                k.wait_at(scalar, "ve", ve_pred(idx - 2))
                            fs = slice(f * 512, (f + 1) * 512)
                            scalar.activation(AG[:, fs],
                                              PSP[:, par * 3 + 0, :],
                                              AF.Sigmoid)
                            scalar.activation(TA2[:, par, :],
                                              PSP[:, par * 3 + 0, :],
                                              AF.Sigmoid, scale=-1.0)
                            scalar.activation(TR[:, par, :],
                                              PSP[:, par * 3 + 1, :],
                                              AF.Sigmoid)
                            k.inc(scalar.activation(TV[:, par, :],
                                                    PSP[:, par * 3 + 2, :],
                                                    AF.Gelu), "se")
                            assert k.cnt["se"] == se_pred(idx)

                        @block.vector
                        def _(vector, g=g, f=f, idx=idx):
                            par = f % 2
                            k.wait_at(vector, "se", se_pred(idx))
                            for h2 in range(2):
                                hs = slice(f * 512 + h2 * 256,
                                           f * 512 + h2 * 256 + 256)
                                ph = slice(h2 * 256, h2 * 256 + 256)
                                vector.tensor_tensor(UG[:, hs], TR[:, par, ph],
                                                     TV[:, par, ph], ALU.mult)
                            for h2 in range(2):
                                hs = slice(f * 512 + h2 * 256,
                                           f * 512 + h2 * 256 + 256)
                                vector.tensor_scalar(UG[:, hs], UG[:, hs],
                                                     LAM[:, g:g + 1], None,
                                                     ALU.mult)
                            for h2 in range(2):
                                hs = slice(f * 512 + h2 * 256,
                                           f * 512 + h2 * 256 + 256)
                                ph = slice(h2 * 256, h2 * 256 + 256)
                                ins = vector.tensor_tensor(
                                    UG[:, hs], UG[:, hs], TA2[:, par, ph],
                                    ALU.mult)
                            k.inc(ins, "ve")
                            assert k.cnt["ve"] == ve_pred(idx)

                    @block.vector
                    def _(vector, g=g):
                        k.wait_at(vector, "ve", ve_pred(g * 4 + 3))
                        k.wait_at(vector, "se", se_pred(g * 4 + 3))
                        k.inc(vector.tensor_tensor_scan(
                            YB[:, g, :], AG[:, :], UG[:, :], 0.0,
                            ALU.mult, ALU.add), "ve")
                        assert k.cnt["ve"] == scan_pred(g)

            tap("ag", AG)
            tap("ug", UG)
            tap("yb", YB)

            with contextlib.ExitStack() as PW:
                PWO = PW.enter_context(nc.psum_tensor("aPWO", [P, 2, D], F32))
                mm_wo = []
                ve_wo = []

                for tt in range(NTT):
                    @block.tensor
                    def _(tensor, tt=tt):
                        if tt == 0:
                            k.wait(tensor, "ve")
                            k.wait(tensor, "se")
                            k.wait_at(tensor, "wld", 32)
                        if tt >= 2:
                            k.wait_at(tensor, "ve", ve_wo[tt - 2])
                        par = tt % 2
                        for g in range(NCH):
                            for nh in range(2):
                                k.inc(tensor.matmul(
                                    PWO[:, par, nh * 512:(nh + 1) * 512],
                                    YB[:, g, tt * P:(tt + 1) * P],
                                    WOB[:, g, nh * 512:(nh + 1) * 512],
                                    start=(g == 0), stop=(g == NCH - 1)), "mm")
                        mm_wo.append(k.cnt["mm"])

                    @block.vector
                    def _(vector, tt=tt):
                        k.wait_at(vector, "mm", mm_wo[tt])
                        par = tt % 2
                        for nh in range(2):
                            vector.scalar_tensor_tensor(
                                T1[:, nh, :], X[:, tt, nh * 512:(nh + 1) * 512],
                                LNS[:, tt, 3:4],
                                PWO[:, par, nh * 512:(nh + 1) * 512],
                                ALU.mult, ALU.add)
                        for nh in range(2):
                            hs = slice(nh * 512, (nh + 1) * 512)
                            k.inc(vector.tensor_scalar(X[:, tt, hs],
                                                       T1[:, nh, :],
                                                       LNS[:, tt, 4:5], None,
                                                       ALU.add), "ve")
                        ve_wo.append(k.cnt["ve"])

        def emit_moe(ci):
            ve_pfxread = []
            mm_pfx = []
            with contextlib.ExitStack() as PB:
                bsb = lambda n, s, d=F32: PB.enter_context(
                    nc.sbuf_tensor(n + str(ci), s, d))

                LOG = bsb("bLOG", [P, NTT, NEXP])
                MX = bsb("bMX", [P, NTT])
                NMX = bsb("bNMX", [P, NTT])
                SM = bsb("bSM", [P, NTT])
                RSM = bsb("bRSM", [P, NTT])
                EXL = bsb("bEXL", [P, NTT, NEXP])
                PRB = bsb("bPRB", [P, NTT, NEXP])
                M8 = bsb("bM8", [P, NTT, NEXP])
                IXU = bsb("bIXU", [P, NTT, NEXP], mybir.dt.uint32)
                W12 = bsb("bW12", [P, 2 * NTT])
                E12 = bsb("bE12", [P, 2 * NTT])
                INDA = bsb("bINDA", [P, NEXP, 2 * NTT])
                T2A = bsb("bT2A", [P, NEXP, 2 * NTT])
                RT = bsb("bRT", [P, 16])
                PFX = bsb("bPFX", [P, 16])
                CSX = bsb("bCSX", [P, 16])
                TTMP = bsb("bTTMP", [P, 2 * NTT])
                TTM2 = bsb("bTTM2", [P, 2 * NTT])
                LP = bsb("bLP", [P, 2 * NTT])
                GP = bsb("bGP", [P, 2 * NTT])
                VLD = bsb("bVLD", [P, 2 * NTT])
                WF = bsb("bWF", [P, 2 * NTT])
                SLOT = bsb("bSLOT", [P, 2 * NTT])
                SLI = bsb("bSLI", [P, 2 * NTT], I32)
                SLH = bsb("bSLH", [P, 2 * NTT], I32)
                SH1 = bsb("bSH1", [P, 2 * NTT], I32)
                SH2 = bsb("bSH2", [P, 2 * NTT], I32)
                CNT8 = bsb("bCNT8", [1, NEXP])
                CCS = bsb("bCCS", [NEXP, NEXP])
                BASE = bsb("bBASE", [1, NEXP])
                BASEB = bsb("bBASEB", [P, NEXP])
                ONE16 = bsb("bONE16", [P, 16])
                ZI = bsb("bZI", [P, 40], I32)
                INVR = bsb("bINVR", [P, 40], I32)

                PSAB = PB.enter_context(
                    nc.psum_tensor("bPSA" + str(ci), [P, 2048], BF16))

                @block.vector
                def _(vector):
                    k.wait(vector, "mm")
                    vector.memset(ONE16[:, :], 1.0)
                    vector.memset(LP[:, :], 0.0)
                    vector.memset(GP[:, :], 0.0)
                    k.inc(vector.memset(ZI[:, :], 0), "ve")

                with contextlib.ExitStack() as PBG:
                    XFB = PBG.enter_context(
                        nc.sbuf_tensor("bXFB" + str(ci), [P, NCH, T], BF16))
                    PSG = PBG.enter_context(
                        nc.psum_tensor("bPSG" + str(ci), [NEXP, T], F32))
                    PSL = PBG.enter_context(
                        nc.psum_tensor("bPSL" + str(ci), [P, NTT, NEXP], F32))
                    GSB = PBG.enter_context(
                        nc.sbuf_tensor("bGSB" + str(ci), [NEXP, T], F32))

                    with contextlib.ExitStack() as PBL:
                        XLTB = PBL.enter_context(
                            nc.sbuf_tensor("bXLT" + str(ci), [P, NTT, D], BF16))
                        emit_ln(XLTB, rms=False)

                        @block.sync
                        def _(sync):
                            k.wait(sync, "ve")
                            dst = AP(xtok_h, 0, [[D, P], [P * D, NTT], [1, D]])
                            k.dinc(sync.dma_start(out=dst, in_=XLTB[:, :, :]),
                                   "xk")

                        emit_transposes(XLTB, XFB, PSAB, scale_wnc=False)
                    mm_xltb_done = k.cnt["mm"]

                    @block.tensor
                    def _(tensor):
                        k.wait(tensor, "ev2")
                        k.wait(tensor, "wld")
                        for tt in range(NTT):
                            for c in range(NCH):
                                k.inc(tensor.matmul(
                                    PSG[:, tt * P:(tt + 1) * P],
                                    GW[:, c, :],
                                    XFB[:, c, tt * P:(tt + 1) * P],
                                    start=(c == 0), stop=(c == NCH - 1)), "mm")

                    mm_gate = k.cnt["mm"]

                    @block.vector
                    def _(vector):
                        k.wait_at(vector, "mm", mm_gate)
                        k.inc(vector.tensor_copy(GSB[:, :], PSG[:, :]), "ve")

                    ve_gsb = k.cnt["ve"]

                    @block.tensor
                    def _(tensor):
                        k.wait_at(tensor, "ve", ve_gsb)
                        for tt in range(NTT):
                            k.inc(tensor.transpose(
                                PSL[:, tt, :], GSB[:, tt * P:(tt + 1) * P],
                                IDF[0:NEXP, 0:NEXP]), "mm")

                    mm_psl = k.cnt["mm"]

                    @block.vector
                    def _(vector):
                        k.wait_at(vector, "mm", mm_psl)
                        k.inc(vector.tensor_copy(LOG[:, :, :], PSL[:, :, :]),
                              "ve")
                        for tt in range(NTT):
                            k.inc(vector.reduce_max(MX[:, tt:tt + 1],
                                                    LOG[:, tt, :], AX), "ve")
                        for tt in range(NTT):
                            k.inc(vector.tensor_scalar(NMX[:, tt:tt + 1],
                                                       MX[:, tt:tt + 1], -1.0,
                                                       None, ALU.mult), "ve")

                    se_exp0 = k.cnt["se"]

                    @block.scalar
                    def _(scalar):
                        k.wait(scalar, "ve")
                        for tt in range(NTT):
                            k.inc(scalar.activation(
                                EXL[:, tt, :], LOG[:, tt, :], AF.Exp,
                                bias=NMX[:, tt:tt + 1],
                                accum_out=SM[:, tt:tt + 1]), "se")

                    @block.vector
                    def _(vector):
                        k.wait(vector, "se")
                        for tt in range(NTT):
                            k.inc(vector.reciprocal(RSM[:, tt:tt + 1],
                                                    SM[:, tt:tt + 1]), "ve")
                        for tt in range(NTT):
                            vector.tensor_scalar(PRB[:, tt, :], EXL[:, tt, :],
                                                 RSM[:, tt:tt + 1], None,
                                                 ALU.mult)
                        for tt in range(NTT):
                            k.inc(vector.max(M8[:, tt, :], PRB[:, tt, :]),
                                  "ve")
                        for tt in range(NTT):
                            k.inc(vector.max_index(IXU[:, tt, :], M8[:, tt, :],
                                                   PRB[:, tt, :]), "ve")
                        for tt in range(NTT):
                            vector.tensor_copy(W12[:, 2 * tt:2 * tt + 2],
                                               M8[:, tt, 0:2])
                        for tt in range(NTT):
                            k.inc(vector.tensor_copy(E12[:, 2 * tt:2 * tt + 2],
                                                     IXU[:, tt, 0:2]), "ve")

                with contextlib.ExitStack() as PBR:
                    PSR = PBR.enter_context(
                        nc.psum_tensor("bPSR" + str(ci), [P, 64], F32))

                    for e in range(NEXP):
                        @block.vector
                        def _(vector, e=e):
                            if e == 0:
                                k.wait(vector, "ve")
                            if e >= 1:
                                k.wait_at(vector, "mm", mm_pfx[e - 1])
                            k.inc(vector.tensor_single_scalar(
                                INDA[:, e, :], E12[:, :], float(e),
                                ALU.is_equal), "ve")
                            k.wait(vector, "ve")
                            k.inc(vector.tensor_tensor(
                                RT[:, :], INDA[:, e, 0:32:2],
                                INDA[:, e, 1:32:2], ALU.add), "ve")

                        ve_rt = k.cnt["ve"]

                        @block.tensor
                        def _(tensor, e=e):
                            k.wait_at(tensor, "ve", ve_rt)
                            if e >= 1:
                                k.wait_at(tensor, "ve", ve_pfxread[e - 1])
                            k.inc(tensor.matmul(PSR[:, 0:16], LST[:, :],
                                                RT[:, :], start=True,
                                                stop=True), "mm")
                            k.inc(tensor.matmul(PSR[:, 16:32], ON128[:, :],
                                                RT[:, :], start=True,
                                                stop=True), "mm")

                        mm_pfx.append(k.cnt["mm"])

                        @block.vector
                        def _(vector, e=e):
                            k.wait_at(vector, "mm", mm_pfx[e])
                            vector.tensor_copy(PFX[:, :], PSR[:, 0:16])
                            vector.tensor_tensor_scan(
                                CSX[:, 1:16], ONE16[:, 1:16], PSR[:, 16:31],
                                0.0, ALU.mult, ALU.add)
                            vector.memset(CSX[:, 0:1], 0.0)
                            k.inc(vector.reduce_sum(CNT8[0:1, e:e + 1],
                                                    PSR[0:1, 16:32], AX), "ve")
                            k.wait(vector, "ve")
                            vector.tensor_tensor(TTMP[:, 0:16], PFX[:, :],
                                                 CSX[:, :], ALU.add)
                            k.inc(vector.engine_nop(), "ve")
                            k.wait(vector, "ve")
                            vector.tensor_copy(T2A[:, e, 0:32:2], TTMP[:, 0:16])
                            vector.tensor_tensor(T2A[:, e, 1:32:2],
                                                 TTMP[:, 0:16],
                                                 INDA[:, e, 0:32:2], ALU.add)
                            k.inc(vector.engine_nop(), "ve")
                            k.wait(vector, "ve")
                            vector.tensor_tensor(TTM2[:, :], INDA[:, e, :],
                                                 T2A[:, e, :], ALU.mult)
                            k.inc(vector.engine_nop(), "ve")
                            k.wait(vector, "ve")
                            k.inc(vector.tensor_tensor(LP[:, :], LP[:, :],
                                                       TTM2[:, :], ALU.add),
                                  "ve")
                            ve_pfxread.append(k.cnt["ve"])

                    @block.sync
                    def _(sync):
                        k.wait(sync, "ve")
                        k.dinc(sync.dma_start(out=cc_in[:, :], in_=CNT8[:, :]),
                               "cc")

                    @block.gpsimd
                    def _(gpsimd):
                        k.wait(gpsimd, "cc")
                        k.inc(gpsimd.collective_compute(
                            "AllGather", ALU.bypass,
                            replica_groups=[list(range(8))],
                            ins=[cc_in.ap().opt()],
                            outs=[cc_out.ap().opt()]), "cc")
                        k.wait(gpsimd, "cc")
                        k.dinc(gpsimd.dma_start(out=CCS[:, :],
                                                in_=cc_out[:, :]), "cc2")

                    @block.tensor
                    def _(tensor):
                        k.wait(tensor, "cc2")
                        k.inc(tensor.matmul(PSR[0:1, 32:40], PMASK[:, :],
                                            CCS[:, :], start=True, stop=True),
                              "mm")

                    mm_base = k.cnt["mm"]

                    @block.vector
                    def _(vector):
                        k.wait_at(vector, "mm", mm_base)
                        k.inc(vector.tensor_copy(BASE[:, :], PSR[0:1, 32:40]),
                              "ve")

                    @block.sync
                    def _(sync):
                        k.wait(sync, "ve")
                        k.dinc(sync.dma_start(out=base_h[:, :], in_=BASE[:, :]),
                               "cc")
                        k.wait(sync, "cc")
                        k.dinc(sync.dma_start(
                            out=BASEB[:, :],
                            in_=base_h[:, :].to_broadcast((P, NEXP))), "cc")

                    @block.vector
                    def _(vector):
                        k.wait(vector, "cc")
                        for e in range(NEXP):
                            vector.tensor_scalar(TTMP[:, :], T2A[:, e, :],
                                                 BASEB[:, e:e + 1], None,
                                                 ALU.add)
                            k.inc(vector.engine_nop(), "ve")
                            k.wait(vector, "ve")
                            vector.tensor_tensor(TTM2[:, :], INDA[:, e, :],
                                                 TTMP[:, :], ALU.mult)
                            k.inc(vector.engine_nop(), "ve")
                            k.wait(vector, "ve")
                            k.inc(vector.tensor_tensor(GP[:, :], GP[:, :],
                                                       TTM2[:, :], ALU.add),
                                  "ve")
                            k.wait(vector, "ve")
                        vector.tensor_single_scalar(VLD[:, :], GP[:, :],
                                                    float(CAPACITY - 1),
                                                    ALU.is_le)
                        vector.tensor_scalar_min(SLOT[:, :], LP[:, :],
                                                 float(ECAP - 1))
                        k.inc(vector.engine_nop(), "ve")
                        k.wait(vector, "ve")
                        vector.tensor_tensor(WF[:, :], W12[:, :], VLD[:, :],
                                             ALU.mult)
                        vector.scalar_tensor_tensor(SLOT[:, :], E12[:, :],
                                                    float(ECAP), SLOT[:, :],
                                                    ALU.mult, ALU.add)
                        k.inc(vector.engine_nop(), "ve")
                        k.wait(vector, "ve")
                        k.inc(vector.tensor_copy(SLI[:, :], SLOT[:, :]), "ve")
                        k.wait(vector, "ve")
                        vector.tensor_single_scalar(SH1[:, :], SLI[:, :], 127,
                                                    ALU.bitwise_and)
                        vector.tensor_single_scalar(SH2[:, :], SLI[:, :], 7,
                                                    ALU.arith_shift_right)
                        k.inc(vector.engine_nop(), "ve")
                        k.wait(vector, "ve")
                        vector.tensor_single_scalar(SH1[:, :], SH1[:, :], 40,
                                                    ALU.mult)
                        k.inc(vector.engine_nop(), "ve")
                        k.wait(vector, "ve")
                        k.inc(vector.tensor_tensor(SLH[:, :], SH1[:, :],
                                                   SH2[:, :], ALU.add), "ve")

                    ve_slots = k.cnt["ve"]

                    @block.sync
                    def _(sync):
                        k.wait(sync, "ve")
                        iv = AP(inv_h, 0, [[40, P], [1, 40]])
                        k.dinc(sync.dma_start(out=iv, in_=ZI[:, :]), "pz")

                    @block.gpsimd
                    def _(gpsimd):
                        k.wait(gpsimd, "pz")
                        k.wait_at(gpsimd, "ve", ve_slots)
                        for col in range(2 * NTT):
                            tt = col // 2
                            k.dinc(gpsimd.indirect_dma_start(
                                inv_h[:, :],
                                IndirectOffsetOnAxis(SLH[:, col:col + 1], 0),
                                TOK[:, tt:tt + 1], None), "pk")

                    @block.sync
                    def _(sync):
                        k.wait(sync, "pk")
                        iv = AP(inv_h, 0, [[40, P], [1, 40]])
                        k.dinc(sync.dma_start(out=INVR[:, :], in_=iv), "pz")

                PBE = PB.enter_context(contextlib.ExitStack())
                esb = lambda n, s, d=F32: PBE.enter_context(
                    nc.sbuf_tensor(n + str(ci), s, d))
                PKT = esb("bPKT", [P, 2, D], BF16)
                PACKE = esb("bPACKE", [P, 2, NCH, ECAP], BF16)
                CFCE = esb("bCFCE", [P, NCH, DFF], BF16)
                CPJ = esb("bCPJ", [P, 12, D], BF16)
                HBF = esb("bHBF", [P, 12, ECAP], BF16)
                TSIL = esb("bTSIL", [P, 2, 512])
                SLOB = esb("bSLOB", [P, 2, D])

                PFC = PBE.enter_context(
                    nc.psum_tensor("bPFC" + str(ci), [P, 4, 512], F32))
                PPJ = PBE.enter_context(
                    nc.psum_tensor("bPPJ" + str(ci), [P, D], F32))

                pk_base = k.cnt["pk"]
                so_base = k.cnt["so"]
                tr_base = k.cnt["tr"]
                mm_tr = []
                ve_pke = []
                mm_fc = []
                ve_hbf = []
                ve_hbf_cb = []
                mm_pj = []
                ve_slb = []
                cbw = [(0, 512), (512, 128)]

                for e in range(NEXP):
                    @block.gpsimd
                    def _(gpsimd, e=e):
                        if e == 0:
                            k.wait(gpsimd, "pz")
                            k.wait(gpsimd, "xk")
                        for st in range(5):
                            gst = e * 5 + st
                            if gst >= 2:
                                k.wait_at(gpsimd, "tr", tr_base + gst - 1)
                            k.dinc(gpsimd.indirect_dma_start(
                                PKT[:, gst % 2, :], None,
                                xtok_h[:, :],
                                IndirectOffsetOnAxis(
                                    INVR[:, gst:gst + 1], 0)), "pk")
                        if e >= 1:
                            k.wait_at(gpsimd, "mm",
                                      mm_fc[((e - 1) * 12 + 11) * 2 + 1])
                        for kc in range(NCH):
                            k.dinc(gpsimd.dma_start(
                                out=CFCE[:, kc, :],
                                in_=c_fc[e, kc * P:(kc + 1) * P, :]), "wld2")
                        if e >= 1:
                            k.wait_at(gpsimd, "mm", mm_pj[e * 5 - 1])
                        for k2 in range(12):
                            k.dinc(gpsimd.dma_start(
                                out=CPJ[:, k2, :],
                                in_=c_proj[e, k2 * P:(k2 + 1) * P, :]), "wld2")

                    for st in range(5):
                        gst = e * 5 + st

                        @block.tensor
                        def _(tensor, e=e, st=st, gst=gst):
                            k.wait_at(tensor, "pk",
                                      pk_base + 16 * (gst + 1))
                            if gst < 2:
                                k.wait(tensor, "ev2")
                            if gst >= 1:
                                k.wait_at(tensor, "ve", ve_pke[gst - 1])
                            for c in range(NCH):
                                slot = (gst % 2) * 8 + c
                                k.inc(tensor.transpose(
                                    PSAB[:, slot * P:(slot + 1) * P],
                                    PKT[:, gst % 2, c * P:(c + 1) * P],
                                    IDB[:, :]), "mm")
                            k.inc(tensor.nop(), "tr")
                            mm_tr.append(k.cnt["mm"])

                        @block.vector
                        def _(vector, e=e, st=st, gst=gst):
                            k.wait_at(vector, "mm", mm_tr[gst])
                            for c in range(NCH):
                                slot = (gst % 2) * 8 + c
                                vector.tensor_copy(
                                    PACKE[:, e % 2, c,
                                          st * P:(st + 1) * P],
                                    PSAB[:, slot * P:(slot + 1) * P])
                            k.inc(vector.engine_nop(), "ve")
                            ve_pke.append(k.cnt["ve"])

                    for pb in range(12):
                        @block.tensor
                        def _(tensor, e=e, pb=pb):
                            if pb == 0:
                                k.wait_at(tensor, "wld2", k.cnt["wld2"])
                                k.wait_at(tensor, "ve", ve_pke[e * 5 + 4])
                            for cb, (c0, cw) in enumerate(cbw):
                                if e * 12 + pb >= 1:
                                    k.wait_at(tensor, "ve",
                                              ve_hbf_cb[
                                                  (e * 12 + pb - 1) * 2 + cb])
                                for half, mc in enumerate((pb, pb + 12)):
                                    for kc in range(NCH):
                                        k.inc(tensor.matmul(
                                            PFC[:, cb * 2 + half, 0:cw],
                                            CFCE[:, kc,
                                                 mc * P:(mc + 1) * P],
                                            PACKE[:, e % 2, kc,
                                                  c0:c0 + cw],
                                            start=(kc == 0),
                                            stop=(kc == NCH - 1)), "mm")
                                mm_fc.append(k.cnt["mm"])

                        @block.scalar
                        def _(scalar, e=e, pb=pb):
                            if e * 12 + pb >= 1:
                                k.wait_at(scalar, "ve",
                                          ve_hbf[e * 12 + pb - 1])
                            for cb, (c0, cw) in enumerate(cbw):
                                k.wait_at(scalar, "mm",
                                          mm_fc[(e * 12 + pb) * 2 + cb])
                                k.inc(scalar.activation(
                                    TSIL[:, cb, 0:cw],
                                    PFC[:, cb * 2 + 0, 0:cw], AF.Silu), "se")

                        se_here = k.cnt["se"]

                        @block.vector
                        def _(vector, e=e, pb=pb, se_here=se_here):
                            k.wait_at(vector, "se", se_here)
                            if pb == 0 and e >= 1:
                                k.wait_at(vector, "mm", mm_pj[e * 5 - 1])
                            for cb, (c0, cw) in enumerate(cbw):
                                k.inc(vector.tensor_tensor(
                                    HBF[:, pb, c0:c0 + cw],
                                    TSIL[:, cb, 0:cw],
                                    PFC[:, cb * 2 + 1, 0:cw], ALU.mult), "ve")
                                ve_hbf_cb.append(k.cnt["ve"])
                            ve_hbf.append(k.cnt["ve"])

                    for s in range(5):
                        @block.tensor
                        def _(tensor, e=e, s=s):
                            if s == 0:
                                k.wait_at(tensor, "ve", ve_hbf[e * 12 + 11])
                            if e * 5 + s >= 1:
                                k.wait_at(tensor, "ve", ve_slb[e * 5 + s - 1])
                            for k2 in range(12):
                                for nh in range(2):
                                    k.inc(tensor.matmul(
                                        PPJ[:, nh * 512:(nh + 1) * 512],
                                        HBF[:, k2, s * P:(s + 1) * P],
                                        CPJ[:, k2, nh * 512:(nh + 1) * 512],
                                        start=(k2 == 0), stop=(k2 == 11)),
                                        "mm")
                            mm_pj.append(k.cnt["mm"])

                        @block.vector
                        def _(vector, e=e, s=s):
                            gs2 = e * 5 + s
                            k.wait_at(vector, "mm", mm_pj[gs2])
                            if gs2 >= 2:
                                k.wait_at(vector, "so",
                                          so_base + 16 * (gs2 - 1))
                            for nh in range(2):
                                k.inc(vector.tensor_copy(
                                    SLOB[:, gs2 % 2, nh * 512:(nh + 1) * 512],
                                    PPJ[:, nh * 512:(nh + 1) * 512]), "ve")
                            ve_slb.append(k.cnt["ve"])

                        @block.sync
                        def _(sync, e=e, s=s):
                            gs2 = e * 5 + s
                            k.wait_at(sync, "ve", ve_slb[gs2])
                            r0 = e * ECAP + s * P
                            k.dinc(sync.dma_start(
                                out=slotout_h[r0:r0 + P, :],
                                in_=SLOB[:, gs2 % 2, :]), "so")

                PBE.close()
                UPK = bsb("bUPK", [P, 2, 2, D])

                veu0 = k.cnt["ve"]

                @block.gpsimd
                def _(gpsimd):
                    k.wait(gpsimd, "so")
                    for tt in range(NTT):
                        if tt >= 2:
                            k.wait_at(gpsimd, "ve", veu0 + 3 * (tt - 1))
                        for c in range(2):
                            k.dinc(gpsimd.indirect_dma_start(
                                UPK[:, tt % 2, c, :], None,
                                slotout_h[:, :],
                                IndirectOffsetOnAxis(
                                    SLI[:, 2 * tt + c:2 * tt + c + 1], 0)),
                                "pk")

                pk_upk0 = k.cnt["pk"] - 16 * 2 * NTT

                for tt in range(NTT):
                    @block.vector
                    def _(vector, tt=tt):
                        k.wait_at(vector, "pk", pk_upk0 + 32 * (tt + 1))
                        par = tt % 2
                        vector.tensor_scalar(UPK[:, par, 1, :],
                                             UPK[:, par, 1, :],
                                             WF[:, 2 * tt + 1:2 * tt + 2],
                                             None, ALU.mult)
                        vector.tensor_scalar(UPK[:, par, 0, :],
                                             UPK[:, par, 0, :],
                                             WF[:, 2 * tt:2 * tt + 1],
                                             None, ALU.mult)
                        k.inc(vector.engine_nop(), "ve")
                        k.wait(vector, "ve")
                        vector.tensor_tensor(UPK[:, par, 0, :],
                                             UPK[:, par, 0, :],
                                             UPK[:, par, 1, :], ALU.add)
                        k.inc(vector.engine_nop(), "ve")
                        k.wait(vector, "ve")
                        k.inc(vector.tensor_tensor(X[:, tt, :], X[:, tt, :],
                                                   UPK[:, par, 0, :], ALU.add),
                              "ve")
                        assert k.cnt["ve"] == veu0 + 3 * (tt + 1)

        qf_h = dint("qf_h", [H * (DH + 1), QFW], BF16)
        kf_h = dint("kf_h", [H * (DH + 1), T], BF16)

        def emit_attn_b():
            with contextlib.ExitStack() as PC:
                csb = lambda n, s, d=F32: PC.enter_context(
                    nc.sbuf_tensor(n, s, d))

                MK = csb("cMK", [DH, H])
                MK1 = csb("cMK1", [1, H])
                MRP = csb("cMRP", [DH, H])
                MRP1 = csb("cMRP1", [1, H])
                MKS = csb("cMKS", [DH, 1])
                MKSB = csb("cMKSB", [DH, 1], BF16)
                MH = csb("cMH", [1, H + 2])
                NMHB = csb("cNMHB", [P, H])
                UBTB = csb("cUBTB", [DH, H], BF16)
                VBTB = csb("cVBTB", [DH, H], BF16)
                XFC = csb("cXFC", [P, NCH, T], BF16)

                @block.vector
                def _(vector):
                    k.wait(vector, "ld")
                    vector.tensor_copy(UBTB[:, :], UBT[:, :])
                    k.inc(vector.tensor_copy(VBTB[:, :], VBT[:, :]), "ve")

                with contextlib.ExitStack() as PCW:
                    wsb = lambda n, s, d=F32: PCW.enter_context(
                        nc.sbuf_tensor(n, s, d))
                    PSAC = PCW.enter_context(
                        nc.psum_tensor("cPSAC", [P, 2048], BF16))
                    PC1 = PCW.enter_context(
                        nc.psum_tensor("cPC1", [P, 4, 512], F32))
                    PB1 = PCW.enter_context(
                        nc.psum_tensor("cPB1", [1, 2, 512], F32))
                    BNDS = SQ[0:1, :, :].rearrange("p a b -> p (a b)")

                    with contextlib.ExitStack() as PCL:
                        XLTC = PCL.enter_context(
                            nc.sbuf_tensor("cXLT", [P, NTT, D], BF16))
                        emit_ln(XLTC, rms=False)
                        emit_transposes(XLTC, XFC, PSAC, scale_wnc=False)

                    WRAW = wsb("cWRAW", [P, NCH, D], BF16)
                    WT = wsb("cWT", [P, NCH, D], BF16)
                    QFS = wsb("cQFS", [DH + 1, 2, QFW], BF16)
                    KFS = wsb("cKFS", [DH, 2, T], BF16)
                    KRW = wsb("cKRW", [1, 2, T], BF16)
                    RRW = wsb("cRRW", [1, 2, 1024], BF16)
                    ABSQ = wsb("cABSQ", [DH, T], BF16)
                    VS = wsb("cVS", [P, 2, H * (DH + 1)], BF16)
                    MRP2 = wsb("cMRP2", [DH, 1])

                    @block.vector
                    def _(vector):
                        k.wait(vector, "mm")
                        vector.memset(QFS[DH:DH + 1, :, :], 1.0)
                        vector.memset(QFS[:, :, T:QFW], 0.0)
                        k.inc(vector.memset(VS[:, :, :], 1.0), "ve")

                    def load_wt(wmat_d):
                        @block.gpsimd
                        def _(gpsimd):
                            k.wait(gpsimd, "mm")
                            src = AP(wmat_d, 0,
                                     [[D, P], [P * D, NCH], [1, D]])
                            k.dinc(gpsimd.dma_start(out=WRAW[:, :, :],
                                                    in_=src), "wld2")

                        mmw0 = k.cnt["mm"]
                        ev0 = k.cnt["ev2"]

                        @block.tensor
                        def _(tensor):
                            k.wait(tensor, "wld2")
                            k.wait(tensor, "ev2")
                            for n in range(NCH * NCH):
                                oc, ic = divmod(n, NCH)
                                if n >= 16:
                                    k.wait_at(tensor, "ev2",
                                              ev0 + (n - 16) // 8 + 1)
                                slot = n % 16
                                k.inc(tensor.transpose(
                                    PSAC[:, slot * P:(slot + 1) * P],
                                    WRAW[:, oc, ic * P:(ic + 1) * P],
                                    IDB[:, :]), "mm")

                        @block.vector
                        def _(vector):
                            for g in range(NCH * NCH // 8):
                                k.wait_at(vector, "mm", mmw0 + (g + 1) * 8)
                                for j in range(8):
                                    n = g * 8 + j
                                    oc, ic = divmod(n, NCH)
                                    slot = n % 16
                                    vector.tensor_copy(
                                        WT[:, ic, oc * P:(oc + 1) * P],
                                        PSAC[:, slot * P:(slot + 1) * P])
                                k.inc(vector.engine_nop(), "ev2")

                    load_wt(wr_d)
                    mm_rpj = []
                    ve_rpj = []
                    ve_rr = []
                    rp_s0 = k.cnt["sk"]
                    with contextlib.ExitStack() as PCR:
                        REMBF = PCR.enter_context(
                            nc.sbuf_tensor("cREMBF", [P, NCH, 1024], BF16))
                        RPJS = PCR.enter_context(
                            nc.sbuf_tensor("cRPJS", [DH, 2, 1024], BF16))
                        for q in range(4):
                            @block.gpsimd
                            def _(gpsimd, q=q):
                                if q == 0:
                                    k.wait(gpsimd, "mm")
                                else:
                                    k.wait_at(gpsimd, "mm", mm_rpj[-1])
                                src = AP(c_remb_t, q * 1024,
                                         [[RPN, P], [P * RPN, NCH], [1, 1024]])
                                k.dinc(gpsimd.dma_start(out=REMBF[:, :, :],
                                                        in_=src), "wld2")

                            for h in range(H):
                                hpar = h % 2
                                hi = q * H + h

                                @block.tensor
                                def _(tensor, q=q, h=h, hi=hi):
                                    k.wait(tensor, "wld2")
                                    if hi == 0:
                                        k.wait(tensor, "ev2")
                                    if hi >= 2:
                                        k.wait_at(tensor, "sk",
                                                  rp_s0 + 32 * (hi - 1))
                                    for fc in range(2):
                                        for ic in range(NCH):
                                            k.inc(tensor.matmul(
                                                PC1[0:DH, fc, :],
                                                WT[:, ic,
                                                   h * DH:(h + 1) * DH],
                                                REMBF[:, ic,
                                                      fc * 512:(fc + 1) * 512],
                                                start=(ic == 0),
                                                stop=(ic == NCH - 1)), "mm")
                                    mm_rpj.append(k.cnt["mm"])

                                @block.vector
                                def _(vector, hi=hi, hpar=hpar):
                                    k.wait_at(vector, "mm", mm_rpj[hi])
                                    for fc in range(2):
                                        vector.tensor_copy(
                                            RPJS[:, hpar,
                                                 fc * 512:(fc + 1) * 512],
                                            PC1[0:DH, fc, :])
                                    k.inc(vector.engine_nop(), "ve")
                                    ve_rpj.append(k.cnt["ve"])

                                @block.tensor
                                def _(tensor, h=h, hi=hi, hpar=hpar):
                                    k.wait_at(tensor, "ve", ve_rpj[hi])
                                    for fc in range(2):
                                        k.inc(tensor.matmul(
                                            PB1[:, fc, :],
                                            VBTB[:, h:h + 1],
                                            RPJS[:, hpar,
                                                 fc * 512:(fc + 1) * 512],
                                            start=True, stop=True), "mm")

                                mm_rrow = k.cnt["mm"]

                                @block.vector
                                def _(vector, q=q, h=h, hpar=hpar,
                                      mm_rrow=mm_rrow):
                                    k.wait_at(vector, "mm", mm_rrow)
                                    for fc in range(2):
                                        vector.tensor_copy(
                                            RRW[:, hpar,
                                                fc * 512:(fc + 1) * 512],
                                            PB1[:, fc, :])
                                    vector.reduce_max(
                                        MRP2[:, 0:1], RPJS[:, hpar, :], AX,
                                        apply_absolute_value=True)
                                    k.inc(vector.engine_nop(), "ve")
                                    k.wait(vector, "ve")
                                    if q == 0:
                                        vector.tensor_copy(MRP[:, h:h + 1],
                                                           MRP2[:, 0:1])
                                        vector.reduce_max(
                                            MRP1[:, h:h + 1], RRW[:, hpar, :],
                                            AX, apply_absolute_value=True)
                                    else:
                                        vector.tensor_tensor(
                                            MRP[:, h:h + 1], MRP[:, h:h + 1],
                                            MRP2[:, 0:1], ALU.max)
                                        vector.reduce_max(
                                            MH[:, H + 1:H + 2],
                                            RRW[:, hpar, :], AX,
                                            apply_absolute_value=True)
                                        k.inc(vector.engine_nop(), "ve")
                                        k.wait(vector, "ve")
                                        vector.tensor_tensor(
                                            MRP1[:, h:h + 1], MRP1[:, h:h + 1],
                                            MH[:, H + 1:H + 2], ALU.max)
                                    k.inc(vector.engine_nop(), "ve")

                                @block.sync
                                def _(sync, q=q, h=h, hpar=hpar):
                                    k.wait(sync, "ve")
                                    cs = slice(q * 1024, (q + 1) * 1024)
                                    k.dinc(sync.dma_start(
                                        out=rp_hd[h * (DH + 1):
                                                  h * (DH + 1) + DH, cs],
                                        in_=RPJS[:, hpar, :]), "sk")
                                    k.dinc(sync.dma_start(
                                        out=rp_hd[h * (DH + 1) + DH:
                                                  h * (DH + 1) + DH + 1, cs],
                                        in_=RRW[:, hpar, :]), "sk")

                    if _C_STAGES < 2:
                        return
                    load_wt(wk_d)
                    mm_k = []
                    ve_k = []
                    ve_kr = []
                    k_s0 = k.cnt["sk"]
                    for h in range(H):
                        hpar = h % 2

                        @block.tensor
                        def _(tensor, h=h, hpar=hpar):
                            if h == 0:
                                k.wait(tensor, "ev2")
                            if h >= 2:
                                k.wait_at(tensor, "sk", k_s0 + 32 * (h - 1))
                            for fc in range(4):
                                for ic in range(NCH):
                                    k.inc(tensor.matmul(
                                        PC1[0:DH, fc, :],
                                        WT[:, ic, h * DH:(h + 1) * DH],
                                        XFC[:, ic, fc * 512:(fc + 1) * 512],
                                        start=(ic == 0),
                                        stop=(ic == NCH - 1)), "mm")
                            mm_k.append(k.cnt["mm"])

                        @block.vector
                        def _(vector, h=h, hpar=hpar):
                            k.wait_at(vector, "mm", mm_k[h])
                            for fc in range(4):
                                vector.tensor_copy(
                                    KFS[:, hpar, fc * 512:(fc + 1) * 512],
                                    PC1[0:DH, fc, :])
                            k.inc(vector.engine_nop(), "ve")
                            k.wait(vector, "ve")
                            k.inc(vector.reduce_max(
                                MK[:, h:h + 1], KFS[:, hpar, :], AX,
                                apply_absolute_value=True), "ve")
                            ve_k.append(k.cnt["ve"])

                        for rnd in range(2):
                            @block.tensor
                            def _(tensor, h=h, hpar=hpar, rnd=rnd):
                                if rnd == 0:
                                    k.wait_at(tensor, "ve", ve_k[h])
                                else:
                                    k.wait_at(tensor, "ve", ve_kr[-1])
                                for fc2 in range(2):
                                    fc = rnd * 2 + fc2
                                    k.inc(tensor.matmul(
                                        PB1[:, fc2, :], UBTB[:, h:h + 1],
                                        KFS[:, hpar, fc * 512:(fc + 1) * 512],
                                        start=True, stop=True), "mm")

                            mm_krow = k.cnt["mm"]

                            @block.vector
                            def _(vector, hpar=hpar, rnd=rnd,
                                  mm_krow=mm_krow):
                                k.wait_at(vector, "mm", mm_krow)
                                for fc2 in range(2):
                                    fc = rnd * 2 + fc2
                                    vector.tensor_copy(
                                        KRW[:, hpar,
                                            fc * 512:(fc + 1) * 512],
                                        PB1[:, fc2, :])
                                k.inc(vector.engine_nop(), "ve")
                                ve_kr.append(k.cnt["ve"])

                        @block.vector
                        def _(vector, h=h, hpar=hpar):
                            k.wait(vector, "ve")
                            k.inc(vector.reduce_max(
                                MK1[:, h:h + 1], KRW[:, hpar, :], AX,
                                apply_absolute_value=True), "ve")

                        @block.sync
                        def _(sync, h=h, hpar=hpar):
                            k.wait(sync, "ve")
                            k.dinc(sync.dma_start(
                                out=kf_h[h * (DH + 1):h * (DH + 1) + DH, :],
                                in_=KFS[:, hpar, :]), "sk")
                            k.dinc(sync.dma_start(
                                out=kf_h[h * (DH + 1) + DH:
                                         h * (DH + 1) + DH + 1, :],
                                in_=KRW[:, hpar, :]), "sk")

                    load_wt(wq_d)
                    mm_q = []
                    ve_q = []
                    ve_bn = []
                    q_s0 = k.cnt["sk"]
                    for h in range(H):
                        hpar = h % 2

                        @block.tensor
                        def _(tensor, h=h, hpar=hpar):
                            if h == 0:
                                k.wait(tensor, "ev2")
                            if h >= 2:
                                k.wait_at(tensor, "sk", q_s0 + 16 * (h - 1))
                            for fc in range(4):
                                for ic in range(NCH):
                                    k.inc(tensor.matmul(
                                        PC1[0:DH, fc, :],
                                        WT[:, ic, h * DH:(h + 1) * DH],
                                        XFC[:, ic, fc * 512:(fc + 1) * 512],
                                        start=(ic == 0),
                                        stop=(ic == NCH - 1)), "mm")
                            mm_q.append(k.cnt["mm"])

                        @block.vector
                        def _(vector, h=h, hpar=hpar):
                            k.wait_at(vector, "mm", mm_q[h])
                            for fc in range(4):
                                vector.tensor_copy(
                                    QFS[0:DH, hpar, fc * 512:(fc + 1) * 512],
                                    PC1[0:DH, fc, :])
                            vector.tensor_tensor(MKS[:, 0:1], MK[:, h:h + 1],
                                                 MRP[:, h:h + 1], ALU.add)
                            k.inc(vector.engine_nop(), "ve")
                            k.wait(vector, "ve")
                            k.inc(vector.tensor_copy(MKSB[:, 0:1],
                                                     MKS[:, 0:1]), "ve")
                            ve_q.append(k.cnt["ve"])

                        @block.scalar
                        def _(scalar, h=h, hpar=hpar):
                            k.wait_at(scalar, "ve", ve_q[h])
                            k.inc(scalar.activation(
                                ABSQ[:, :], QFS[0:DH, hpar, 0:T],
                                AF.Abs), "se")

                        se_abs = k.cnt["se"]

                        for rnd in range(2):
                            @block.tensor
                            def _(tensor, h=h, hpar=hpar, rnd=rnd,
                                  se_abs=se_abs):
                                if rnd == 0:
                                    k.wait_at(tensor, "se", se_abs)
                                else:
                                    k.wait_at(tensor, "ve", ve_bn[-1])
                                for fc2 in range(2):
                                    fc = rnd * 2 + fc2
                                    k.inc(tensor.matmul(
                                        PB1[:, fc2, :], MKSB[:, 0:1],
                                        ABSQ[:,
                                             fc * 512:(fc + 1) * 512],
                                        start=True, stop=True), "mm")

                            mm_bnd = k.cnt["mm"]

                            @block.vector
                            def _(vector, rnd=rnd, mm_bnd=mm_bnd):
                                k.wait_at(vector, "mm", mm_bnd)
                                vector.tensor_copy(
                                    BNDS[:, rnd * 1024:rnd * 1024 + 512],
                                    PB1[:, 0, :])
                                vector.tensor_copy(
                                    BNDS[:, rnd * 1024 + 512:
                                         (rnd + 1) * 1024],
                                    PB1[:, 1, :])
                                k.inc(vector.engine_nop(), "ve")
                                ve_bn.append(k.cnt["ve"])

                        @block.vector
                        def _(vector, h=h):
                            k.wait(vector, "ve")
                            vector.reduce_max(MH[:, H:H + 1], BNDS[:, :], AX)
                            k.inc(vector.engine_nop(), "ve")
                            k.wait(vector, "ve")
                            vector.tensor_tensor(MH[:, H:H + 1],
                                                 MH[:, H:H + 1],
                                                 MK1[:, h:h + 1], ALU.add)
                            k.inc(vector.engine_nop(), "ve")
                            k.wait(vector, "ve")
                            k.inc(vector.tensor_tensor(
                                MH[:, h:h + 1], MH[:, H:H + 1],
                                MRP1[:, h:h + 1], ALU.add), "ve")

                        @block.sync
                        def _(sync, h=h, hpar=hpar):
                            k.wait(sync, "ve")
                            k.dinc(sync.dma_start(
                                out=qf_h[h * (DH + 1):(h + 1) * (DH + 1), :],
                                in_=QFS[:, hpar, :]), "sk")

                    @block.vector
                    def _(vector):
                        k.wait(vector, "ve")
                        k.inc(vector.tensor_scalar(MH[:, 0:H], MH[:, 0:H],
                                                   -0.125, None, ALU.mult),
                              "ve")

                    @block.sync
                    def _(sync):
                        k.wait(sync, "ve")
                        k.dinc(sync.dma_start(out=mh_hd[:, :],
                                              in_=MH[:, 0:H]), "cc")
                        k.wait(sync, "cc")
                        k.dinc(sync.dma_start(
                            out=NMHB[:, :],
                            in_=mh_hd[:, :].to_broadcast((P, H))), "cc")

                    load_wt(wv_d)
                    mm_v = []
                    ve_v = []
                    v_s0 = k.cnt["sk"]
                    for tt in range(NTT):
                        @block.tensor
                        def _(tensor, tt=tt):
                            if tt == 0:
                                k.wait(tensor, "ev2")
                            if tt >= 2:
                                k.wait_at(tensor, "sk", v_s0 + 16 * (tt - 1))
                            if tt >= 1:
                                k.wait_at(tensor, "ve", ve_v[tt - 1])
                            for nh in range(2):
                                for ic in range(NCH):
                                    k.inc(tensor.matmul(
                                        PC1[:, (tt % 2) * 2 + nh, :],
                                        XFC[:, ic, tt * P:(tt + 1) * P],
                                        WT[:, ic, nh * 512:(nh + 1) * 512],
                                        start=(ic == 0),
                                        stop=(ic == NCH - 1)), "mm")
                            mm_v.append(k.cnt["mm"])

                        @block.vector
                        def _(vector, tt=tt):
                            k.wait_at(vector, "mm", mm_v[tt])
                            for nh in range(2):
                                dst = AP(VS.ap().tensor,
                                         (tt % 2) * H * (DH + 1)
                                         + nh * 8 * (DH + 1),
                                         [[2 * H * (DH + 1), P],
                                          [DH + 1, 8], [1, DH]])
                                k.inc(vector.tensor_copy(
                                    dst, PC1[:, (tt % 2) * 2 + nh, :]), "ve")
                            ve_v.append(k.cnt["ve"])

                        @block.sync
                        def _(sync, tt=tt):
                            k.wait_at(sync, "ve", ve_v[tt])
                            dst = AP(vt_hd, tt * P * H * (DH + 1),
                                     [[H * (DH + 1), P], [1, H * (DH + 1)]])
                            k.dinc(sync.dma_start(
                                out=dst, in_=VS[:, tt % 2, :]), "sk")

                if _C_STAGES < 3:
                    return
                ssb = lambda n, s, d=F32: PC.enter_context(
                    nc.sbuf_tensor(n, s, d))
                QFL = ssb("cQFL", [DH + 1, 2, QFW], BF16)
                KFL = ssb("cKFL", [DH + 1, 2, T], BF16)
                RPJL = ssb("cRPJL", [DH + 1, 2, RPN], BF16)
                VTH = ssb("cVTH", [P, 2, NTT, DH + 1], BF16)
                SKB = ssb("cSKB", [P, 2, SKW // 32, 32], BF16)
                BAND = ssb("cBAND", [P, 2, 2, T], BF16)
                PQ = ssb("cPQ", [P, 2, T], BF16)
                PQT = ssb("cPQT", [P, T], BF16)
                PT = ssb("cPT", [P, NTT, 256], BF16)
                OSB = ssb("cOSB", [DH + 1, 256])
                RCL = ssb("cRCL", [P, 2])

                PSC = PC.enter_context(
                    nc.psum_tensor("cPSC", [P, 4, 512], F32))
                PPV = PC.enter_context(
                    nc.psum_tensor("cPPV", [DH + 1, 256], F32))
                PSE = PC.enter_context(
                    nc.psum_tensor("cPSE", [P, 512], F32))

                hl0 = k.cnt["ld"]
                hd_mm = []
                for h in range(_C_HEADS):
                    hpar = h % 2
                    exp_se = []
                    exp_sub = []
                    exp_sub_all = []
                    ac_mm2 = []
                    pmul_ve = []
                    ptr_mm = []

                    @block.gpsimd
                    def _(sync, h=h, hpar=hpar):
                        if h == 0:
                            k.wait(sync, "sk")
                        if h >= 2:
                            k.wait_at(sync, "mm", hd_mm[h - 2])
                        k.dinc(sync.dma_start(
                            out=QFL[:, hpar, :],
                            in_=qf_h[h * (DH + 1):(h + 1) * (DH + 1), :]),
                            "hl")
                        k.dinc(sync.dma_start(
                            out=KFL[:, hpar, :],
                            in_=kf_h[h * (DH + 1):(h + 1) * (DH + 1), :]),
                            "hl")
                        k.dinc(sync.dma_start(
                            out=RPJL[:, hpar, :],
                            in_=rp_hd[h * (DH + 1):(h + 1) * (DH + 1), :]),
                            "hl")
                        src = AP(vt_hd, h * (DH + 1),
                                 [[H * (DH + 1), P],
                                  [P * H * (DH + 1), NTT], [1, DH + 1]])
                        k.dinc(sync.dma_start(out=VTH[:, hpar, :, :],
                                              in_=src), "hl")

                    hload = k.cnt["hl"]

                    bd_sk0 = k.cnt["sk"]
                    bd_tr0 = k.cnt["tr"]
                    for qt in range(NTT):
                        i0 = qt * P

                        @block.tensor
                        def _(tensor, h=h, hpar=hpar, qt=qt, i0=i0,
                              hload=hload):
                            if qt == 0:
                                k.wait_at(tensor, "hl", hload)
                                k.wait(tensor, "ve")
                                k.wait(tensor, "se")
                            if qt >= 1:
                                k.wait_at(tensor, "tr", bd_tr0 + qt)
                            pend = 128 + i0
                            for uc in range(5):
                                u0 = uc * 512
                                u1 = min(u0 + 512, SKW)
                                segs = []
                                if u0 <= pend:
                                    segs.append((u0, min(u1 - 1, pend),
                                                 3967 - i0, 0))
                                if u1 - 1 >= pend + 1:
                                    segs.append((max(u0, pend + 1), u1 - 1,
                                                 -129 - i0, 1))
                                for (a, b2, roff, fut) in segs:
                                    q0 = i0 + 1 if fut else i0
                                    dst = (PSC[:, uc, a - u0:b2 + 1 - u0]
                                           if uc < 4 else
                                           PSE[:, a - u0:b2 + 1 - u0])
                                    k.inc(tensor.matmul(
                                        dst,
                                        QFL[:, hpar, q0:q0 + P],
                                        RPJL[:, hpar, a + roff:b2 + 1 + roff],
                                        start=True, stop=True), "mm")

                        mm_bd = k.cnt["mm"]

                        @block.scalar
                        def _(scalar, h=h, qt=qt, mm_bd=mm_bd):
                            k.wait_at(scalar, "mm", mm_bd)
                            if qt >= 2:
                                k.wait_at(scalar, "sk",
                                          bd_sk0 + 16 * (qt - 1))
                            par = qt % 2
                            flat = PSC[:, :, :].rearrange("p a b -> p (a b)")
                            scalar.activation(SKB[:, par, 0:32, :],
                                              flat[:, 0:1024], AF.Exp,
                                              scale=0.125)
                            scalar.activation(SKB[:, par, 32:64, :],
                                              flat[:, 1024:2048], AF.Exp,
                                              scale=0.125)
                            k.inc(scalar.activation(SKB[:, par, 64:68, :],
                                                    PSE[:, 0:128], AF.Exp,
                                                    scale=0.125), "tr")

                        @block.sync
                        def _(sync, h=h, qt=qt):
                            k.wait_at(sync, "tr", bd_tr0 + qt + 1)
                            r0 = (h * NTT + qt) * P
                            k.dinc(sync.dma_start(
                                out=skw_h[r0:r0 + P, :],
                                in_=SKB[:, qt % 2, :, :]), "sk")

                    bd_done = k.cnt["sk"]

                    ac_mm = []
                    pv_ve = []
                    for qp in range(NTT // 2 if _C_SC >= 2 else 0):
                        @block.gpsimd
                        def _(sync, h=h, qp=qp):
                            k.wait_at(sync, "sk", bd_done)
                            if qp >= 2:
                                k.wait_at(sync, "ve",
                                          pmul_ve[2 * (qp - 2) + 1])
                            for qtl in range(2):
                                qt = 2 * qp + qtl
                                base = (h * NTT + qt) * P * SKW + 127
                                src = AP(skw_h, base, [[SKW - 1, P], [1, T]])
                                k.dinc(sync.dma_start(
                                    out=BAND[:, qp % 2, qtl, :], in_=src),
                                    "hl")

                        band_ld = k.cnt["hl"]

                        for qtl in range(2):
                            for jp in range(2):
                                gi = (2 * qp + qtl) * 2 + jp

                                @block.tensor
                                def _(tensor, h=h, hpar=hpar, qp=qp, qtl=qtl,
                                      jp=jp, gi=gi, band_ld=band_ld):
                                    if gi >= 2:
                                        k.wait_at(tensor, "se",
                                                  exp_sub_all[gi - 2])
                                    qt = 2 * qp + qtl
                                    for jc in (jp * 2, jp * 2 + 1):
                                        js = slice(jc * 512, (jc + 1) * 512)
                                        k.inc(tensor.matmul(
                                            PSC[:, jc, :],
                                            QFL[:, hpar,
                                                qt * P:(qt + 1) * P],
                                            KFL[:, hpar, js],
                                            start=True, stop=True), "mm")
                                    ac_mm2.append(k.cnt["mm"])

                                @block.scalar
                                def _(scalar, h=h, qp=qp, qtl=qtl, jp=jp,
                                      gi=gi):
                                    k.wait_at(scalar, "mm", ac_mm2[-1])
                                    if jp == 0 and pmul_ve:
                                        k.wait_at(scalar, "ve", pmul_ve[-1])
                                    flat = PSC[:, :, :].rearrange(
                                        "p a b -> p (a b)")
                                    js = slice(jp * 1024, (jp + 1) * 1024)
                                    k.inc(scalar.activation(
                                        PQT[:, js], flat[:, js],
                                        AF.Exp, scale=0.125,
                                        bias=NMHB[:, h:h + 1]), "se")
                                    exp_sub_all.append(k.cnt["se"])

                            exp_sub.append(k.cnt["se"])
                            ac_mm.append(k.cnt["mm"])

                            @block.vector
                            def _(vector, qp=qp, qtl=qtl, band_ld=band_ld):
                                k.wait_at(vector, "hl", band_ld)
                                k.wait_at(vector, "se", exp_sub[-1])
                                k.inc(vector.tensor_tensor(
                                    PQ[:, qtl, :], PQT[:, :],
                                    BAND[:, qp % 2, qtl, :], ALU.mult), "ve")
                                pmul_ve.append(k.cnt["ve"])

                        exp_se.append(k.cnt["se"])

                        if _C_SC < 3:
                            ptr_mm.append(k.cnt["mm"])
                            pv_ve.append(k.cnt["ve"])
                            continue
                        @block.vector
                        def _(vector, qp=qp):
                            ident = list(range(32))
                            for qtl in range(2):
                                vector.transpose(SKB[:, qtl, 0:64, :],
                                                 PQ[:, qtl, :])
                            ii = None
                            for qtl in range(2):
                                for qi in range(4):
                                    for qo in range(4):
                                        src = SKB[32 * qi:32 * (qi + 1),
                                                  qtl, qo:64:4, :]
                                        dst = PT[32 * qo:32 * (qo + 1), :,
                                                 qtl * P + 32 * qi:
                                                 qtl * P + 32 * (qi + 1)]
                                        ii = vector.stream_shuffle(
                                            dst, src, ident)
                            k.inc(ii, "ve")

                        ptr_mm.append(k.cnt["mm"])
                        sh_ve = k.cnt["ve"]

                        @block.tensor
                        def _(tensor, h=h, hpar=hpar, qp=qp, sh_ve=sh_ve):
                            k.wait_at(tensor, "ve", sh_ve)
                            for jt in range(NTT):
                                k.inc(tensor.matmul(
                                    PPV[:, :], VTH[:, hpar, jt, :],
                                    PT[:, jt, :],
                                    start=(jt == 0), stop=(jt == NTT - 1)),
                                    "mm")

                        pv_mm = k.cnt["mm"]

                        @block.vector
                        def _(vector, qp=qp, pv_mm=pv_mm):
                            k.wait_at(vector, "mm", pv_mm)
                            k.inc(vector.tensor_copy(OSB[:, :], PPV[:, :]),
                                  "ve")

                        os_ve = k.cnt["ve"]

                        @block.tensor
                        def _(tensor, qp=qp, os_ve=os_ve):
                            k.wait_at(tensor, "ve", os_ve)
                            for qtl in range(2):
                                k.inc(tensor.transpose(
                                    PSE[:, qtl * P:qtl * P + DH + 1],
                                    OSB[:, qtl * P:(qtl + 1) * P],
                                    IDF[0:DH + 1, 0:DH + 1]), "mm")

                        ep_mm = k.cnt["mm"]

                        @block.vector
                        def _(vector, h=h, qp=qp, ep_mm=ep_mm):
                            k.wait_at(vector, "mm", ep_mm)
                            for qtl in range(2):
                                vector.reciprocal(
                                    RCL[:, qtl:qtl + 1],
                                    PSE[:, qtl * P + DH:qtl * P + DH + 1])
                            for qtl in range(2):
                                qt = 2 * qp + qtl
                                k.inc(vector.scalar_tensor_tensor(
                                    X[:, qt, h * DH:(h + 1) * DH],
                                    PSE[:, qtl * P:qtl * P + DH],
                                    RCL[:, qtl:qtl + 1],
                                    X[:, qt, h * DH:(h + 1) * DH],
                                    ALU.mult, ALU.add), "ve")
                            pv_ve.append(k.cnt["ve"])

                    hd_mm.append(k.cnt["mm"])

        if phases >= 2:
            emit_moe(0)
        if phases >= 3:
            emit_attn_b()
        if phases >= 4:
            emit_moe(1)

        @block.sync
        def _(sync):
            k.wait(sync, "ve")
            k.wait(sync, "se")
            k.wait(sync, "mm")
            dst = AP(out_d, 0, [[D, P], [P * D, NTT], [1, D]])
            k.dinc(sync.dma_start(out=dst, in_=X[:, :, :]), "st")
            k.wait(sync, "st")

    return nc


def _shard_inputs(inputs):
    consts = _host_consts()
    lam = np.ascontiguousarray(
        np.asarray(inputs["L0_attn_a_lambda_log"]).reshape(NCH, P).T)
    wnc = np.ascontiguousarray(
        np.asarray(inputs["L0_attn_a_w_norm"]).reshape(NCH, P).T)
    ubt = np.ascontiguousarray(np.asarray(inputs["L0_attn_b_u_bias"]).T)
    vbt = np.ascontiguousarray(np.asarray(inputs["L0_attn_b_v_bias"]).T)
    common = {
        "w_in": np.asarray(inputs["L0_attn_a_W_in"]),
        "lam_log": lam.astype(np.float32),
        "w_norm_c": wnc.astype(np.float32),
        "w_out": np.asarray(inputs["L0_attn_a_W_out"]),
        "wq": np.asarray(inputs["L0_attn_b_Wq"]),
        "wk": np.asarray(inputs["L0_attn_b_Wk"]),
        "wv": np.asarray(inputs["L0_attn_b_Wv"]),
        "wr": np.asarray(inputs["L0_attn_b_Wr"]),
        "u_bias_t": ubt.astype(np.float32),
        "v_bias_t": vbt.astype(np.float32),
        "gate_w": np.asarray(inputs["L0_ffn_w_gate"]),
        "c_fc": np.asarray(inputs["L0_ffn_c_fc"]),
        "c_proj": np.asarray(inputs["L0_ffn_c_proj"]),
    }
    common.update(consts)
    xfull = np.ascontiguousarray(np.asarray(inputs["L0_attn_a_x"]))
    per_core = []
    for b in range(8):
        m = dict(common)
        m["x_in"] = np.ascontiguousarray(xfull[b])
        pm = np.zeros((NEXP, 1), np.float32)
        pm[:b, 0] = 1.0
        m["prev_mask"] = pm
        per_core.append(m)
    return per_core


def kernel(**inputs):
    in_maps = _shard_inputs(inputs)
    nc = build_program()
    mybir.codegen_inst_isa_subclasses(nc)
    res = run_bass_kernel_spmd(nc, in_maps, core_ids=list(range(8)))
    return np.stack([r["out"] for r in res.results], axis=0)


# version 11
# speedup vs baseline: 1.2911x; 1.2911x over previous
import contextlib
import numpy as np
import ml_dtypes
import concourse.bass as bass
import concourse.mybir as mybir
from concourse.ap import AP
from concourse.bass_core_rust import IndirectOffsetOnAxis
from concourse.bass_utils import run_bass_kernel_spmd

F32 = mybir.dt.float32
BF16 = mybir.dt.bfloat16
I32 = mybir.dt.int32
AF = mybir.ActivationFunctionType
ALU = mybir.AluOpType
AX = mybir.AxisListType.X

P = 128
D = 1024
T = 2048
NTT = T // P
NCH = D // P
H = 16
DH = 64
DFF = 3072
NEXP = 8
ECAP = 640
NSLOT = NEXP * ECAP
CAPACITY = 4096
SKW = 2176
RPN = 4096
QFW = 2064

_PHASES = 4
_TAPS = ()
_C_STAGES = 3
_C_HEADS = H
_C_SC = 3


def _host_consts():
    ident = np.eye(P, dtype=np.float64)
    lstrict = (np.arange(P)[:, None] < np.arange(P)[None, :]).astype(np.float64)
    tokids = (np.arange(NTT)[None, :] * P + np.arange(P)[:, None]).astype(np.int32)
    inv_freq = 1.0 / (10000.0 ** (np.arange(0, D, 2, dtype=np.float64) / D))
    pos = np.arange(-(T - 1), T, dtype=np.float64)
    ang = pos[:, None] * inv_freq[None, :]
    emb = np.zeros((2 * T - 1, D), np.float64)
    emb[:, 0::2] = np.sin(ang)
    emb[:, 1::2] = np.cos(ang)
    remb_t = np.zeros((D, RPN), np.float32)
    remb_t[:, : 2 * T - 1] = emb.T.astype(np.float32)
    return {
        "c_identb": ident.astype(ml_dtypes.bfloat16),
        "c_identf": ident.astype(np.float32),
        "c_lstrict": lstrict.astype(np.float32),
        "c_ones128": np.ones((P, P), np.float32),
        "c_tokids": tokids,
        "c_remb_t": remb_t,
    }


class K:

    def __init__(self, nc, stack):
        self.nc = nc
        self.stack = stack
        self.sem = {}
        self.cnt = {}

    def mksem(self, *names):
        for n in names:
            self.sem[n] = self.stack.enter_context(self.nc.semaphore(n))
            self.cnt[n] = 0

    def inc(self, instr, name, n=1):
        instr.then_inc(self.sem[name], n)
        self.cnt[name] += n
        return instr

    def dinc(self, instr, name):
        return self.inc(instr, name, 16)

    def wait(self, eng, name):
        if self.cnt[name] > 0:
            eng.wait_ge(self.sem[name], self.cnt[name])

    def wait_at(self, eng, name, val):
        if val > 0:
            eng.wait_ge(self.sem[name], val)


def build_program(phases=None):
    if phases is None:
        phases = _PHASES
    nc = bass.Bass("TRN2", target_bir_lowering=False,
                   detect_race_conditions=False)

    def din(name, shape, dtype=F32):
        return nc.dram_tensor(name, shape, dtype, kind="ExternalInput")

    def dint(name, shape, dtype=F32):
        return nc.dram_tensor(name, shape, dtype, kind="Internal")

    x_in = din("x_in", [T, D])
    w_in = din("w_in", [D, DFF])
    lam_log = din("lam_log", [P, NCH])
    w_norm_c = din("w_norm_c", [P, NCH])
    w_out = din("w_out", [D, D])
    wq_d = din("wq", [D, D])
    wk_d = din("wk", [D, D])
    wv_d = din("wv", [D, D])
    wr_d = din("wr", [D, D])
    u_bias_t = din("u_bias_t", [DH, H])
    v_bias_t = din("v_bias_t", [DH, H])
    gate_w = din("gate_w", [D, NEXP])
    c_fc = din("c_fc", [NEXP, D, DFF])
    c_proj = din("c_proj", [NEXP, DFF // 2, D])
    prev_mask = din("prev_mask", [NEXP, 1])
    c_identb = din("c_identb", [P, P], BF16)
    c_identf = din("c_identf", [P, P])
    c_lstrict = din("c_lstrict", [P, P])
    c_ones128 = din("c_ones128", [P, P])
    c_tokids = din("c_tokids", [P, NTT], I32)
    c_remb_t = din("c_remb_t", [D, RPN])

    out_d = nc.dram_tensor("out", [T, D], F32, kind="ExternalOutput")

    xtok_h = dint("xtok_h", [T, D], BF16)
    inv_h = dint("inv_h", [NSLOT, 1], I32)
    slotout_h = dint("slotout_h", [NSLOT, D], F32)
    skw_h = dint("skw_h", [H * NTT * P, SKW], BF16)
    rp_hd = dint("rp_h", [H * (DH + 1), RPN], BF16)
    vt_hd = dint("vt_h", [T, H * (DH + 1)], BF16)
    cc_in = dint("cc_in", [1, NEXP])
    cc_out = dint("cc_out", [NEXP, NEXP])
    base_h = dint("base_h", [1, NEXP])
    mh_hd = dint("mh_h", [1, H])

    with contextlib.ExitStack() as ST:
        k = K(nc, ST)
        k.mksem("ld", "wld", "wld2", "mm", "ve", "se", "st", "pk", "sk",
                "cc", "ev2", "xk", "so", "tr", "pz", "cc2", "hl")
        block = ST.enter_context(nc.Block())

        sb = lambda n, s, d=F32: ST.enter_context(nc.sbuf_tensor(n, s, d))

        X = sb("X", [P, NTT, D])
        IDB = sb("IDB", [P, P], BF16)
        IDF = sb("IDF", [P, P])
        LST = sb("LST", [P, P])
        ON128 = sb("ON128", [P, P])
        TOK = sb("TOK", [P, NTT], I32)
        LAM = sb("LAM", [P, NCH])
        LAMH = sb("LAMH", [P, NCH])
        LAMLOG = sb("LAMLOG", [P, NCH])
        WNC = sb("WNC", [P, NCH])
        UBT = sb("UBT", [DH, H])
        VBT = sb("VBT", [DH, H])
        PMASK = sb("PMASK", [NEXP, 1])
        GW = sb("GW", [P, NCH, NEXP], BF16)
        LNS = sb("LNS", [P, NTT, 12])
        EPS5 = sb("EPS5", [P, 1])
        EPS11 = sb("EPS11", [P, 1])
        SQ = sb("SQ", [P, 2, D])

        @block.sync
        def _(sync):
            src = AP(x_in, 0, [[D, P], [P * D, NTT], [1, D]])
            k.dinc(sync.dma_start(out=X[:, :, :], in_=src), "ld")
            k.dinc(sync.dma_start(out=IDB[:, :], in_=c_identb[:, :]), "ld")
            k.dinc(sync.dma_start(out=IDF[:, :], in_=c_identf[:, :]), "ld")
            k.dinc(sync.dma_start(out=LST[:, :], in_=c_lstrict[:, :]), "ld")
            k.dinc(sync.dma_start(out=ON128[:, :], in_=c_ones128[:, :]), "ld")
            k.dinc(sync.dma_start(out=TOK[:, :], in_=c_tokids[:, :]), "ld")
            k.dinc(sync.dma_start(out=WNC[:, :], in_=w_norm_c[:, :]), "ld")
            k.dinc(sync.dma_start(out=UBT[:, :], in_=u_bias_t[:, :]), "ld")
            k.dinc(sync.dma_start(out=VBT[:, :], in_=v_bias_t[:, :]), "ld")
            k.dinc(sync.dma_start(out=PMASK[:, :], in_=prev_mask[:, :]), "ld")
            k.dinc(sync.dma_start(out=LAMLOG[:, :], in_=lam_log[:, :]), "ld")

        @block.gpsimd
        def _(gpsimd):
            src = AP(gate_w, 0, [[NEXP, P], [P * NEXP, NCH], [1, NEXP]])
            k.dinc(gpsimd.dma_start(out=GW[:, :, :], in_=src), "wld")

        @block.vector
        def _(vector):
            vector.memset(EPS5[:, :], 1e-5)
            k.inc(vector.memset(EPS11[:, :], 1e-11), "ve")

        @block.scalar
        def _(scalar):
            k.wait(scalar, "ld")
            k.inc(scalar.activation(LAM[:, :], LAMLOG[:, :], AF.Sigmoid), "se")

        @block.vector
        def _(vector):
            k.wait(vector, "se")
            k.inc(vector.tensor_scalar(LAMH[:, :], LAM[:, :], 0.5, None,
                                       ALU.mult), "ve")

        def emit_ln(XLT, rms):
            se0 = k.cnt["se"]
            ve0 = k.cnt["ve"]

            @block.scalar
            def _(scalar):
                k.wait(scalar, "ld")
                k.wait(scalar, "ve")
                k.wait(scalar, "mm")
                for tt in range(NTT):
                    if tt >= 2:
                        k.wait_at(scalar, "ve", ve0 + tt - 1)
                    k.inc(scalar.activation(SQ[:, tt % 2, :], X[:, tt, :],
                                            AF.Square), "se")

            @block.vector
            def _(vector):
                k.wait(vector, "ld")
                for tt in range(NTT):
                    k.wait_at(vector, "se", se0 + tt + 1)
                    vector.reduce_sum(LNS[:, tt, 5:6], X[:, tt, :], AX)
                    k.inc(vector.reduce_sum(LNS[:, tt, 6:7], SQ[:, tt % 2, :],
                                            AX), "ve")
                for tt in range(NTT):
                    vector.tensor_scalar(LNS[:, tt, 0:1], LNS[:, tt, 5:6],
                                         -1.0 / D, None, ALU.mult)
                for tt in range(NTT):
                    vector.tensor_scalar(LNS[:, tt, 7:8], LNS[:, tt, 5:6],
                                         1.0 / D, None, ALU.mult)
                for tt in range(NTT):
                    vector.tensor_tensor(LNS[:, tt, 8:9], LNS[:, tt, 7:8],
                                         LNS[:, tt, 7:8], ALU.mult)
                for tt in range(NTT):
                    vector.tensor_scalar(LNS[:, tt, 9:10], LNS[:, tt, 6:7],
                                         1.0 / D, None, ALU.mult)
                for tt in range(NTT):
                    k.inc(vector.tensor_tensor(LNS[:, tt, 10:11],
                                               LNS[:, tt, 9:10],
                                               LNS[:, tt, 8:9], ALU.subtract),
                          "ve")

            @block.scalar
            def _(scalar):
                k.wait(scalar, "ve")
                for tt in range(NTT):
                    k.inc(scalar.activation(LNS[:, tt, 11:12], LNS[:, tt, 10:11],
                                            AF.Sqrt, bias=EPS5[:, :]), "se")
                if rms:
                    for tt in range(NTT):
                        k.inc(scalar.activation(LNS[:, tt, 5:6],
                                                LNS[:, tt, 10:11], AF.Sqrt,
                                                bias=EPS11[:, :],
                                                scale=1.000001), "se")

            @block.vector
            def _(vector):
                k.wait(vector, "se")
                for tt in range(NTT):
                    k.inc(vector.reciprocal(LNS[:, tt, 1:2], LNS[:, tt, 11:12]),
                          "ve")
                if rms:
                    for tt in range(NTT):
                        k.inc(vector.reciprocal(LNS[:, tt, 2:3], LNS[:, tt, 5:6]),
                              "ve")
                for tt in range(NTT):
                    vector.tensor_scalar(LNS[:, tt, 3:4], LNS[:, tt, 1:2], 1.0,
                                         None, ALU.add)
                for tt in range(NTT):
                    k.inc(vector.tensor_tensor(LNS[:, tt, 4:5], LNS[:, tt, 0:1],
                                               LNS[:, tt, 1:2], ALU.mult), "ve")
                sc = 2 if rms else 1
                for tt in range(NTT):
                    k.inc(vector.tensor_scalar(XLT[:, tt, :], X[:, tt, :],
                                               LNS[:, tt, 0:1],
                                               LNS[:, tt, sc:sc + 1],
                                               ALU.add, ALU.mult), "ve")

        def tap(name, buf):
            if name not in _TAPS:
                return
            dt_ = nc.dram_tensor("dbg_" + name,
                                 [buf.shape[0], int(np.prod(buf.shape[1:]))],
                                 buf.ap().dtype, kind="ExternalOutput")

            @block.sync
            def _(sync):
                k.wait(sync, "ve")
                k.wait(sync, "se")
                k.wait(sync, "mm")
                src = buf.ap().rearrange("p ... -> p (...)")
                k.dinc(sync.dma_start(out=dt_[:, :], in_=src), "st")

        def emit_transposes(XLT, XF, PSA, scale_wnc):
            mm0 = k.cnt["mm"]
            ev0 = k.cnt["ev2"]
            ntile = NTT * NCH

            @block.tensor
            def _(tensor):
                k.wait(tensor, "ve")
                for n in range(ntile):
                    tt, c = divmod(n, NCH)
                    if n >= 16:
                        k.wait_at(tensor, "ev2", ev0 + (n - 16) // 8 + 1)
                    slot = n % 16
                    k.inc(tensor.transpose(
                        PSA[:, slot * P:(slot + 1) * P],
                        XLT[:, tt, c * P:(c + 1) * P], IDB[:, :]), "mm")

            @block.vector
            def _(vector):
                for g in range(ntile // 8):
                    k.wait_at(vector, "mm", mm0 + (g + 1) * 8)
                    for j in range(8):
                        n = g * 8 + j
                        tt, c = divmod(n, NCH)
                        slot = n % 16
                        src = PSA[:, slot * P:(slot + 1) * P]
                        dst = XF[:, c, tt * P:(tt + 1) * P]
                        if scale_wnc:
                            vector.tensor_scalar(dst, src, WNC[:, c:c + 1],
                                                 None, ALU.mult)
                        else:
                            vector.tensor_copy(dst, src)
                    k.inc(vector.engine_nop(), "ev2")

        with contextlib.ExitStack() as PA:
            asb = lambda n, s, d=F32: PA.enter_context(nc.sbuf_tensor(n, s, d))

            XF = asb("aXF", [P, NCH, T], BF16)
            YB = asb("aYB", [P, NCH, T], BF16)

            with contextlib.ExitStack() as PT:
                XLT = PT.enter_context(nc.sbuf_tensor("aXLT", [P, NTT, D], BF16))
                PSA = PT.enter_context(nc.psum_tensor("aPSA", [P, 2048], BF16))
                emit_ln(XLT, rms=True)
                tap("xlt", XLT)
                emit_transposes(XLT, XF, PSA, scale_wnc=True)
                tap("xf", XF)
            mm_xlt_done = k.cnt["mm"]

            WIT = asb("aWIT", [P, 2, NCH, 3, P], BF16)
            WOB = asb("aWOB", [P, NCH, D], BF16)
            AG = asb("aAG", [P, T])
            UG = asb("aUG", [P, T])
            TA2 = asb("aTA2", [P, 2, 512])
            TR = asb("aTR", [P, 2, 512])
            TV = asb("aTV", [P, 2, 512])
            TZV = asb("aTZV", [P, 2, 512])
            T1 = asb("aT1", [P, 2, 512])

            with contextlib.ExitStack() as PP:
                PSP = PP.enter_context(nc.psum_tensor("aPSP", [P, 6, 512], F32))

                @block.gpsimd
                def _(gpsimd):
                    k.wait_at(gpsimd, "mm", mm_xlt_done)
                    if _TAPS:
                        k.wait(gpsimd, "st")
                    src = AP(w_out, 0, [[D, P], [P * D, NCH], [1, D]])
                    k.dinc(gpsimd.dma_start(out=WOB[:, :, :], in_=src), "wld")

                mm_proj = []
                SE_BASE = k.cnt["se"]
                VE_BASE = k.cnt["ve"]
                se_pred = lambda idx: SE_BASE + idx + 1
                ve_pred = lambda idx: VE_BASE + (idx // 4) * 5 + (idx % 4) + 1
                scan_pred = lambda g: VE_BASE + (g + 1) * 5

                for g in range(NCH):
                    @block.gpsimd
                    def _(gpsimd, g=g):
                        if g >= 2:
                            k.wait_at(gpsimd, "mm", mm_proj[(g - 2) * 4 + 3])
                        for kc in range(NCH):
                            src = AP(w_in, kc * P * DFF + g * P,
                                     [[DFF, P], [D, 3], [1, P]])
                            k.dinc(gpsimd.dma_start(
                                out=WIT[:, g % 2, kc, :, :], in_=src), "wld2")

                    @block.tensor
                    def _(tensor, g=g):
                        k.wait_at(tensor, "wld2", 16 * NCH * (g + 1))
                        if g == 0:
                            k.wait(tensor, "ev2")
                        for f in range(4):
                            par = f % 2
                            if g * 4 + f >= 2:
                                k.wait_at(tensor, "se",
                                          se_pred(g * 4 + f - 2))
                            for kc in range(NCH):
                                for s in range(3):
                                    k.inc(tensor.matmul(
                                        PSP[:, par * 3 + s, :],
                                        WIT[:, g % 2, kc, s, :],
                                        XF[:, kc, f * 512:(f + 1) * 512],
                                        start=(kc == 0), stop=(kc == NCH - 1)),
                                        "mm")
                            mm_proj.append(k.cnt["mm"])

                    for f in range(4):
                        idx = g * 4 + f

                        @block.scalar
                        def _(scalar, g=g, f=f, idx=idx):
                            par = f % 2
                            k.wait_at(scalar, "mm", mm_proj[idx])
                            if f == 0 and g >= 1:
                                k.wait_at(scalar, "ve", scan_pred(g - 1))
                            if idx >= 2:
                                k.wait_at(scalar, "ve", ve_pred(idx - 2))
                            fs = slice(f * 512, (f + 1) * 512)
                            scalar.activation(AG[:, fs],
                                              PSP[:, par * 3 + 0, :],
                                              AF.Sigmoid)
                            scalar.activation(TA2[:, par, :],
                                              PSP[:, par * 3 + 0, :],
                                              AF.Sigmoid, scale=-1.0)
                            scalar.activation(TR[:, par, :],
                                              PSP[:, par * 3 + 1, :],
                                              AF.Sigmoid)
                            k.inc(scalar.activation(TV[:, par, :],
                                                    PSP[:, par * 3 + 2, :],
                                                    AF.Gelu), "se")
                            assert k.cnt["se"] == se_pred(idx)

                        @block.vector
                        def _(vector, g=g, f=f, idx=idx):
                            par = f % 2
                            k.wait_at(vector, "se", se_pred(idx))
                            for h2 in range(2):
                                hs = slice(f * 512 + h2 * 256,
                                           f * 512 + h2 * 256 + 256)
                                ph = slice(h2 * 256, h2 * 256 + 256)
                                vector.tensor_tensor(UG[:, hs], TR[:, par, ph],
                                                     TV[:, par, ph], ALU.mult)
                            for h2 in range(2):
                                hs = slice(f * 512 + h2 * 256,
                                           f * 512 + h2 * 256 + 256)
                                vector.tensor_scalar(UG[:, hs], UG[:, hs],
                                                     LAM[:, g:g + 1], None,
                                                     ALU.mult)
                            for h2 in range(2):
                                hs = slice(f * 512 + h2 * 256,
                                           f * 512 + h2 * 256 + 256)
                                ph = slice(h2 * 256, h2 * 256 + 256)
                                ins = vector.tensor_tensor(
                                    UG[:, hs], UG[:, hs], TA2[:, par, ph],
                                    ALU.mult)
                            k.inc(ins, "ve")
                            assert k.cnt["ve"] == ve_pred(idx)

                    @block.vector
                    def _(vector, g=g):
                        k.wait_at(vector, "ve", ve_pred(g * 4 + 3))
                        k.wait_at(vector, "se", se_pred(g * 4 + 3))
                        k.inc(vector.tensor_tensor_scan(
                            YB[:, g, :], AG[:, :], UG[:, :], 0.0,
                            ALU.mult, ALU.add), "ve")
                        assert k.cnt["ve"] == scan_pred(g)

            tap("ag", AG)
            tap("ug", UG)
            tap("yb", YB)

            with contextlib.ExitStack() as PW:
                PWO = PW.enter_context(nc.psum_tensor("aPWO", [P, 2, D], F32))
                mm_wo = []
                ve_wo = []

                for tt in range(NTT):
                    @block.tensor
                    def _(tensor, tt=tt):
                        if tt == 0:
                            k.wait(tensor, "ve")
                            k.wait(tensor, "se")
                            k.wait_at(tensor, "wld", 32)
                        if tt >= 2:
                            k.wait_at(tensor, "ve", ve_wo[tt - 2])
                        par = tt % 2
                        for g in range(NCH):
                            for nh in range(2):
                                k.inc(tensor.matmul(
                                    PWO[:, par, nh * 512:(nh + 1) * 512],
                                    YB[:, g, tt * P:(tt + 1) * P],
                                    WOB[:, g, nh * 512:(nh + 1) * 512],
                                    start=(g == 0), stop=(g == NCH - 1)), "mm")
                        mm_wo.append(k.cnt["mm"])

                    @block.vector
                    def _(vector, tt=tt):
                        k.wait_at(vector, "mm", mm_wo[tt])
                        par = tt % 2
                        for nh in range(2):
                            vector.scalar_tensor_tensor(
                                T1[:, nh, :], X[:, tt, nh * 512:(nh + 1) * 512],
                                LNS[:, tt, 3:4],
                                PWO[:, par, nh * 512:(nh + 1) * 512],
                                ALU.mult, ALU.add)
                        for nh in range(2):
                            hs = slice(nh * 512, (nh + 1) * 512)
                            k.inc(vector.tensor_scalar(X[:, tt, hs],
                                                       T1[:, nh, :],
                                                       LNS[:, tt, 4:5], None,
                                                       ALU.add), "ve")
                        ve_wo.append(k.cnt["ve"])

        def emit_moe(ci):
            ve_pfxread = []
            mm_pfx = []
            with contextlib.ExitStack() as PB:
                bsb = lambda n, s, d=F32: PB.enter_context(
                    nc.sbuf_tensor(n + str(ci), s, d))

                LOG = bsb("bLOG", [P, NTT, NEXP])
                MX = bsb("bMX", [P, NTT])
                NMX = bsb("bNMX", [P, NTT])
                SM = bsb("bSM", [P, NTT])
                RSM = bsb("bRSM", [P, NTT])
                EXL = bsb("bEXL", [P, NTT, NEXP])
                PRB = bsb("bPRB", [P, NTT, NEXP])
                M8 = bsb("bM8", [P, NTT, NEXP])
                IXU = bsb("bIXU", [P, NTT, NEXP], mybir.dt.uint32)
                W12 = bsb("bW12", [P, 2 * NTT])
                E12 = bsb("bE12", [P, 2 * NTT])
                INDA = bsb("bINDA", [P, NEXP, 2 * NTT])
                T2A = bsb("bT2A", [P, NEXP, 2 * NTT])
                RT = bsb("bRT", [P, 16])
                PFX = bsb("bPFX", [P, 16])
                CSX = bsb("bCSX", [P, 16])
                TTMP = bsb("bTTMP", [P, 2 * NTT])
                TTM2 = bsb("bTTM2", [P, 2 * NTT])
                LP = bsb("bLP", [P, 2 * NTT])
                GP = bsb("bGP", [P, 2 * NTT])
                VLD = bsb("bVLD", [P, 2 * NTT])
                WF = bsb("bWF", [P, 2 * NTT])
                SLOT = bsb("bSLOT", [P, 2 * NTT])
                SLI = bsb("bSLI", [P, 2 * NTT], I32)
                SLH = bsb("bSLH", [P, 2 * NTT], I32)
                SH1 = bsb("bSH1", [P, 2 * NTT], I32)
                SH2 = bsb("bSH2", [P, 2 * NTT], I32)
                CNT8 = bsb("bCNT8", [1, NEXP])
                CCS = bsb("bCCS", [NEXP, NEXP])
                BASE = bsb("bBASE", [1, NEXP])
                BASEB = bsb("bBASEB", [P, NEXP])
                ONE16 = bsb("bONE16", [P, 16])
                ZI = bsb("bZI", [P, 40], I32)
                INVR = bsb("bINVR", [P, 40], I32)

                PSAB = PB.enter_context(
                    nc.psum_tensor("bPSA" + str(ci), [P, 2048], BF16))

                @block.vector
                def _(vector):
                    k.wait(vector, "mm")
                    vector.memset(ONE16[:, :], 1.0)
                    vector.memset(LP[:, :], 0.0)
                    vector.memset(GP[:, :], 0.0)
                    k.inc(vector.memset(ZI[:, :], 0), "ve")

                with contextlib.ExitStack() as PBG:
                    XFB = PBG.enter_context(
                        nc.sbuf_tensor("bXFB" + str(ci), [P, NCH, T], BF16))
                    PSG = PBG.enter_context(
                        nc.psum_tensor("bPSG" + str(ci), [NEXP, T], F32))
                    PSL = PBG.enter_context(
                        nc.psum_tensor("bPSL" + str(ci), [P, NTT, NEXP], F32))
                    GSB = PBG.enter_context(
                        nc.sbuf_tensor("bGSB" + str(ci), [NEXP, T], F32))

                    with contextlib.ExitStack() as PBL:
                        XLTB = PBL.enter_context(
                            nc.sbuf_tensor("bXLT" + str(ci), [P, NTT, D], BF16))
                        emit_ln(XLTB, rms=False)

                        @block.sync
                        def _(sync):
                            k.wait(sync, "ve")
                            dst = AP(xtok_h, 0, [[D, P], [P * D, NTT], [1, D]])
                            k.dinc(sync.dma_start(out=dst, in_=XLTB[:, :, :]),
                                   "xk")

                        emit_transposes(XLTB, XFB, PSAB, scale_wnc=False)
                    mm_xltb_done = k.cnt["mm"]

                    @block.tensor
                    def _(tensor):
                        k.wait(tensor, "ev2")
                        k.wait(tensor, "wld")
                        for tt in range(NTT):
                            for c in range(NCH):
                                k.inc(tensor.matmul(
                                    PSG[:, tt * P:(tt + 1) * P],
                                    GW[:, c, :],
                                    XFB[:, c, tt * P:(tt + 1) * P],
                                    start=(c == 0), stop=(c == NCH - 1)), "mm")

                    mm_gate = k.cnt["mm"]

                    @block.vector
                    def _(vector):
                        k.wait_at(vector, "mm", mm_gate)
                        k.inc(vector.tensor_copy(GSB[:, :], PSG[:, :]), "ve")

                    ve_gsb = k.cnt["ve"]

                    @block.tensor
                    def _(tensor):
                        k.wait_at(tensor, "ve", ve_gsb)
                        for tt in range(NTT):
                            k.inc(tensor.transpose(
                                PSL[:, tt, :], GSB[:, tt * P:(tt + 1) * P],
                                IDF[0:NEXP, 0:NEXP]), "mm")

                    mm_psl = k.cnt["mm"]

                    @block.vector
                    def _(vector):
                        k.wait_at(vector, "mm", mm_psl)
                        k.inc(vector.tensor_copy(LOG[:, :, :], PSL[:, :, :]),
                              "ve")
                        for tt in range(NTT):
                            k.inc(vector.reduce_max(MX[:, tt:tt + 1],
                                                    LOG[:, tt, :], AX), "ve")
                        for tt in range(NTT):
                            k.inc(vector.tensor_scalar(NMX[:, tt:tt + 1],
                                                       MX[:, tt:tt + 1], -1.0,
                                                       None, ALU.mult), "ve")

                    se_exp0 = k.cnt["se"]

                    @block.scalar
                    def _(scalar):
                        k.wait(scalar, "ve")
                        for tt in range(NTT):
                            k.inc(scalar.activation(
                                EXL[:, tt, :], LOG[:, tt, :], AF.Exp,
                                bias=NMX[:, tt:tt + 1],
                                accum_out=SM[:, tt:tt + 1]), "se")

                    @block.vector
                    def _(vector):
                        k.wait(vector, "se")
                        for tt in range(NTT):
                            k.inc(vector.reciprocal(RSM[:, tt:tt + 1],
                                                    SM[:, tt:tt + 1]), "ve")
                        for tt in range(NTT):
                            vector.tensor_scalar(PRB[:, tt, :], EXL[:, tt, :],
                                                 RSM[:, tt:tt + 1], None,
                                                 ALU.mult)
                        for tt in range(NTT):
                            k.inc(vector.max(M8[:, tt, :], PRB[:, tt, :]),
                                  "ve")
                        for tt in range(NTT):
                            k.inc(vector.max_index(IXU[:, tt, :], M8[:, tt, :],
                                                   PRB[:, tt, :]), "ve")
                        for tt in range(NTT):
                            vector.tensor_copy(W12[:, 2 * tt:2 * tt + 2],
                                               M8[:, tt, 0:2])
                        for tt in range(NTT):
                            k.inc(vector.tensor_copy(E12[:, 2 * tt:2 * tt + 2],
                                                     IXU[:, tt, 0:2]), "ve")

                with contextlib.ExitStack() as PBR:
                    PSR = PBR.enter_context(
                        nc.psum_tensor("bPSR" + str(ci), [P, 64], F32))

                    for e in range(NEXP):
                        @block.vector
                        def _(vector, e=e):
                            if e == 0:
                                k.wait(vector, "ve")
                            if e >= 1:
                                k.wait_at(vector, "mm", mm_pfx[e - 1])
                            k.inc(vector.tensor_single_scalar(
                                INDA[:, e, :], E12[:, :], float(e),
                                ALU.is_equal), "ve")
                            k.wait(vector, "ve")
                            k.inc(vector.tensor_tensor(
                                RT[:, :], INDA[:, e, 0:32:2],
                                INDA[:, e, 1:32:2], ALU.add), "ve")

                        ve_rt = k.cnt["ve"]

                        @block.tensor
                        def _(tensor, e=e):
                            k.wait_at(tensor, "ve", ve_rt)
                            if e >= 1:
                                k.wait_at(tensor, "ve", ve_pfxread[e - 1])
                            k.inc(tensor.matmul(PSR[:, 0:16], LST[:, :],
                                                RT[:, :], start=True,
                                                stop=True), "mm")
                            k.inc(tensor.matmul(PSR[:, 16:32], ON128[:, :],
                                                RT[:, :], start=True,
                                                stop=True), "mm")

                        mm_pfx.append(k.cnt["mm"])

                        @block.vector
                        def _(vector, e=e):
                            k.wait_at(vector, "mm", mm_pfx[e])
                            vector.tensor_copy(PFX[:, :], PSR[:, 0:16])
                            vector.tensor_tensor_scan(
                                CSX[:, 1:16], ONE16[:, 1:16], PSR[:, 16:31],
                                0.0, ALU.mult, ALU.add)
                            vector.memset(CSX[:, 0:1], 0.0)
                            k.inc(vector.reduce_sum(CNT8[0:1, e:e + 1],
                                                    PSR[0:1, 16:32], AX), "ve")
                            k.wait(vector, "ve")
                            vector.tensor_tensor(TTMP[:, 0:16], PFX[:, :],
                                                 CSX[:, :], ALU.add)
                            k.inc(vector.engine_nop(), "ve")
                            k.wait(vector, "ve")
                            vector.tensor_copy(T2A[:, e, 0:32:2], TTMP[:, 0:16])
                            vector.tensor_tensor(T2A[:, e, 1:32:2],
                                                 TTMP[:, 0:16],
                                                 INDA[:, e, 0:32:2], ALU.add)
                            k.inc(vector.engine_nop(), "ve")
                            k.wait(vector, "ve")
                            vector.tensor_tensor(TTM2[:, :], INDA[:, e, :],
                                                 T2A[:, e, :], ALU.mult)
                            k.inc(vector.engine_nop(), "ve")
                            k.wait(vector, "ve")
                            k.inc(vector.tensor_tensor(LP[:, :], LP[:, :],
                                                       TTM2[:, :], ALU.add),
                                  "ve")
                            ve_pfxread.append(k.cnt["ve"])

                    @block.sync
                    def _(sync):
                        k.wait(sync, "ve")
                        k.dinc(sync.dma_start(out=cc_in[:, :], in_=CNT8[:, :]),
                               "cc")

                    @block.gpsimd
                    def _(gpsimd):
                        k.wait(gpsimd, "cc")
                        k.inc(gpsimd.collective_compute(
                            "AllGather", ALU.bypass,
                            replica_groups=[list(range(8))],
                            ins=[cc_in.ap().opt()],
                            outs=[cc_out.ap().opt()]), "cc")
                        k.wait(gpsimd, "cc")
                        k.dinc(gpsimd.dma_start(out=CCS[:, :],
                                                in_=cc_out[:, :]), "cc2")

                    @block.tensor
                    def _(tensor):
                        k.wait(tensor, "cc2")
                        k.inc(tensor.matmul(PSR[0:1, 32:40], PMASK[:, :],
                                            CCS[:, :], start=True, stop=True),
                              "mm")

                    mm_base = k.cnt["mm"]

                    @block.vector
                    def _(vector):
                        k.wait_at(vector, "mm", mm_base)
                        k.inc(vector.tensor_copy(BASE[:, :], PSR[0:1, 32:40]),
                              "ve")

                    @block.sync
                    def _(sync):
                        k.wait(sync, "ve")
                        k.dinc(sync.dma_start(out=base_h[:, :], in_=BASE[:, :]),
                               "cc")
                        k.wait(sync, "cc")
                        k.dinc(sync.dma_start(
                            out=BASEB[:, :],
                            in_=base_h[:, :].to_broadcast((P, NEXP))), "cc")

                    @block.vector
                    def _(vector):
                        k.wait(vector, "cc")
                        for e in range(NEXP):
                            vector.tensor_scalar(TTMP[:, :], T2A[:, e, :],
                                                 BASEB[:, e:e + 1], None,
                                                 ALU.add)
                            k.inc(vector.engine_nop(), "ve")
                            k.wait(vector, "ve")
                            vector.tensor_tensor(TTM2[:, :], INDA[:, e, :],
                                                 TTMP[:, :], ALU.mult)
                            k.inc(vector.engine_nop(), "ve")
                            k.wait(vector, "ve")
                            k.inc(vector.tensor_tensor(GP[:, :], GP[:, :],
                                                       TTM2[:, :], ALU.add),
                                  "ve")
                            k.wait(vector, "ve")
                        vector.tensor_single_scalar(VLD[:, :], GP[:, :],
                                                    float(CAPACITY - 1),
                                                    ALU.is_le)
                        vector.tensor_scalar_min(SLOT[:, :], LP[:, :],
                                                 float(ECAP - 1))
                        k.inc(vector.engine_nop(), "ve")
                        k.wait(vector, "ve")
                        vector.tensor_tensor(WF[:, :], W12[:, :], VLD[:, :],
                                             ALU.mult)
                        vector.scalar_tensor_tensor(SLOT[:, :], E12[:, :],
                                                    float(ECAP), SLOT[:, :],
                                                    ALU.mult, ALU.add)
                        k.inc(vector.engine_nop(), "ve")
                        k.wait(vector, "ve")
                        k.inc(vector.tensor_copy(SLI[:, :], SLOT[:, :]), "ve")
                        k.wait(vector, "ve")
                        vector.tensor_single_scalar(SH1[:, :], SLI[:, :], 127,
                                                    ALU.bitwise_and)
                        vector.tensor_single_scalar(SH2[:, :], SLI[:, :], 7,
                                                    ALU.arith_shift_right)
                        k.inc(vector.engine_nop(), "ve")
                        k.wait(vector, "ve")
                        vector.tensor_single_scalar(SH1[:, :], SH1[:, :], 40,
                                                    ALU.mult)
                        k.inc(vector.engine_nop(), "ve")
                        k.wait(vector, "ve")
                        k.inc(vector.tensor_tensor(SLH[:, :], SH1[:, :],
                                                   SH2[:, :], ALU.add), "ve")

                    ve_slots = k.cnt["ve"]

                    @block.sync
                    def _(sync):
                        k.wait(sync, "ve")
                        iv = AP(inv_h, 0, [[40, P], [1, 40]])
                        k.dinc(sync.dma_start(out=iv, in_=ZI[:, :]), "pz")

                    @block.gpsimd
                    def _(gpsimd):
                        k.wait(gpsimd, "pz")
                        k.wait_at(gpsimd, "ve", ve_slots)
                        for col in range(2 * NTT):
                            tt = col // 2
                            k.dinc(gpsimd.indirect_dma_start(
                                inv_h[:, :],
                                IndirectOffsetOnAxis(SLH[:, col:col + 1], 0),
                                TOK[:, tt:tt + 1], None), "pk")

                    @block.sync
                    def _(sync):
                        k.wait(sync, "pk")
                        iv = AP(inv_h, 0, [[40, P], [1, 40]])
                        k.dinc(sync.dma_start(out=INVR[:, :], in_=iv), "pz")

                PBE = PB.enter_context(contextlib.ExitStack())
                esb = lambda n, s, d=F32: PBE.enter_context(
                    nc.sbuf_tensor(n + str(ci), s, d))
                PKT = esb("bPKT", [P, 2, D], BF16)
                PACKE = esb("bPACKE", [P, 2, NCH, ECAP], BF16)
                CFCE = esb("bCFCE", [P, NCH, DFF], BF16)
                CPJ = esb("bCPJ", [P, 12, D], BF16)
                HBF = esb("bHBF", [P, 12, ECAP], BF16)
                TSIL = esb("bTSIL", [P, 2, 512])
                SLOB = esb("bSLOB", [P, 2, D])

                PFC = PBE.enter_context(
                    nc.psum_tensor("bPFC" + str(ci), [P, 4, 512], F32))
                PPJ = PBE.enter_context(
                    nc.psum_tensor("bPPJ" + str(ci), [P, D], F32))

                pk_base = k.cnt["pk"]
                so_base = k.cnt["so"]
                tr_base = k.cnt["tr"]
                mm_tr = []
                ve_pke = []
                mm_fc = []
                ve_hbf = []
                ve_hbf_cb = []
                mm_pj = []
                ve_slb = []
                cbw = [(0, 512), (512, 128)]

                for e in range(NEXP):
                    @block.gpsimd
                    def _(gpsimd, e=e):
                        if e == 0:
                            k.wait(gpsimd, "pz")
                            k.wait(gpsimd, "xk")
                        for st in range(5):
                            gst = e * 5 + st
                            if gst >= 2:
                                k.wait_at(gpsimd, "tr", tr_base + gst - 1)
                            k.dinc(gpsimd.indirect_dma_start(
                                PKT[:, gst % 2, :], None,
                                xtok_h[:, :],
                                IndirectOffsetOnAxis(
                                    INVR[:, gst:gst + 1], 0)), "pk")
                        if e >= 1:
                            k.wait_at(gpsimd, "mm",
                                      mm_fc[((e - 1) * 12 + 11) * 2 + 1])
                        for kc in range(NCH):
                            k.dinc(gpsimd.dma_start(
                                out=CFCE[:, kc, :],
                                in_=c_fc[e, kc * P:(kc + 1) * P, :]), "wld2")
                        if e >= 1:
                            k.wait_at(gpsimd, "mm", mm_pj[e * 5 - 1])
                        for k2 in range(12):
                            k.dinc(gpsimd.dma_start(
                                out=CPJ[:, k2, :],
                                in_=c_proj[e, k2 * P:(k2 + 1) * P, :]), "wld2")

                    for st in range(5):
                        gst = e * 5 + st

                        @block.tensor
                        def _(tensor, e=e, st=st, gst=gst):
                            k.wait_at(tensor, "pk",
                                      pk_base + 16 * (gst + 1))
                            if gst < 2:
                                k.wait(tensor, "ev2")
                            if gst >= 1:
                                k.wait_at(tensor, "ve", ve_pke[gst - 1])
                            for c in range(NCH):
                                slot = (gst % 2) * 8 + c
                                k.inc(tensor.transpose(
                                    PSAB[:, slot * P:(slot + 1) * P],
                                    PKT[:, gst % 2, c * P:(c + 1) * P],
                                    IDB[:, :]), "mm")
                            k.inc(tensor.nop(), "tr")
                            mm_tr.append(k.cnt["mm"])

                        @block.vector
                        def _(vector, e=e, st=st, gst=gst):
                            k.wait_at(vector, "mm", mm_tr[gst])
                            for c in range(NCH):
                                slot = (gst % 2) * 8 + c
                                vector.tensor_copy(
                                    PACKE[:, e % 2, c,
                                          st * P:(st + 1) * P],
                                    PSAB[:, slot * P:(slot + 1) * P])
                            k.inc(vector.engine_nop(), "ve")
                            ve_pke.append(k.cnt["ve"])

                    for pb in range(12):
                        @block.tensor
                        def _(tensor, e=e, pb=pb):
                            if pb == 0:
                                k.wait_at(tensor, "wld2", k.cnt["wld2"])
                                k.wait_at(tensor, "ve", ve_pke[e * 5 + 4])
                            for cb, (c0, cw) in enumerate(cbw):
                                if e * 12 + pb >= 1:
                                    k.wait_at(tensor, "ve",
                                              ve_hbf_cb[
                                                  (e * 12 + pb - 1) * 2 + cb])
                                for half, mc in enumerate((pb, pb + 12)):
                                    for kc in range(NCH):
                                        k.inc(tensor.matmul(
                                            PFC[:, cb * 2 + half, 0:cw],
                                            CFCE[:, kc,
                                                 mc * P:(mc + 1) * P],
                                            PACKE[:, e % 2, kc,
                                                  c0:c0 + cw],
                                            start=(kc == 0),
                                            stop=(kc == NCH - 1)), "mm")
                                mm_fc.append(k.cnt["mm"])

                        @block.scalar
                        def _(scalar, e=e, pb=pb):
                            if e * 12 + pb >= 1:
                                k.wait_at(scalar, "ve",
                                          ve_hbf[e * 12 + pb - 1])
                            for cb, (c0, cw) in enumerate(cbw):
                                k.wait_at(scalar, "mm",
                                          mm_fc[(e * 12 + pb) * 2 + cb])
                                k.inc(scalar.activation(
                                    TSIL[:, cb, 0:cw],
                                    PFC[:, cb * 2 + 0, 0:cw], AF.Silu), "se")

                        se_here = k.cnt["se"]

                        @block.vector
                        def _(vector, e=e, pb=pb, se_here=se_here):
                            k.wait_at(vector, "se", se_here)
                            if pb == 0 and e >= 1:
                                k.wait_at(vector, "mm", mm_pj[e * 5 - 1])
                            for cb, (c0, cw) in enumerate(cbw):
                                k.inc(vector.tensor_tensor(
                                    HBF[:, pb, c0:c0 + cw],
                                    TSIL[:, cb, 0:cw],
                                    PFC[:, cb * 2 + 1, 0:cw], ALU.mult), "ve")
                                ve_hbf_cb.append(k.cnt["ve"])
                            ve_hbf.append(k.cnt["ve"])

                    for s in range(5):
                        @block.tensor
                        def _(tensor, e=e, s=s):
                            if s == 0:
                                k.wait_at(tensor, "ve", ve_hbf[e * 12 + 11])
                            if e * 5 + s >= 1:
                                k.wait_at(tensor, "ve", ve_slb[e * 5 + s - 1])
                            for k2 in range(12):
                                for nh in range(2):
                                    k.inc(tensor.matmul(
                                        PPJ[:, nh * 512:(nh + 1) * 512],
                                        HBF[:, k2, s * P:(s + 1) * P],
                                        CPJ[:, k2, nh * 512:(nh + 1) * 512],
                                        start=(k2 == 0), stop=(k2 == 11)),
                                        "mm")
                            mm_pj.append(k.cnt["mm"])

                        @block.vector
                        def _(vector, e=e, s=s):
                            gs2 = e * 5 + s
                            k.wait_at(vector, "mm", mm_pj[gs2])
                            if gs2 >= 2:
                                k.wait_at(vector, "so",
                                          so_base + 16 * (gs2 - 1))
                            for nh in range(2):
                                k.inc(vector.tensor_copy(
                                    SLOB[:, gs2 % 2, nh * 512:(nh + 1) * 512],
                                    PPJ[:, nh * 512:(nh + 1) * 512]), "ve")
                            ve_slb.append(k.cnt["ve"])

                        @block.sync
                        def _(sync, e=e, s=s):
                            gs2 = e * 5 + s
                            k.wait_at(sync, "ve", ve_slb[gs2])
                            r0 = e * ECAP + s * P
                            k.dinc(sync.dma_start(
                                out=slotout_h[r0:r0 + P, :],
                                in_=SLOB[:, gs2 % 2, :]), "so")

                PBE.close()
                UPK = bsb("bUPK", [P, 2, 2, D])

                veu0 = k.cnt["ve"]

                @block.gpsimd
                def _(gpsimd):
                    k.wait(gpsimd, "so")
                    for tt in range(NTT):
                        if tt >= 2:
                            k.wait_at(gpsimd, "ve", veu0 + 3 * (tt - 1))
                        for c in range(2):
                            k.dinc(gpsimd.indirect_dma_start(
                                UPK[:, tt % 2, c, :], None,
                                slotout_h[:, :],
                                IndirectOffsetOnAxis(
                                    SLI[:, 2 * tt + c:2 * tt + c + 1], 0)),
                                "pk")

                pk_upk0 = k.cnt["pk"] - 16 * 2 * NTT

                for tt in range(NTT):
                    @block.vector
                    def _(vector, tt=tt):
                        k.wait_at(vector, "pk", pk_upk0 + 32 * (tt + 1))
                        par = tt % 2
                        vector.tensor_scalar(UPK[:, par, 1, :],
                                             UPK[:, par, 1, :],
                                             WF[:, 2 * tt + 1:2 * tt + 2],
                                             None, ALU.mult)
                        vector.tensor_scalar(UPK[:, par, 0, :],
                                             UPK[:, par, 0, :],
                                             WF[:, 2 * tt:2 * tt + 1],
                                             None, ALU.mult)
                        k.inc(vector.engine_nop(), "ve")
                        k.wait(vector, "ve")
                        vector.tensor_tensor(UPK[:, par, 0, :],
                                             UPK[:, par, 0, :],
                                             UPK[:, par, 1, :], ALU.add)
                        k.inc(vector.engine_nop(), "ve")
                        k.wait(vector, "ve")
                        k.inc(vector.tensor_tensor(X[:, tt, :], X[:, tt, :],
                                                   UPK[:, par, 0, :], ALU.add),
                              "ve")
                        assert k.cnt["ve"] == veu0 + 3 * (tt + 1)

        qf_h = dint("qf_h", [H * (DH + 1), QFW], BF16)
        kf_h = dint("kf_h", [H * (DH + 1), T], BF16)

        def emit_attn_b():
            with contextlib.ExitStack() as PC:
                csb = lambda n, s, d=F32: PC.enter_context(
                    nc.sbuf_tensor(n, s, d))

                MK = csb("cMK", [DH, H])
                MK1 = csb("cMK1", [1, H])
                MRP = csb("cMRP", [DH, H])
                MRP1 = csb("cMRP1", [1, H])
                MKS = csb("cMKS", [DH, 1])
                MKSB = csb("cMKSB", [DH, 1], BF16)
                MH = csb("cMH", [1, H + 2])
                NMHB = csb("cNMHB", [P, H])
                UBTB = csb("cUBTB", [DH, H], BF16)
                VBTB = csb("cVBTB", [DH, H], BF16)
                XFC = csb("cXFC", [P, NCH, T], BF16)

                @block.vector
                def _(vector):
                    k.wait(vector, "ld")
                    vector.tensor_copy(UBTB[:, :], UBT[:, :])
                    k.inc(vector.tensor_copy(VBTB[:, :], VBT[:, :]), "ve")

                with contextlib.ExitStack() as PCW:
                    wsb = lambda n, s, d=F32: PCW.enter_context(
                        nc.sbuf_tensor(n, s, d))
                    PSAC = PCW.enter_context(
                        nc.psum_tensor("cPSAC", [P, 2048], BF16))
                    PC1 = PCW.enter_context(
                        nc.psum_tensor("cPC1", [P, 4, 512], F32))
                    PB1 = PCW.enter_context(
                        nc.psum_tensor("cPB1", [1, 2, 512], F32))
                    BNDS = SQ[0:1, :, :].rearrange("p a b -> p (a b)")

                    with contextlib.ExitStack() as PCL:
                        XLTC = PCL.enter_context(
                            nc.sbuf_tensor("cXLT", [P, NTT, D], BF16))
                        emit_ln(XLTC, rms=False)
                        emit_transposes(XLTC, XFC, PSAC, scale_wnc=False)

                    WRAW = wsb("cWRAW", [P, NCH, D], BF16)
                    WT = wsb("cWT", [P, NCH, D], BF16)
                    QFS = wsb("cQFS", [DH + 1, 2, QFW], BF16)
                    KFS = wsb("cKFS", [DH, 2, T], BF16)
                    KRW = wsb("cKRW", [1, 2, T], BF16)
                    RRW = wsb("cRRW", [1, 2, 1024], BF16)
                    ABSQ = wsb("cABSQ", [DH, T], BF16)
                    VS = wsb("cVS", [P, 2, H * (DH + 1)], BF16)
                    MRP2 = wsb("cMRP2", [DH, 1])

                    @block.vector
                    def _(vector):
                        k.wait(vector, "mm")
                        vector.memset(QFS[DH:DH + 1, :, :], 1.0)
                        vector.memset(QFS[:, :, T:QFW], 0.0)
                        k.inc(vector.memset(VS[:, :, :], 1.0), "ve")

                    def load_wt(wmat_d):
                        @block.gpsimd
                        def _(gpsimd):
                            k.wait(gpsimd, "mm")
                            src = AP(wmat_d, 0,
                                     [[D, P], [P * D, NCH], [1, D]])
                            k.dinc(gpsimd.dma_start(out=WRAW[:, :, :],
                                                    in_=src), "wld2")

                        mmw0 = k.cnt["mm"]
                        ev0 = k.cnt["ev2"]

                        @block.tensor
                        def _(tensor):
                            k.wait(tensor, "wld2")
                            k.wait(tensor, "ev2")
                            for n in range(NCH * NCH):
                                oc, ic = divmod(n, NCH)
                                if n >= 16:
                                    k.wait_at(tensor, "ev2",
                                              ev0 + (n - 16) // 8 + 1)
                                slot = n % 16
                                k.inc(tensor.transpose(
                                    PSAC[:, slot * P:(slot + 1) * P],
                                    WRAW[:, oc, ic * P:(ic + 1) * P],
                                    IDB[:, :]), "mm")

                        @block.vector
                        def _(vector):
                            for g in range(NCH * NCH // 8):
                                k.wait_at(vector, "mm", mmw0 + (g + 1) * 8)
                                for j in range(8):
                                    n = g * 8 + j
                                    oc, ic = divmod(n, NCH)
                                    slot = n % 16
                                    vector.tensor_copy(
                                        WT[:, ic, oc * P:(oc + 1) * P],
                                        PSAC[:, slot * P:(slot + 1) * P])
                                k.inc(vector.engine_nop(), "ev2")

                    load_wt(wr_d)
                    mm_rpj = []
                    ve_rpj = []
                    ve_rr = []
                    rp_s0 = k.cnt["sk"]
                    with contextlib.ExitStack() as PCR:
                        REMBF = PCR.enter_context(
                            nc.sbuf_tensor("cREMBF", [P, NCH, 1024], BF16))
                        RPJS = PCR.enter_context(
                            nc.sbuf_tensor("cRPJS", [DH, 2, 1024], BF16))
                        for q in range(4):
                            @block.gpsimd
                            def _(gpsimd, q=q):
                                if q == 0:
                                    k.wait(gpsimd, "mm")
                                else:
                                    k.wait_at(gpsimd, "mm", mm_rpj[-1])
                                src = AP(c_remb_t, q * 1024,
                                         [[RPN, P], [P * RPN, NCH], [1, 1024]])
                                k.dinc(gpsimd.dma_start(out=REMBF[:, :, :],
                                                        in_=src), "wld2")

                            for h in range(H):
                                hpar = h % 2
                                hi = q * H + h

                                @block.tensor
                                def _(tensor, q=q, h=h, hi=hi):
                                    k.wait(tensor, "wld2")
                                    if hi == 0:
                                        k.wait(tensor, "ev2")
                                    if hi >= 2:
                                        k.wait_at(tensor, "sk",
                                                  rp_s0 + 32 * (hi - 1))
                                    for fc in range(2):
                                        for ic in range(NCH):
                                            k.inc(tensor.matmul(
                                                PC1[0:DH, fc, :],
                                                WT[:, ic,
                                                   h * DH:(h + 1) * DH],
                                                REMBF[:, ic,
                                                      fc * 512:(fc + 1) * 512],
                                                start=(ic == 0),
                                                stop=(ic == NCH - 1)), "mm")
                                    mm_rpj.append(k.cnt["mm"])

                                @block.vector
                                def _(vector, hi=hi, hpar=hpar):
                                    k.wait_at(vector, "mm", mm_rpj[hi])
                                    for fc in range(2):
                                        vector.tensor_copy(
                                            RPJS[:, hpar,
                                                 fc * 512:(fc + 1) * 512],
                                            PC1[0:DH, fc, :])
                                    k.inc(vector.engine_nop(), "ve")
                                    ve_rpj.append(k.cnt["ve"])

                                @block.tensor
                                def _(tensor, h=h, hi=hi, hpar=hpar):
                                    k.wait_at(tensor, "ve", ve_rpj[hi])
                                    for fc in range(2):
                                        k.inc(tensor.matmul(
                                            PB1[:, fc, :],
                                            VBTB[:, h:h + 1],
                                            RPJS[:, hpar,
                                                 fc * 512:(fc + 1) * 512],
                                            start=True, stop=True), "mm")

                                mm_rrow = k.cnt["mm"]

                                @block.vector
                                def _(vector, q=q, h=h, hpar=hpar,
                                      mm_rrow=mm_rrow):
                                    k.wait_at(vector, "mm", mm_rrow)
                                    for fc in range(2):
                                        vector.tensor_copy(
                                            RRW[:, hpar,
                                                fc * 512:(fc + 1) * 512],
                                            PB1[:, fc, :])
                                    vector.reduce_max(
                                        MRP2[:, 0:1], RPJS[:, hpar, :], AX,
                                        apply_absolute_value=True)
                                    k.inc(vector.engine_nop(), "ve")
                                    k.wait(vector, "ve")
                                    if q == 0:
                                        vector.tensor_copy(MRP[:, h:h + 1],
                                                           MRP2[:, 0:1])
                                        vector.reduce_max(
                                            MRP1[:, h:h + 1], RRW[:, hpar, :],
                                            AX, apply_absolute_value=True)
                                    else:
                                        vector.tensor_tensor(
                                            MRP[:, h:h + 1], MRP[:, h:h + 1],
                                            MRP2[:, 0:1], ALU.max)
                                        vector.reduce_max(
                                            MH[:, H + 1:H + 2],
                                            RRW[:, hpar, :], AX,
                                            apply_absolute_value=True)
                                        k.inc(vector.engine_nop(), "ve")
                                        k.wait(vector, "ve")
                                        vector.tensor_tensor(
                                            MRP1[:, h:h + 1], MRP1[:, h:h + 1],
                                            MH[:, H + 1:H + 2], ALU.max)
                                    k.inc(vector.engine_nop(), "ve")

                                @block.sync
                                def _(sync, q=q, h=h, hpar=hpar):
                                    k.wait(sync, "ve")
                                    cs = slice(q * 1024, (q + 1) * 1024)
                                    k.dinc(sync.dma_start(
                                        out=rp_hd[h * (DH + 1):
                                                  h * (DH + 1) + DH, cs],
                                        in_=RPJS[:, hpar, :]), "sk")
                                    k.dinc(sync.dma_start(
                                        out=rp_hd[h * (DH + 1) + DH:
                                                  h * (DH + 1) + DH + 1, cs],
                                        in_=RRW[:, hpar, :]), "sk")

                    if _C_STAGES < 2:
                        return
                    load_wt(wk_d)
                    mm_k = []
                    ve_k = []
                    ve_kr = []
                    k_s0 = k.cnt["sk"]
                    for h in range(H):
                        hpar = h % 2

                        @block.tensor
                        def _(tensor, h=h, hpar=hpar):
                            if h == 0:
                                k.wait(tensor, "ev2")
                            if h >= 2:
                                k.wait_at(tensor, "sk", k_s0 + 32 * (h - 1))
                            for fc in range(4):
                                for ic in range(NCH):
                                    k.inc(tensor.matmul(
                                        PC1[0:DH, fc, :],
                                        WT[:, ic, h * DH:(h + 1) * DH],
                                        XFC[:, ic, fc * 512:(fc + 1) * 512],
                                        start=(ic == 0),
                                        stop=(ic == NCH - 1)), "mm")
                            mm_k.append(k.cnt["mm"])

                        @block.vector
                        def _(vector, h=h, hpar=hpar):
                            k.wait_at(vector, "mm", mm_k[h])
                            for fc in range(4):
                                vector.tensor_copy(
                                    KFS[:, hpar, fc * 512:(fc + 1) * 512],
                                    PC1[0:DH, fc, :])
                            k.inc(vector.engine_nop(), "ve")
                            k.wait(vector, "ve")
                            k.inc(vector.reduce_max(
                                MK[:, h:h + 1], KFS[:, hpar, :], AX,
                                apply_absolute_value=True), "ve")
                            ve_k.append(k.cnt["ve"])

                        for rnd in range(2):
                            @block.tensor
                            def _(tensor, h=h, hpar=hpar, rnd=rnd):
                                if rnd == 0:
                                    k.wait_at(tensor, "ve", ve_k[h])
                                else:
                                    k.wait_at(tensor, "ve", ve_kr[-1])
                                for fc2 in range(2):
                                    fc = rnd * 2 + fc2
                                    k.inc(tensor.matmul(
                                        PB1[:, fc2, :], UBTB[:, h:h + 1],
                                        KFS[:, hpar, fc * 512:(fc + 1) * 512],
                                        start=True, stop=True), "mm")

                            mm_krow = k.cnt["mm"]

                            @block.vector
                            def _(vector, hpar=hpar, rnd=rnd,
                                  mm_krow=mm_krow):
                                k.wait_at(vector, "mm", mm_krow)
                                for fc2 in range(2):
                                    fc = rnd * 2 + fc2
                                    vector.tensor_copy(
                                        KRW[:, hpar,
                                            fc * 512:(fc + 1) * 512],
                                        PB1[:, fc2, :])
                                k.inc(vector.engine_nop(), "ve")
                                ve_kr.append(k.cnt["ve"])

                        @block.vector
                        def _(vector, h=h, hpar=hpar):
                            k.wait(vector, "ve")
                            k.inc(vector.reduce_max(
                                MK1[:, h:h + 1], KRW[:, hpar, :], AX,
                                apply_absolute_value=True), "ve")

                        @block.sync
                        def _(sync, h=h, hpar=hpar):
                            k.wait(sync, "ve")
                            k.dinc(sync.dma_start(
                                out=kf_h[h * (DH + 1):h * (DH + 1) + DH, :],
                                in_=KFS[:, hpar, :]), "sk")
                            k.dinc(sync.dma_start(
                                out=kf_h[h * (DH + 1) + DH:
                                         h * (DH + 1) + DH + 1, :],
                                in_=KRW[:, hpar, :]), "sk")

                    load_wt(wq_d)
                    mm_q = []
                    ve_q = []
                    ve_bn = []
                    q_s0 = k.cnt["sk"]
                    for h in range(H):
                        hpar = h % 2

                        @block.tensor
                        def _(tensor, h=h, hpar=hpar):
                            if h == 0:
                                k.wait(tensor, "ev2")
                            if h >= 2:
                                k.wait_at(tensor, "sk", q_s0 + 16 * (h - 1))
                            for fc in range(4):
                                for ic in range(NCH):
                                    k.inc(tensor.matmul(
                                        PC1[0:DH, fc, :],
                                        WT[:, ic, h * DH:(h + 1) * DH],
                                        XFC[:, ic, fc * 512:(fc + 1) * 512],
                                        start=(ic == 0),
                                        stop=(ic == NCH - 1)), "mm")
                            mm_q.append(k.cnt["mm"])

                        @block.vector
                        def _(vector, h=h, hpar=hpar):
                            k.wait_at(vector, "mm", mm_q[h])
                            for fc in range(4):
                                vector.tensor_copy(
                                    QFS[0:DH, hpar, fc * 512:(fc + 1) * 512],
                                    PC1[0:DH, fc, :])
                            vector.tensor_tensor(MKS[:, 0:1], MK[:, h:h + 1],
                                                 MRP[:, h:h + 1], ALU.add)
                            k.inc(vector.engine_nop(), "ve")
                            k.wait(vector, "ve")
                            k.inc(vector.tensor_copy(MKSB[:, 0:1],
                                                     MKS[:, 0:1]), "ve")
                            ve_q.append(k.cnt["ve"])

                        @block.scalar
                        def _(scalar, h=h, hpar=hpar):
                            k.wait_at(scalar, "ve", ve_q[h])
                            k.inc(scalar.activation(
                                ABSQ[:, :], QFS[0:DH, hpar, 0:T],
                                AF.Abs), "se")

                        se_abs = k.cnt["se"]

                        for rnd in range(2):
                            @block.tensor
                            def _(tensor, h=h, hpar=hpar, rnd=rnd,
                                  se_abs=se_abs):
                                if rnd == 0:
                                    k.wait_at(tensor, "se", se_abs)
                                else:
                                    k.wait_at(tensor, "ve", ve_bn[-1])
                                for fc2 in range(2):
                                    fc = rnd * 2 + fc2
                                    k.inc(tensor.matmul(
                                        PB1[:, fc2, :], MKSB[:, 0:1],
                                        ABSQ[:,
                                             fc * 512:(fc + 1) * 512],
                                        start=True, stop=True), "mm")

                            mm_bnd = k.cnt["mm"]

                            @block.vector
                            def _(vector, rnd=rnd, mm_bnd=mm_bnd):
                                k.wait_at(vector, "mm", mm_bnd)
                                vector.tensor_copy(
                                    BNDS[:, rnd * 1024:rnd * 1024 + 512],
                                    PB1[:, 0, :])
                                vector.tensor_copy(
                                    BNDS[:, rnd * 1024 + 512:
                                         (rnd + 1) * 1024],
                                    PB1[:, 1, :])
                                k.inc(vector.engine_nop(), "ve")
                                ve_bn.append(k.cnt["ve"])

                        @block.vector
                        def _(vector, h=h):
                            k.wait(vector, "ve")
                            vector.reduce_max(MH[:, H:H + 1], BNDS[:, :], AX)
                            k.inc(vector.engine_nop(), "ve")
                            k.wait(vector, "ve")
                            vector.tensor_tensor(MH[:, H:H + 1],
                                                 MH[:, H:H + 1],
                                                 MK1[:, h:h + 1], ALU.add)
                            k.inc(vector.engine_nop(), "ve")
                            k.wait(vector, "ve")
                            k.inc(vector.tensor_tensor(
                                MH[:, h:h + 1], MH[:, H:H + 1],
                                MRP1[:, h:h + 1], ALU.add), "ve")

                        @block.sync
                        def _(sync, h=h, hpar=hpar):
                            k.wait(sync, "ve")
                            k.dinc(sync.dma_start(
                                out=qf_h[h * (DH + 1):(h + 1) * (DH + 1), :],
                                in_=QFS[:, hpar, :]), "sk")

                    @block.vector
                    def _(vector):
                        k.wait(vector, "ve")
                        k.inc(vector.tensor_scalar(MH[:, 0:H], MH[:, 0:H],
                                                   -0.125, None, ALU.mult),
                              "ve")

                    @block.sync
                    def _(sync):
                        k.wait(sync, "ve")
                        k.dinc(sync.dma_start(out=mh_hd[:, :],
                                              in_=MH[:, 0:H]), "cc")
                        k.wait(sync, "cc")
                        k.dinc(sync.dma_start(
                            out=NMHB[:, :],
                            in_=mh_hd[:, :].to_broadcast((P, H))), "cc")

                    load_wt(wv_d)
                    mm_v = []
                    ve_v = []
                    v_s0 = k.cnt["sk"]
                    for tt in range(NTT):
                        @block.tensor
                        def _(tensor, tt=tt):
                            if tt == 0:
                                k.wait(tensor, "ev2")
                            if tt >= 2:
                                k.wait_at(tensor, "sk", v_s0 + 16 * (tt - 1))
                            if tt >= 1:
                                k.wait_at(tensor, "ve", ve_v[tt - 1])
                            for nh in range(2):
                                for ic in range(NCH):
                                    k.inc(tensor.matmul(
                                        PC1[:, (tt % 2) * 2 + nh, :],
                                        XFC[:, ic, tt * P:(tt + 1) * P],
                                        WT[:, ic, nh * 512:(nh + 1) * 512],
                                        start=(ic == 0),
                                        stop=(ic == NCH - 1)), "mm")
                            mm_v.append(k.cnt["mm"])

                        @block.vector
                        def _(vector, tt=tt):
                            k.wait_at(vector, "mm", mm_v[tt])
                            for nh in range(2):
                                dst = AP(VS.ap().tensor,
                                         (tt % 2) * H * (DH + 1)
                                         + nh * 8 * (DH + 1),
                                         [[2 * H * (DH + 1), P],
                                          [DH + 1, 8], [1, DH]])
                                k.inc(vector.tensor_copy(
                                    dst, PC1[:, (tt % 2) * 2 + nh, :]), "ve")
                            ve_v.append(k.cnt["ve"])

                        @block.sync
                        def _(sync, tt=tt):
                            k.wait_at(sync, "ve", ve_v[tt])
                            dst = AP(vt_hd, tt * P * H * (DH + 1),
                                     [[H * (DH + 1), P], [1, H * (DH + 1)]])
                            k.dinc(sync.dma_start(
                                out=dst, in_=VS[:, tt % 2, :]), "sk")

                if _C_STAGES < 3:
                    return
                ssb = lambda n, s, d=F32: PC.enter_context(
                    nc.sbuf_tensor(n, s, d))
                QFL = ssb("cQFL", [DH + 1, 2, QFW], BF16)
                KFL = ssb("cKFL", [DH + 1, 2, T], BF16)
                RPJL = ssb("cRPJL", [DH + 1, 2, RPN], BF16)
                VTH = ssb("cVTH", [P, 2, NTT, DH + 1], BF16)
                SKB = ssb("cSKB", [P, 2, SKW], BF16)
                BAND = ssb("cBAND", [P, 2, 2, T], BF16)
                PQ = ssb("cPQ", [P, 2, 2, T], BF16)
                PQT = ssb("cPQT", [P, T], BF16)
                PT = ssb("cPT", [P, NTT, 256], BF16)
                OSB = ssb("cOSB", [DH + 1, 256])
                RCL = ssb("cRCL", [P, 2])

                PSC = PC.enter_context(
                    nc.psum_tensor("cPSC", [P, 4, 512], F32))
                PSPT = PC.enter_context(
                    nc.psum_tensor("cPSPT", [P, 2048], BF16))
                PPV = PC.enter_context(
                    nc.psum_tensor("cPPV", [DH + 1, 256], F32))
                PSE = PC.enter_context(
                    nc.psum_tensor("cPSE", [P, 512], F32))

                hl0 = k.cnt["ld"]
                hd_mm = []
                for h in range(_C_HEADS):
                    hpar = h % 2
                    exp_se = []
                    exp_sub = []
                    exp_sub_all = []
                    ac_mm2 = []
                    pmul_ve = []
                    ptr_mm = []

                    @block.gpsimd
                    def _(sync, h=h, hpar=hpar):
                        if h == 0:
                            k.wait(sync, "sk")
                        if h >= 2:
                            k.wait_at(sync, "mm", hd_mm[h - 2])
                        k.dinc(sync.dma_start(
                            out=QFL[:, hpar, :],
                            in_=qf_h[h * (DH + 1):(h + 1) * (DH + 1), :]),
                            "hl")
                        k.dinc(sync.dma_start(
                            out=KFL[:, hpar, :],
                            in_=kf_h[h * (DH + 1):(h + 1) * (DH + 1), :]),
                            "hl")
                        k.dinc(sync.dma_start(
                            out=RPJL[:, hpar, :],
                            in_=rp_hd[h * (DH + 1):(h + 1) * (DH + 1), :]),
                            "hl")
                        src = AP(vt_hd, h * (DH + 1),
                                 [[H * (DH + 1), P],
                                  [P * H * (DH + 1), NTT], [1, DH + 1]])
                        k.dinc(sync.dma_start(out=VTH[:, hpar, :, :],
                                              in_=src), "hl")

                    hload = k.cnt["hl"]

                    bd_sk0 = k.cnt["sk"]
                    bd_tr0 = k.cnt["tr"]
                    for qt in range(NTT):
                        i0 = qt * P

                        @block.tensor
                        def _(tensor, h=h, hpar=hpar, qt=qt, i0=i0,
                              hload=hload):
                            if qt == 0:
                                k.wait_at(tensor, "hl", hload)
                                k.wait(tensor, "ve")
                                k.wait(tensor, "se")
                            if qt >= 1:
                                k.wait_at(tensor, "tr", bd_tr0 + qt)
                            pend = 128 + i0
                            for uc in range(5):
                                u0 = uc * 512
                                u1 = min(u0 + 512, SKW)
                                segs = []
                                if u0 <= pend:
                                    segs.append((u0, min(u1 - 1, pend),
                                                 3967 - i0, 0))
                                if u1 - 1 >= pend + 1:
                                    segs.append((max(u0, pend + 1), u1 - 1,
                                                 -129 - i0, 1))
                                for (a, b2, roff, fut) in segs:
                                    q0 = i0 + 1 if fut else i0
                                    dst = (PSC[:, uc, a - u0:b2 + 1 - u0]
                                           if uc < 4 else
                                           PSE[:, a - u0:b2 + 1 - u0])
                                    k.inc(tensor.matmul(
                                        dst,
                                        QFL[:, hpar, q0:q0 + P],
                                        RPJL[:, hpar, a + roff:b2 + 1 + roff],
                                        start=True, stop=True), "mm")

                        mm_bd = k.cnt["mm"]

                        @block.scalar
                        def _(scalar, h=h, qt=qt, mm_bd=mm_bd):
                            k.wait_at(scalar, "mm", mm_bd)
                            if qt >= 2:
                                k.wait_at(scalar, "sk",
                                          bd_sk0 + 16 * (qt - 1))
                            par = qt % 2
                            flat = PSC[:, :, :].rearrange("p a b -> p (a b)")
                            scalar.activation(SKB[:, par, 0:1024],
                                              flat[:, 0:1024], AF.Exp,
                                              scale=0.125)
                            scalar.activation(SKB[:, par, 1024:2048],
                                              flat[:, 1024:2048], AF.Exp,
                                              scale=0.125)
                            k.inc(scalar.activation(SKB[:, par, 2048:SKW],
                                                    PSE[:, 0:128], AF.Exp,
                                                    scale=0.125), "tr")

                        @block.sync
                        def _(sync, h=h, qt=qt):
                            k.wait_at(sync, "tr", bd_tr0 + qt + 1)
                            r0 = (h * NTT + qt) * P
                            k.dinc(sync.dma_start(
                                out=skw_h[r0:r0 + P, :],
                                in_=SKB[:, qt % 2, :]), "sk")

                    bd_done = k.cnt["sk"]

                    ac_mm = []
                    pv_ve = []
                    os_ve_l = []
                    nqp = NTT // 2 if _C_SC >= 2 else 0

                    def emit_front(qp, h=h, hpar=hpar):
                        @block.gpsimd
                        def _(sync, h=h, qp=qp):
                            k.wait_at(sync, "sk", bd_done)
                            if qp >= 2:
                                k.wait_at(sync, "ve",
                                          pmul_ve[2 * (qp - 2) + 1])
                            for qtl in range(2):
                                qt = 2 * qp + qtl
                                base = (h * NTT + qt) * P * SKW + 127
                                src = AP(skw_h, base, [[SKW - 1, P], [1, T]])
                                k.dinc(sync.dma_start(
                                    out=BAND[:, qp % 2, qtl, :], in_=src),
                                    "hl")

                        band_ld = k.cnt["hl"]

                        for qtl in range(2):
                            for jp in range(2):
                                gi = qp * 4 + qtl * 2 + jp

                                @block.tensor
                                def _(tensor, h=h, hpar=hpar, qp=qp, qtl=qtl,
                                      jp=jp, gi=gi):
                                    if gi >= 2:
                                        k.wait_at(tensor, "se",
                                                  exp_sub_all[gi - 2])
                                    qt = 2 * qp + qtl
                                    for jc in (jp * 2, jp * 2 + 1):
                                        js = slice(jc * 512, (jc + 1) * 512)
                                        k.inc(tensor.matmul(
                                            PSC[:, jc, :],
                                            QFL[:, hpar,
                                                qt * P:(qt + 1) * P],
                                            KFL[:, hpar, js],
                                            start=True, stop=True), "mm")
                                    ac_mm2.append(k.cnt["mm"])

                                @block.scalar
                                def _(scalar, h=h, qp=qp, qtl=qtl, jp=jp):
                                    k.wait_at(scalar, "mm", ac_mm2[-1])
                                    if jp == 0 and pmul_ve:
                                        k.wait_at(scalar, "ve", pmul_ve[-1])
                                    flat = PSC[:, :, :].rearrange(
                                        "p a b -> p (a b)")
                                    js = slice(jp * 1024, (jp + 1) * 1024)
                                    k.inc(scalar.activation(
                                        PQT[:, js], flat[:, js],
                                        AF.Exp, scale=0.125,
                                        bias=NMHB[:, h:h + 1]), "se")
                                    exp_sub_all.append(k.cnt["se"])

                            exp_sub.append(k.cnt["se"])
                            ac_mm.append(k.cnt["mm"])

                            @block.vector
                            def _(vector, qp=qp, qtl=qtl, band_ld=band_ld):
                                k.wait_at(vector, "hl", band_ld)
                                k.wait_at(vector, "se", exp_sub[-1])
                                k.inc(vector.tensor_tensor(
                                    PQ[:, qp % 2, qtl, :], PQT[:, :],
                                    BAND[:, qp % 2, qtl, :], ALU.mult), "ve")
                                pmul_ve.append(k.cnt["ve"])

                        exp_se.append(k.cnt["se"])

                    def emit_tail(qp, h=h, hpar=hpar):
                        ptr0 = k.cnt["mm"]
                        ev20 = k.cnt["ev2"]

                        @block.tensor
                        def _(tensor, qp=qp, ptr0=ptr0, ev20=ev20):
                            k.wait_at(tensor, "ve", pmul_ve[2 * qp + 1])
                            k.wait(tensor, "ev2")
                            n = 0
                            for jt in range(NTT):
                                for qtl in range(2):
                                    if n >= 16:
                                        k.wait_at(tensor, "ev2",
                                                  ev20 + (n - 16) // 8 + 1)
                                    slot = n % 16
                                    k.inc(tensor.transpose(
                                        PSPT[:, slot * P:(slot + 1) * P],
                                        PQ[:, qp % 2, qtl,
                                           jt * P:(jt + 1) * P],
                                        IDB[:, :]), "mm")
                                    n += 1

                        @block.vector
                        def _(vector, qp=qp, ptr0=ptr0):
                            for g in range(4):
                                k.wait_at(vector, "mm", ptr0 + (g + 1) * 8)
                                vector.tensor_copy(
                                    PT[:, 4 * g:4 * g + 4, :],
                                    PSPT[:, (g % 2) * 1024:
                                         (g % 2) * 1024 + 1024])
                                k.inc(vector.engine_nop(), "ev2")

                        ptr_mm.append(k.cnt["mm"])

                        @block.tensor
                        def _(tensor, h=h, hpar=hpar, qp=qp, ev20=ev20):
                            if qp >= 1:
                                k.wait_at(tensor, "ve", os_ve_l[qp - 1])
                            for jt in range(NTT):
                                k.wait_at(tensor, "ev2",
                                          ev20 + min(jt // 4 + 1, 4))
                                k.inc(tensor.matmul(
                                    PPV[:, :], VTH[:, hpar, jt, :],
                                    PT[:, jt, :],
                                    start=(jt == 0), stop=(jt == NTT - 1)),
                                    "mm")

                        pv_mm = k.cnt["mm"]

                        @block.vector
                        def _(vector, qp=qp, pv_mm=pv_mm):
                            k.wait_at(vector, "mm", pv_mm)
                            k.inc(vector.tensor_copy(OSB[:, :], PPV[:, :]),
                                  "ve")

                        os_ve = k.cnt["ve"]
                        os_ve_l.append(os_ve)

                        @block.tensor
                        def _(tensor, qp=qp, os_ve=os_ve):
                            k.wait_at(tensor, "ve", os_ve)
                            for qtl in range(2):
                                k.inc(tensor.transpose(
                                    PSE[:, qtl * P:qtl * P + DH + 1],
                                    OSB[:, qtl * P:(qtl + 1) * P],
                                    IDF[0:DH + 1, 0:DH + 1]), "mm")

                        ep_mm = k.cnt["mm"]

                        @block.vector
                        def _(vector, h=h, qp=qp, ep_mm=ep_mm):
                            k.wait_at(vector, "mm", ep_mm)
                            for qtl in range(2):
                                vector.reciprocal(
                                    RCL[:, qtl:qtl + 1],
                                    PSE[:, qtl * P + DH:qtl * P + DH + 1])
                            for qtl in range(2):
                                qt = 2 * qp + qtl
                                k.inc(vector.scalar_tensor_tensor(
                                    X[:, qt, h * DH:(h + 1) * DH],
                                    PSE[:, qtl * P:qtl * P + DH],
                                    RCL[:, qtl:qtl + 1],
                                    X[:, qt, h * DH:(h + 1) * DH],
                                    ALU.mult, ALU.add), "ve")
                            pv_ve.append(k.cnt["ve"])

                    for qp in range(nqp):
                        emit_front(qp)
                        if _C_SC >= 3 and qp >= 1:
                            emit_tail(qp - 1)
                        if _C_SC < 3:
                            ptr_mm.append(k.cnt["mm"])
                            pv_ve.append(k.cnt["ve"])
                    if _C_SC >= 3 and nqp:
                        emit_tail(nqp - 1)

                    hd_mm.append(k.cnt["mm"])

        if phases >= 2:
            emit_moe(0)
        if phases >= 3:
            emit_attn_b()
        if phases >= 4:
            emit_moe(1)

        @block.sync
        def _(sync):
            k.wait(sync, "ve")
            k.wait(sync, "se")
            k.wait(sync, "mm")
            dst = AP(out_d, 0, [[D, P], [P * D, NTT], [1, D]])
            k.dinc(sync.dma_start(out=dst, in_=X[:, :, :]), "st")
            k.wait(sync, "st")

    return nc


def _shard_inputs(inputs):
    consts = _host_consts()
    lam = np.ascontiguousarray(
        np.asarray(inputs["L0_attn_a_lambda_log"]).reshape(NCH, P).T)
    wnc = np.ascontiguousarray(
        np.asarray(inputs["L0_attn_a_w_norm"]).reshape(NCH, P).T)
    ubt = np.ascontiguousarray(np.asarray(inputs["L0_attn_b_u_bias"]).T)
    vbt = np.ascontiguousarray(np.asarray(inputs["L0_attn_b_v_bias"]).T)
    common = {
        "w_in": np.asarray(inputs["L0_attn_a_W_in"]),
        "lam_log": lam.astype(np.float32),
        "w_norm_c": wnc.astype(np.float32),
        "w_out": np.asarray(inputs["L0_attn_a_W_out"]),
        "wq": np.asarray(inputs["L0_attn_b_Wq"]),
        "wk": np.asarray(inputs["L0_attn_b_Wk"]),
        "wv": np.asarray(inputs["L0_attn_b_Wv"]),
        "wr": np.asarray(inputs["L0_attn_b_Wr"]),
        "u_bias_t": ubt.astype(np.float32),
        "v_bias_t": vbt.astype(np.float32),
        "gate_w": np.asarray(inputs["L0_ffn_w_gate"]),
        "c_fc": np.asarray(inputs["L0_ffn_c_fc"]),
        "c_proj": np.asarray(inputs["L0_ffn_c_proj"]),
    }
    common.update(consts)
    xfull = np.ascontiguousarray(np.asarray(inputs["L0_attn_a_x"]))
    per_core = []
    for b in range(8):
        m = dict(common)
        m["x_in"] = np.ascontiguousarray(xfull[b])
        pm = np.zeros((NEXP, 1), np.float32)
        pm[:b, 0] = 1.0
        m["prev_mask"] = pm
        per_core.append(m)
    return per_core


def kernel(**inputs):
    in_maps = _shard_inputs(inputs)
    nc = build_program()
    mybir.codegen_inst_isa_subclasses(nc)
    res = run_bass_kernel_spmd(nc, in_maps, core_ids=list(range(8)))
    return np.stack([r["out"] for r in res.results], axis=0)


# version 14
# speedup vs baseline: 1.3628x; 1.0555x over previous
import contextlib
import numpy as np
import ml_dtypes
import concourse.bass as bass
import concourse.mybir as mybir
from concourse.ap import AP
from concourse.bass_core_rust import IndirectOffsetOnAxis
from concourse.bass_utils import run_bass_kernel_spmd

F32 = mybir.dt.float32
BF16 = mybir.dt.bfloat16
I32 = mybir.dt.int32
AF = mybir.ActivationFunctionType
ALU = mybir.AluOpType
AX = mybir.AxisListType.X

P = 128
D = 1024
T = 2048
NTT = T // P
NCH = D // P
H = 16
DH = 64
DFF = 3072
NEXP = 8
ECAP = 640
NSLOT = NEXP * ECAP
CAPACITY = 4096
SKW = 2176
RPN = 4096
QFW = 2064

_PHASES = 4
_TAPS = ()
_C_STAGES = 3
_C_HEADS = H
_C_SC = 3


def _host_consts():
    ident = np.eye(P, dtype=np.float64)
    lstrict = (np.arange(P)[:, None] < np.arange(P)[None, :]).astype(np.float64)
    tokids = (np.arange(NTT)[None, :] * P + np.arange(P)[:, None]).astype(np.int32)
    inv_freq = 1.0 / (10000.0 ** (np.arange(0, D, 2, dtype=np.float64) / D))
    pos = np.arange(-(T - 1), T, dtype=np.float64)
    ang = pos[:, None] * inv_freq[None, :]
    emb = np.zeros((2 * T - 1, D), np.float64)
    emb[:, 0::2] = np.sin(ang)
    emb[:, 1::2] = np.cos(ang)
    remb_t = np.zeros((D, RPN), np.float32)
    remb_t[:, : 2 * T - 1] = emb.T.astype(np.float32)
    return {
        "c_identb": ident.astype(ml_dtypes.bfloat16),
        "c_identf": ident.astype(np.float32),
        "c_lstrict": lstrict.astype(np.float32),
        "c_ones128": np.ones((P, P), np.float32),
        "c_tokids": tokids,
        "c_remb_t": remb_t,
    }


class K:

    def __init__(self, nc, stack):
        self.nc = nc
        self.stack = stack
        self.sem = {}
        self.cnt = {}

    def mksem(self, *names):
        for n in names:
            self.sem[n] = self.stack.enter_context(self.nc.semaphore(n))
            self.cnt[n] = 0

    def inc(self, instr, name, n=1):
        instr.then_inc(self.sem[name], n)
        self.cnt[name] += n
        return instr

    def dinc(self, instr, name):
        return self.inc(instr, name, 16)

    def wait(self, eng, name):
        if self.cnt[name] > 0:
            eng.wait_ge(self.sem[name], self.cnt[name])

    def wait_at(self, eng, name, val):
        if val > 0:
            eng.wait_ge(self.sem[name], val)


def build_program(phases=None):
    if phases is None:
        phases = _PHASES
    nc = bass.Bass("TRN2", target_bir_lowering=False,
                   detect_race_conditions=False)

    def din(name, shape, dtype=F32):
        return nc.dram_tensor(name, shape, dtype, kind="ExternalInput")

    def dint(name, shape, dtype=F32):
        return nc.dram_tensor(name, shape, dtype, kind="Internal")

    x_in = din("x_in", [T, D])
    w_in = din("w_in", [D, DFF])
    lam_log = din("lam_log", [P, NCH])
    w_norm_c = din("w_norm_c", [P, NCH])
    w_out = din("w_out", [D, D])
    wq_d = din("wq", [D, D])
    wk_d = din("wk", [D, D])
    wv_d = din("wv", [D, D])
    wr_d = din("wr", [D, D])
    u_bias_t = din("u_bias_t", [DH, H])
    v_bias_t = din("v_bias_t", [DH, H])
    gate_w = din("gate_w", [D, NEXP])
    c_fc = din("c_fc", [NEXP, D, DFF])
    c_proj = din("c_proj", [NEXP, DFF // 2, D])
    prev_mask = din("prev_mask", [NEXP, 1])
    c_identb = din("c_identb", [P, P], BF16)
    c_identf = din("c_identf", [P, P])
    c_lstrict = din("c_lstrict", [P, P])
    c_ones128 = din("c_ones128", [P, P])
    c_tokids = din("c_tokids", [P, NTT], I32)
    c_remb_t = din("c_remb_t", [D, RPN])

    out_d = nc.dram_tensor("out", [T, D], F32, kind="ExternalOutput")

    xtok_h = dint("xtok_h", [T, D], BF16)
    inv_h = dint("inv_h", [NSLOT, 1], I32)
    slotout_h = dint("slotout_h", [NSLOT, D], F32)
    skw_h = dint("skw_h", [H * NTT * P, SKW], BF16)
    rp_hd = dint("rp_h", [H * (DH + 1), RPN], BF16)
    vt_hd = dint("vt_h", [T, H * (DH + 1)], BF16)
    cc_in = dint("cc_in", [1, NEXP])
    cc_out = dint("cc_out", [NEXP, NEXP])
    base_h = dint("base_h", [1, NEXP])
    mh_hd = dint("mh_h", [1, H])

    with contextlib.ExitStack() as ST:
        k = K(nc, ST)
        k.mksem("ld", "wld", "wld2", "mm", "ve", "se", "st", "pk", "sk",
                "cc", "ev2", "xk", "so", "tr", "pz", "cc2", "hl")
        block = ST.enter_context(nc.Block())

        sb = lambda n, s, d=F32: ST.enter_context(nc.sbuf_tensor(n, s, d))

        X = sb("X", [P, NTT, D])
        IDB = sb("IDB", [P, P], BF16)
        IDF = sb("IDF", [P, P])
        LST = sb("LST", [P, P])
        ON128 = sb("ON128", [P, P])
        TOK = sb("TOK", [P, NTT], I32)
        LAM = sb("LAM", [P, NCH])
        LAMH = sb("LAMH", [P, NCH])
        LAMLOG = sb("LAMLOG", [P, NCH])
        WNC = sb("WNC", [P, NCH])
        UBT = sb("UBT", [DH, H])
        VBT = sb("VBT", [DH, H])
        PMASK = sb("PMASK", [NEXP, 1])
        GW = sb("GW", [P, NCH, NEXP], BF16)
        LNS = sb("LNS", [P, NTT, 12])
        EPS5 = sb("EPS5", [P, 1])
        EPS11 = sb("EPS11", [P, 1])
        SQ = sb("SQ", [P, 2, D])

        @block.sync
        def _(sync):
            src = AP(x_in, 0, [[D, P], [P * D, NTT], [1, D]])
            k.dinc(sync.dma_start(out=X[:, :, :], in_=src), "ld")
            k.dinc(sync.dma_start(out=IDB[:, :], in_=c_identb[:, :]), "ld")
            k.dinc(sync.dma_start(out=IDF[:, :], in_=c_identf[:, :]), "ld")
            k.dinc(sync.dma_start(out=LST[:, :], in_=c_lstrict[:, :]), "ld")
            k.dinc(sync.dma_start(out=ON128[:, :], in_=c_ones128[:, :]), "ld")
            k.dinc(sync.dma_start(out=TOK[:, :], in_=c_tokids[:, :]), "ld")
            k.dinc(sync.dma_start(out=WNC[:, :], in_=w_norm_c[:, :]), "ld")
            k.dinc(sync.dma_start(out=UBT[:, :], in_=u_bias_t[:, :]), "ld")
            k.dinc(sync.dma_start(out=VBT[:, :], in_=v_bias_t[:, :]), "ld")
            k.dinc(sync.dma_start(out=PMASK[:, :], in_=prev_mask[:, :]), "ld")
            k.dinc(sync.dma_start(out=LAMLOG[:, :], in_=lam_log[:, :]), "ld")

        @block.gpsimd
        def _(gpsimd):
            src = AP(gate_w, 0, [[NEXP, P], [P * NEXP, NCH], [1, NEXP]])
            k.dinc(gpsimd.dma_start(out=GW[:, :, :], in_=src), "wld")

        @block.vector
        def _(vector):
            vector.memset(EPS5[:, :], 1e-5)
            k.inc(vector.memset(EPS11[:, :], 1e-11), "ve")

        @block.scalar
        def _(scalar):
            k.wait(scalar, "ld")
            k.inc(scalar.activation(LAM[:, :], LAMLOG[:, :], AF.Sigmoid), "se")

        @block.vector
        def _(vector):
            k.wait(vector, "se")
            k.inc(vector.tensor_scalar(LAMH[:, :], LAM[:, :], 0.5, None,
                                       ALU.mult), "ve")

        def emit_ln(XLT, rms):
            se0 = k.cnt["se"]
            ve0 = k.cnt["ve"]

            @block.scalar
            def _(scalar):
                k.wait(scalar, "ld")
                k.wait(scalar, "ve")
                k.wait(scalar, "mm")
                for tt in range(NTT):
                    if tt >= 2:
                        k.wait_at(scalar, "ve", ve0 + tt - 1)
                    k.inc(scalar.activation(SQ[:, tt % 2, :], X[:, tt, :],
                                            AF.Square), "se")

            @block.vector
            def _(vector):
                k.wait(vector, "ld")
                for tt in range(NTT):
                    k.wait_at(vector, "se", se0 + tt + 1)
                    vector.reduce_sum(LNS[:, tt, 5:6], X[:, tt, :], AX)
                    k.inc(vector.reduce_sum(LNS[:, tt, 6:7], SQ[:, tt % 2, :],
                                            AX), "ve")
                for tt in range(NTT):
                    vector.tensor_scalar(LNS[:, tt, 0:1], LNS[:, tt, 5:6],
                                         -1.0 / D, None, ALU.mult)
                for tt in range(NTT):
                    vector.tensor_scalar(LNS[:, tt, 7:8], LNS[:, tt, 5:6],
                                         1.0 / D, None, ALU.mult)
                for tt in range(NTT):
                    vector.tensor_tensor(LNS[:, tt, 8:9], LNS[:, tt, 7:8],
                                         LNS[:, tt, 7:8], ALU.mult)
                for tt in range(NTT):
                    vector.tensor_scalar(LNS[:, tt, 9:10], LNS[:, tt, 6:7],
                                         1.0 / D, None, ALU.mult)
                for tt in range(NTT):
                    k.inc(vector.tensor_tensor(LNS[:, tt, 10:11],
                                               LNS[:, tt, 9:10],
                                               LNS[:, tt, 8:9], ALU.subtract),
                          "ve")

            @block.scalar
            def _(scalar):
                k.wait(scalar, "ve")
                for tt in range(NTT):
                    k.inc(scalar.activation(LNS[:, tt, 11:12], LNS[:, tt, 10:11],
                                            AF.Sqrt, bias=EPS5[:, :]), "se")
                if rms:
                    for tt in range(NTT):
                        k.inc(scalar.activation(LNS[:, tt, 5:6],
                                                LNS[:, tt, 10:11], AF.Sqrt,
                                                bias=EPS11[:, :],
                                                scale=1.000001), "se")

            @block.vector
            def _(vector):
                k.wait(vector, "se")
                for tt in range(NTT):
                    k.inc(vector.reciprocal(LNS[:, tt, 1:2], LNS[:, tt, 11:12]),
                          "ve")
                if rms:
                    for tt in range(NTT):
                        k.inc(vector.reciprocal(LNS[:, tt, 2:3], LNS[:, tt, 5:6]),
                              "ve")
                for tt in range(NTT):
                    vector.tensor_scalar(LNS[:, tt, 3:4], LNS[:, tt, 1:2], 1.0,
                                         None, ALU.add)
                for tt in range(NTT):
                    k.inc(vector.tensor_tensor(LNS[:, tt, 4:5], LNS[:, tt, 0:1],
                                               LNS[:, tt, 1:2], ALU.mult), "ve")
                sc = 2 if rms else 1
                for tt in range(NTT):
                    k.inc(vector.tensor_scalar(XLT[:, tt, :], X[:, tt, :],
                                               LNS[:, tt, 0:1],
                                               LNS[:, tt, sc:sc + 1],
                                               ALU.add, ALU.mult), "ve")

        def tap(name, buf):
            if name not in _TAPS:
                return
            dt_ = nc.dram_tensor("dbg_" + name,
                                 [buf.shape[0], int(np.prod(buf.shape[1:]))],
                                 buf.ap().dtype, kind="ExternalOutput")

            @block.sync
            def _(sync):
                k.wait(sync, "ve")
                k.wait(sync, "se")
                k.wait(sync, "mm")
                src = buf.ap().rearrange("p ... -> p (...)")
                k.dinc(sync.dma_start(out=dt_[:, :], in_=src), "st")

        def emit_transposes(XLT, XF, PSA, scale_wnc):
            mm0 = k.cnt["mm"]
            ev0 = k.cnt["ev2"]
            ntile = NTT * NCH

            @block.tensor
            def _(tensor):
                k.wait(tensor, "ve")
                for n in range(ntile):
                    tt, c = divmod(n, NCH)
                    if n >= 16:
                        k.wait_at(tensor, "ev2", ev0 + (n - 16) // 8 + 1)
                    slot = n % 16
                    k.inc(tensor.transpose(
                        PSA[:, slot * P:(slot + 1) * P],
                        XLT[:, tt, c * P:(c + 1) * P], IDB[:, :]), "mm")

            @block.vector
            def _(vector):
                for g in range(ntile // 8):
                    k.wait_at(vector, "mm", mm0 + (g + 1) * 8)
                    for j in range(8):
                        n = g * 8 + j
                        tt, c = divmod(n, NCH)
                        slot = n % 16
                        src = PSA[:, slot * P:(slot + 1) * P]
                        dst = XF[:, c, tt * P:(tt + 1) * P]
                        if scale_wnc:
                            vector.tensor_scalar(dst, src, WNC[:, c:c + 1],
                                                 None, ALU.mult)
                        else:
                            vector.tensor_copy(dst, src)
                    k.inc(vector.engine_nop(), "ev2")

        with contextlib.ExitStack() as PA:
            asb = lambda n, s, d=F32: PA.enter_context(nc.sbuf_tensor(n, s, d))

            XF = asb("aXF", [P, NCH, T], BF16)
            YB = asb("aYB", [P, NCH, T], BF16)

            with contextlib.ExitStack() as PT:
                XLT = PT.enter_context(nc.sbuf_tensor("aXLT", [P, NTT, D], BF16))
                PSA = PT.enter_context(nc.psum_tensor("aPSA", [P, 2048], BF16))
                emit_ln(XLT, rms=True)
                tap("xlt", XLT)
                emit_transposes(XLT, XF, PSA, scale_wnc=True)
                tap("xf", XF)
            mm_xlt_done = k.cnt["mm"]

            WIT = asb("aWIT", [P, 2, NCH, 3, P], BF16)
            WOB = asb("aWOB", [P, NCH, D], BF16)
            AG = asb("aAG", [P, T])
            UG = asb("aUG", [P, T])
            TA2 = asb("aTA2", [P, 2, 512])
            TR = asb("aTR", [P, 2, 512])
            TV = asb("aTV", [P, 2, 512])
            TZV = asb("aTZV", [P, 2, 512])
            T1 = asb("aT1", [P, 2, 512])

            with contextlib.ExitStack() as PP:
                PSP = PP.enter_context(nc.psum_tensor("aPSP", [P, 6, 512], F32))

                @block.gpsimd
                def _(gpsimd):
                    k.wait_at(gpsimd, "mm", mm_xlt_done)
                    if _TAPS:
                        k.wait(gpsimd, "st")
                    src = AP(w_out, 0, [[D, P], [P * D, NCH], [1, D]])
                    k.dinc(gpsimd.dma_start(out=WOB[:, :, :], in_=src), "wld")

                mm_proj = []
                SE_BASE = k.cnt["se"]
                VE_BASE = k.cnt["ve"]
                se_pred = lambda idx: SE_BASE + idx + 1
                ve_pred = lambda idx: VE_BASE + (idx // 4) * 5 + (idx % 4) + 1
                scan_pred = lambda g: VE_BASE + (g + 1) * 5

                for g in range(NCH):
                    @block.gpsimd
                    def _(gpsimd, g=g):
                        if g >= 2:
                            k.wait_at(gpsimd, "mm", mm_proj[(g - 2) * 4 + 3])
                        for kc in range(NCH):
                            src = AP(w_in, kc * P * DFF + g * P,
                                     [[DFF, P], [D, 3], [1, P]])
                            k.dinc(gpsimd.dma_start(
                                out=WIT[:, g % 2, kc, :, :], in_=src), "wld2")

                    @block.tensor
                    def _(tensor, g=g):
                        k.wait_at(tensor, "wld2", 16 * NCH * (g + 1))
                        if g == 0:
                            k.wait(tensor, "ev2")
                        for f in range(4):
                            par = f % 2
                            if g * 4 + f >= 2:
                                k.wait_at(tensor, "se",
                                          se_pred(g * 4 + f - 2))
                            for kc in range(NCH):
                                for s in range(3):
                                    k.inc(tensor.matmul(
                                        PSP[:, par * 3 + s, :],
                                        WIT[:, g % 2, kc, s, :],
                                        XF[:, kc, f * 512:(f + 1) * 512],
                                        start=(kc == 0), stop=(kc == NCH - 1)),
                                        "mm")
                            mm_proj.append(k.cnt["mm"])

                    for f in range(4):
                        idx = g * 4 + f

                        @block.scalar
                        def _(scalar, g=g, f=f, idx=idx):
                            par = f % 2
                            k.wait_at(scalar, "mm", mm_proj[idx])
                            if f == 0 and g >= 1:
                                k.wait_at(scalar, "ve", scan_pred(g - 1))
                            if idx >= 2:
                                k.wait_at(scalar, "ve", ve_pred(idx - 2))
                            fs = slice(f * 512, (f + 1) * 512)
                            scalar.activation(AG[:, fs],
                                              PSP[:, par * 3 + 0, :],
                                              AF.Sigmoid)
                            scalar.activation(TA2[:, par, :],
                                              PSP[:, par * 3 + 0, :],
                                              AF.Sigmoid, scale=-1.0)
                            scalar.activation(TR[:, par, :],
                                              PSP[:, par * 3 + 1, :],
                                              AF.Sigmoid)
                            k.inc(scalar.activation(TV[:, par, :],
                                                    PSP[:, par * 3 + 2, :],
                                                    AF.Gelu), "se")
                            assert k.cnt["se"] == se_pred(idx)

                        @block.vector
                        def _(vector, g=g, f=f, idx=idx):
                            par = f % 2
                            k.wait_at(vector, "se", se_pred(idx))
                            for h2 in range(2):
                                hs = slice(f * 512 + h2 * 256,
                                           f * 512 + h2 * 256 + 256)
                                ph = slice(h2 * 256, h2 * 256 + 256)
                                vector.tensor_tensor(UG[:, hs], TR[:, par, ph],
                                                     TV[:, par, ph], ALU.mult)
                            for h2 in range(2):
                                hs = slice(f * 512 + h2 * 256,
                                           f * 512 + h2 * 256 + 256)
                                vector.tensor_scalar(UG[:, hs], UG[:, hs],
                                                     LAM[:, g:g + 1], None,
                                                     ALU.mult)
                            for h2 in range(2):
                                hs = slice(f * 512 + h2 * 256,
                                           f * 512 + h2 * 256 + 256)
                                ph = slice(h2 * 256, h2 * 256 + 256)
                                ins = vector.tensor_tensor(
                                    UG[:, hs], UG[:, hs], TA2[:, par, ph],
                                    ALU.mult)
                            k.inc(ins, "ve")
                            assert k.cnt["ve"] == ve_pred(idx)

                    @block.vector
                    def _(vector, g=g):
                        k.wait_at(vector, "ve", ve_pred(g * 4 + 3))
                        k.wait_at(vector, "se", se_pred(g * 4 + 3))
                        k.inc(vector.tensor_tensor_scan(
                            YB[:, g, :], AG[:, :], UG[:, :], 0.0,
                            ALU.mult, ALU.add), "ve")
                        assert k.cnt["ve"] == scan_pred(g)

            tap("ag", AG)
            tap("ug", UG)
            tap("yb", YB)

            with contextlib.ExitStack() as PW:
                PWO = PW.enter_context(nc.psum_tensor("aPWO", [P, 2, D], F32))
                mm_wo = []
                ve_wo = []

                for tt in range(NTT):
                    @block.tensor
                    def _(tensor, tt=tt):
                        if tt == 0:
                            k.wait(tensor, "ve")
                            k.wait(tensor, "se")
                            k.wait_at(tensor, "wld", 32)
                        if tt >= 2:
                            k.wait_at(tensor, "ve", ve_wo[tt - 2])
                        par = tt % 2
                        for g in range(NCH):
                            for nh in range(2):
                                k.inc(tensor.matmul(
                                    PWO[:, par, nh * 512:(nh + 1) * 512],
                                    YB[:, g, tt * P:(tt + 1) * P],
                                    WOB[:, g, nh * 512:(nh + 1) * 512],
                                    start=(g == 0), stop=(g == NCH - 1)), "mm")
                        mm_wo.append(k.cnt["mm"])

                    @block.vector
                    def _(vector, tt=tt):
                        k.wait_at(vector, "mm", mm_wo[tt])
                        par = tt % 2
                        for nh in range(2):
                            vector.scalar_tensor_tensor(
                                T1[:, nh, :], X[:, tt, nh * 512:(nh + 1) * 512],
                                LNS[:, tt, 3:4],
                                PWO[:, par, nh * 512:(nh + 1) * 512],
                                ALU.mult, ALU.add)
                        for nh in range(2):
                            hs = slice(nh * 512, (nh + 1) * 512)
                            k.inc(vector.tensor_scalar(X[:, tt, hs],
                                                       T1[:, nh, :],
                                                       LNS[:, tt, 4:5], None,
                                                       ALU.add), "ve")
                        ve_wo.append(k.cnt["ve"])

        def emit_moe(ci):
            ve_pfxread = []
            mm_pfx = []
            with contextlib.ExitStack() as PB:
                bsb = lambda n, s, d=F32: PB.enter_context(
                    nc.sbuf_tensor(n + str(ci), s, d))

                LOG = bsb("bLOG", [P, NTT, NEXP])
                MX = bsb("bMX", [P, NTT])
                NMX = bsb("bNMX", [P, NTT])
                SM = bsb("bSM", [P, NTT])
                RSM = bsb("bRSM", [P, NTT])
                EXL = bsb("bEXL", [P, NTT, NEXP])
                PRB = bsb("bPRB", [P, NTT, NEXP])
                M8 = bsb("bM8", [P, NTT, NEXP])
                IXU = bsb("bIXU", [P, NTT, NEXP], mybir.dt.uint32)
                W12 = bsb("bW12", [P, 2 * NTT])
                E12 = bsb("bE12", [P, 2 * NTT])
                INDA = bsb("bINDA", [P, NEXP, 2 * NTT])
                T2A = bsb("bT2A", [P, NEXP, 2 * NTT])
                RT = bsb("bRT", [P, 16])
                PFX = bsb("bPFX", [P, 16])
                CSX = bsb("bCSX", [P, 16])
                TTMP = bsb("bTTMP", [P, 2 * NTT])
                TTM2 = bsb("bTTM2", [P, 2 * NTT])
                LP = bsb("bLP", [P, 2 * NTT])
                GP = bsb("bGP", [P, 2 * NTT])
                VLD = bsb("bVLD", [P, 2 * NTT])
                WF = bsb("bWF", [P, 2 * NTT])
                SLOT = bsb("bSLOT", [P, 2 * NTT])
                SLI = bsb("bSLI", [P, 2 * NTT], I32)
                SLH = bsb("bSLH", [P, 2 * NTT], I32)
                SH1 = bsb("bSH1", [P, 2 * NTT], I32)
                SH2 = bsb("bSH2", [P, 2 * NTT], I32)
                CNT8 = bsb("bCNT8", [1, NEXP])
                CCS = bsb("bCCS", [NEXP, NEXP])
                BASE = bsb("bBASE", [1, NEXP])
                BASEB = bsb("bBASEB", [P, NEXP])
                ONE16 = bsb("bONE16", [P, 16])
                ZI = bsb("bZI", [P, 40], I32)
                INVR = bsb("bINVR", [P, 40], I32)

                PSAB = PB.enter_context(
                    nc.psum_tensor("bPSA" + str(ci), [P, 2048], BF16))

                @block.vector
                def _(vector):
                    k.wait(vector, "mm")
                    vector.memset(ONE16[:, :], 1.0)
                    vector.memset(LP[:, :], 0.0)
                    vector.memset(GP[:, :], 0.0)
                    k.inc(vector.memset(ZI[:, :], 0), "ve")

                with contextlib.ExitStack() as PBG:
                    XFB = PBG.enter_context(
                        nc.sbuf_tensor("bXFB" + str(ci), [P, NCH, T], BF16))
                    PSG = PBG.enter_context(
                        nc.psum_tensor("bPSG" + str(ci), [NEXP, T], F32))
                    PSL = PBG.enter_context(
                        nc.psum_tensor("bPSL" + str(ci), [P, NTT, NEXP], F32))
                    GSB = PBG.enter_context(
                        nc.sbuf_tensor("bGSB" + str(ci), [NEXP, T], F32))

                    with contextlib.ExitStack() as PBL:
                        XLTB = PBL.enter_context(
                            nc.sbuf_tensor("bXLT" + str(ci), [P, NTT, D], BF16))
                        emit_ln(XLTB, rms=False)

                        @block.sync
                        def _(sync):
                            k.wait(sync, "ve")
                            dst = AP(xtok_h, 0, [[D, P], [P * D, NTT], [1, D]])
                            k.dinc(sync.dma_start(out=dst, in_=XLTB[:, :, :]),
                                   "xk")

                        emit_transposes(XLTB, XFB, PSAB, scale_wnc=False)
                    mm_xltb_done = k.cnt["mm"]

                    @block.tensor
                    def _(tensor):
                        k.wait(tensor, "ev2")
                        k.wait(tensor, "wld")
                        for tt in range(NTT):
                            for c in range(NCH):
                                k.inc(tensor.matmul(
                                    PSG[:, tt * P:(tt + 1) * P],
                                    GW[:, c, :],
                                    XFB[:, c, tt * P:(tt + 1) * P],
                                    start=(c == 0), stop=(c == NCH - 1)), "mm")

                    mm_gate = k.cnt["mm"]

                    @block.vector
                    def _(vector):
                        k.wait_at(vector, "mm", mm_gate)
                        k.inc(vector.tensor_copy(GSB[:, :], PSG[:, :]), "ve")

                    ve_gsb = k.cnt["ve"]

                    @block.tensor
                    def _(tensor):
                        k.wait_at(tensor, "ve", ve_gsb)
                        for tt in range(NTT):
                            k.inc(tensor.transpose(
                                PSL[:, tt, :], GSB[:, tt * P:(tt + 1) * P],
                                IDF[0:NEXP, 0:NEXP]), "mm")

                    mm_psl = k.cnt["mm"]

                    @block.vector
                    def _(vector):
                        k.wait_at(vector, "mm", mm_psl)
                        k.inc(vector.tensor_copy(LOG[:, :, :], PSL[:, :, :]),
                              "ve")
                        for tt in range(NTT):
                            k.inc(vector.reduce_max(MX[:, tt:tt + 1],
                                                    LOG[:, tt, :], AX), "ve")
                        for tt in range(NTT):
                            k.inc(vector.tensor_scalar(NMX[:, tt:tt + 1],
                                                       MX[:, tt:tt + 1], -1.0,
                                                       None, ALU.mult), "ve")

                    se_exp0 = k.cnt["se"]

                    @block.scalar
                    def _(scalar):
                        k.wait(scalar, "ve")
                        for tt in range(NTT):
                            k.inc(scalar.activation(
                                EXL[:, tt, :], LOG[:, tt, :], AF.Exp,
                                bias=NMX[:, tt:tt + 1],
                                accum_out=SM[:, tt:tt + 1]), "se")

                    @block.vector
                    def _(vector):
                        k.wait(vector, "se")
                        for tt in range(NTT):
                            k.inc(vector.reciprocal(RSM[:, tt:tt + 1],
                                                    SM[:, tt:tt + 1]), "ve")
                        for tt in range(NTT):
                            vector.tensor_scalar(PRB[:, tt, :], EXL[:, tt, :],
                                                 RSM[:, tt:tt + 1], None,
                                                 ALU.mult)
                        for tt in range(NTT):
                            k.inc(vector.max(M8[:, tt, :], PRB[:, tt, :]),
                                  "ve")
                        for tt in range(NTT):
                            k.inc(vector.max_index(IXU[:, tt, :], M8[:, tt, :],
                                                   PRB[:, tt, :]), "ve")
                        for tt in range(NTT):
                            vector.tensor_copy(W12[:, 2 * tt:2 * tt + 2],
                                               M8[:, tt, 0:2])
                        for tt in range(NTT):
                            k.inc(vector.tensor_copy(E12[:, 2 * tt:2 * tt + 2],
                                                     IXU[:, tt, 0:2]), "ve")

                with contextlib.ExitStack() as PBR:
                    PSR = PBR.enter_context(
                        nc.psum_tensor("bPSR" + str(ci), [P, 64], F32))

                    for e in range(NEXP):
                        @block.vector
                        def _(vector, e=e):
                            if e == 0:
                                k.wait(vector, "ve")
                            if e >= 1:
                                k.wait_at(vector, "mm", mm_pfx[e - 1])
                            k.inc(vector.tensor_single_scalar(
                                INDA[:, e, :], E12[:, :], float(e),
                                ALU.is_equal), "ve")
                            k.wait(vector, "ve")
                            k.inc(vector.tensor_tensor(
                                RT[:, :], INDA[:, e, 0:32:2],
                                INDA[:, e, 1:32:2], ALU.add), "ve")

                        ve_rt = k.cnt["ve"]

                        @block.tensor
                        def _(tensor, e=e):
                            k.wait_at(tensor, "ve", ve_rt)
                            if e >= 1:
                                k.wait_at(tensor, "ve", ve_pfxread[e - 1])
                            k.inc(tensor.matmul(PSR[:, 0:16], LST[:, :],
                                                RT[:, :], start=True,
                                                stop=True), "mm")
                            k.inc(tensor.matmul(PSR[:, 16:32], ON128[:, :],
                                                RT[:, :], start=True,
                                                stop=True), "mm")

                        mm_pfx.append(k.cnt["mm"])

                        @block.vector
                        def _(vector, e=e):
                            k.wait_at(vector, "mm", mm_pfx[e])
                            vector.tensor_copy(PFX[:, :], PSR[:, 0:16])
                            vector.tensor_tensor_scan(
                                CSX[:, 1:16], ONE16[:, 1:16], PSR[:, 16:31],
                                0.0, ALU.mult, ALU.add)
                            vector.memset(CSX[:, 0:1], 0.0)
                            k.inc(vector.reduce_sum(CNT8[0:1, e:e + 1],
                                                    PSR[0:1, 16:32], AX), "ve")
                            k.wait(vector, "ve")
                            vector.tensor_tensor(TTMP[:, 0:16], PFX[:, :],
                                                 CSX[:, :], ALU.add)
                            k.inc(vector.engine_nop(), "ve")
                            k.wait(vector, "ve")
                            vector.tensor_copy(T2A[:, e, 0:32:2], TTMP[:, 0:16])
                            vector.tensor_tensor(T2A[:, e, 1:32:2],
                                                 TTMP[:, 0:16],
                                                 INDA[:, e, 0:32:2], ALU.add)
                            k.inc(vector.engine_nop(), "ve")
                            k.wait(vector, "ve")
                            vector.tensor_tensor(TTM2[:, :], INDA[:, e, :],
                                                 T2A[:, e, :], ALU.mult)
                            k.inc(vector.engine_nop(), "ve")
                            k.wait(vector, "ve")
                            k.inc(vector.tensor_tensor(LP[:, :], LP[:, :],
                                                       TTM2[:, :], ALU.add),
                                  "ve")
                            ve_pfxread.append(k.cnt["ve"])

                    @block.sync
                    def _(sync):
                        k.wait(sync, "ve")
                        k.dinc(sync.dma_start(out=cc_in[:, :], in_=CNT8[:, :]),
                               "cc")

                    @block.gpsimd
                    def _(gpsimd):
                        k.wait(gpsimd, "cc")
                        k.inc(gpsimd.collective_compute(
                            "AllGather", ALU.bypass,
                            replica_groups=[list(range(8))],
                            ins=[cc_in.ap().opt()],
                            outs=[cc_out.ap().opt()]), "cc")
                        k.wait(gpsimd, "cc")
                        k.dinc(gpsimd.dma_start(out=CCS[:, :],
                                                in_=cc_out[:, :]), "cc2")

                    @block.tensor
                    def _(tensor):
                        k.wait(tensor, "cc2")
                        k.inc(tensor.matmul(PSR[0:1, 32:40], PMASK[:, :],
                                            CCS[:, :], start=True, stop=True),
                              "mm")

                    mm_base = k.cnt["mm"]

                    @block.vector
                    def _(vector):
                        k.wait_at(vector, "mm", mm_base)
                        k.inc(vector.tensor_copy(BASE[:, :], PSR[0:1, 32:40]),
                              "ve")

                    @block.sync
                    def _(sync):
                        k.wait(sync, "ve")
                        k.dinc(sync.dma_start(out=base_h[:, :], in_=BASE[:, :]),
                               "cc")
                        k.wait(sync, "cc")
                        k.dinc(sync.dma_start(
                            out=BASEB[:, :],
                            in_=base_h[:, :].to_broadcast((P, NEXP))), "cc")

                    @block.vector
                    def _(vector):
                        k.wait(vector, "cc")
                        for e in range(NEXP):
                            vector.tensor_scalar(TTMP[:, :], T2A[:, e, :],
                                                 BASEB[:, e:e + 1], None,
                                                 ALU.add)
                            k.inc(vector.engine_nop(), "ve")
                            k.wait(vector, "ve")
                            vector.tensor_tensor(TTM2[:, :], INDA[:, e, :],
                                                 TTMP[:, :], ALU.mult)
                            k.inc(vector.engine_nop(), "ve")
                            k.wait(vector, "ve")
                            k.inc(vector.tensor_tensor(GP[:, :], GP[:, :],
                                                       TTM2[:, :], ALU.add),
                                  "ve")
                            k.wait(vector, "ve")
                        vector.tensor_single_scalar(VLD[:, :], GP[:, :],
                                                    float(CAPACITY - 1),
                                                    ALU.is_le)
                        vector.tensor_scalar_min(SLOT[:, :], LP[:, :],
                                                 float(ECAP - 1))
                        k.inc(vector.engine_nop(), "ve")
                        k.wait(vector, "ve")
                        vector.tensor_tensor(WF[:, :], W12[:, :], VLD[:, :],
                                             ALU.mult)
                        vector.scalar_tensor_tensor(SLOT[:, :], E12[:, :],
                                                    float(ECAP), SLOT[:, :],
                                                    ALU.mult, ALU.add)
                        k.inc(vector.engine_nop(), "ve")
                        k.wait(vector, "ve")
                        k.inc(vector.tensor_copy(SLI[:, :], SLOT[:, :]), "ve")
                        k.wait(vector, "ve")
                        vector.tensor_single_scalar(SH1[:, :], SLI[:, :], 127,
                                                    ALU.bitwise_and)
                        vector.tensor_single_scalar(SH2[:, :], SLI[:, :], 7,
                                                    ALU.arith_shift_right)
                        k.inc(vector.engine_nop(), "ve")
                        k.wait(vector, "ve")
                        vector.tensor_single_scalar(SH1[:, :], SH1[:, :], 40,
                                                    ALU.mult)
                        k.inc(vector.engine_nop(), "ve")
                        k.wait(vector, "ve")
                        k.inc(vector.tensor_tensor(SLH[:, :], SH1[:, :],
                                                   SH2[:, :], ALU.add), "ve")

                    ve_slots = k.cnt["ve"]

                    @block.sync
                    def _(sync):
                        k.wait(sync, "ve")
                        iv = AP(inv_h, 0, [[40, P], [1, 40]])
                        k.dinc(sync.dma_start(out=iv, in_=ZI[:, :]), "pz")

                    @block.gpsimd
                    def _(gpsimd):
                        k.wait(gpsimd, "pz")
                        k.wait_at(gpsimd, "ve", ve_slots)
                        for col in range(2 * NTT):
                            tt = col // 2
                            k.dinc(gpsimd.indirect_dma_start(
                                inv_h[:, :],
                                IndirectOffsetOnAxis(SLH[:, col:col + 1], 0),
                                TOK[:, tt:tt + 1], None), "pk")

                    @block.sync
                    def _(sync):
                        k.wait(sync, "pk")
                        iv = AP(inv_h, 0, [[40, P], [1, 40]])
                        k.dinc(sync.dma_start(out=INVR[:, :], in_=iv), "pz")

                PBE = PB.enter_context(contextlib.ExitStack())
                esb = lambda n, s, d=F32: PBE.enter_context(
                    nc.sbuf_tensor(n + str(ci), s, d))
                PKT = esb("bPKT", [P, 2, D], BF16)
                PACKE = esb("bPACKE", [P, 2, NCH, ECAP], BF16)
                CFCE = esb("bCFCE", [P, NCH, DFF], BF16)
                CPJ = esb("bCPJ", [P, 12, D], BF16)
                HBF = esb("bHBF", [P, 12, ECAP], BF16)
                TSIL = esb("bTSIL", [P, 2, 512])
                SLOB = esb("bSLOB", [P, 2, D])

                PFC = PBE.enter_context(
                    nc.psum_tensor("bPFC" + str(ci), [P, 4, 512], F32))
                PPJ = PBE.enter_context(
                    nc.psum_tensor("bPPJ" + str(ci), [P, D], F32))

                pk_base = k.cnt["pk"]
                so_base = k.cnt["so"]
                tr_base = k.cnt["tr"]
                mm_tr = []
                ve_pke = []
                mm_fc = []
                ve_hbf = []
                ve_hbf_cb = []
                mm_pj = []
                ve_slb = []
                cbw = [(0, 512), (512, 128)]

                for e in range(NEXP):
                    @block.gpsimd
                    def _(gpsimd, e=e):
                        if e == 0:
                            k.wait(gpsimd, "pz")
                            k.wait(gpsimd, "xk")
                        for st in range(5):
                            gst = e * 5 + st
                            if gst >= 2:
                                k.wait_at(gpsimd, "tr", tr_base + gst - 1)
                            k.dinc(gpsimd.indirect_dma_start(
                                PKT[:, gst % 2, :], None,
                                xtok_h[:, :],
                                IndirectOffsetOnAxis(
                                    INVR[:, gst:gst + 1], 0)), "pk")
                        if e >= 1:
                            k.wait_at(gpsimd, "mm",
                                      mm_fc[((e - 1) * 12 + 11) * 2 + 1])
                        for kc in range(NCH):
                            k.dinc(gpsimd.dma_start(
                                out=CFCE[:, kc, :],
                                in_=c_fc[e, kc * P:(kc + 1) * P, :]), "wld2")
                        if e >= 1:
                            k.wait_at(gpsimd, "mm", mm_pj[e * 5 - 1])
                        for k2 in range(12):
                            k.dinc(gpsimd.dma_start(
                                out=CPJ[:, k2, :],
                                in_=c_proj[e, k2 * P:(k2 + 1) * P, :]), "wld2")

                    for st in range(5):
                        gst = e * 5 + st

                        @block.tensor
                        def _(tensor, e=e, st=st, gst=gst):
                            k.wait_at(tensor, "pk",
                                      pk_base + 16 * (gst + 1))
                            if gst < 2:
                                k.wait(tensor, "ev2")
                            if gst >= 1:
                                k.wait_at(tensor, "ve", ve_pke[gst - 1])
                            for c in range(NCH):
                                slot = (gst % 2) * 8 + c
                                k.inc(tensor.transpose(
                                    PSAB[:, slot * P:(slot + 1) * P],
                                    PKT[:, gst % 2, c * P:(c + 1) * P],
                                    IDB[:, :]), "mm")
                            k.inc(tensor.nop(), "tr")
                            mm_tr.append(k.cnt["mm"])

                        @block.vector
                        def _(vector, e=e, st=st, gst=gst):
                            k.wait_at(vector, "mm", mm_tr[gst])
                            for c in range(NCH):
                                slot = (gst % 2) * 8 + c
                                vector.tensor_copy(
                                    PACKE[:, e % 2, c,
                                          st * P:(st + 1) * P],
                                    PSAB[:, slot * P:(slot + 1) * P])
                            k.inc(vector.engine_nop(), "ve")
                            ve_pke.append(k.cnt["ve"])

                    for pb in range(12):
                        @block.tensor
                        def _(tensor, e=e, pb=pb):
                            if pb == 0:
                                k.wait_at(tensor, "wld2", k.cnt["wld2"])
                                k.wait_at(tensor, "ve", ve_pke[e * 5 + 4])
                            for cb, (c0, cw) in enumerate(cbw):
                                if e * 12 + pb >= 1:
                                    k.wait_at(tensor, "ve",
                                              ve_hbf_cb[
                                                  (e * 12 + pb - 1) * 2 + cb])
                                for half, mc in enumerate((pb, pb + 12)):
                                    for kc in range(NCH):
                                        k.inc(tensor.matmul(
                                            PFC[:, cb * 2 + half, 0:cw],
                                            CFCE[:, kc,
                                                 mc * P:(mc + 1) * P],
                                            PACKE[:, e % 2, kc,
                                                  c0:c0 + cw],
                                            start=(kc == 0),
                                            stop=(kc == NCH - 1)), "mm")
                                mm_fc.append(k.cnt["mm"])

                        @block.scalar
                        def _(scalar, e=e, pb=pb):
                            if e * 12 + pb >= 1:
                                k.wait_at(scalar, "ve",
                                          ve_hbf[e * 12 + pb - 1])
                            for cb, (c0, cw) in enumerate(cbw):
                                k.wait_at(scalar, "mm",
                                          mm_fc[(e * 12 + pb) * 2 + cb])
                                k.inc(scalar.activation(
                                    TSIL[:, cb, 0:cw],
                                    PFC[:, cb * 2 + 0, 0:cw], AF.Silu), "se")

                        se_here = k.cnt["se"]

                        @block.vector
                        def _(vector, e=e, pb=pb, se_here=se_here):
                            k.wait_at(vector, "se", se_here)
                            if pb == 0 and e >= 1:
                                k.wait_at(vector, "mm", mm_pj[e * 5 - 1])
                            for cb, (c0, cw) in enumerate(cbw):
                                k.inc(vector.tensor_tensor(
                                    HBF[:, pb, c0:c0 + cw],
                                    TSIL[:, cb, 0:cw],
                                    PFC[:, cb * 2 + 1, 0:cw], ALU.mult), "ve")
                                ve_hbf_cb.append(k.cnt["ve"])
                            ve_hbf.append(k.cnt["ve"])

                    for s in range(5):
                        @block.tensor
                        def _(tensor, e=e, s=s):
                            if s == 0:
                                k.wait_at(tensor, "ve", ve_hbf[e * 12 + 11])
                            if e * 5 + s >= 1:
                                k.wait_at(tensor, "ve", ve_slb[e * 5 + s - 1])
                            for k2 in range(12):
                                for nh in range(2):
                                    k.inc(tensor.matmul(
                                        PPJ[:, nh * 512:(nh + 1) * 512],
                                        HBF[:, k2, s * P:(s + 1) * P],
                                        CPJ[:, k2, nh * 512:(nh + 1) * 512],
                                        start=(k2 == 0), stop=(k2 == 11)),
                                        "mm")
                            mm_pj.append(k.cnt["mm"])

                        @block.vector
                        def _(vector, e=e, s=s):
                            gs2 = e * 5 + s
                            k.wait_at(vector, "mm", mm_pj[gs2])
                            if gs2 >= 2:
                                k.wait_at(vector, "so",
                                          so_base + 16 * (gs2 - 1))
                            for nh in range(2):
                                k.inc(vector.tensor_copy(
                                    SLOB[:, gs2 % 2, nh * 512:(nh + 1) * 512],
                                    PPJ[:, nh * 512:(nh + 1) * 512]), "ve")
                            ve_slb.append(k.cnt["ve"])

                        @block.sync
                        def _(sync, e=e, s=s):
                            gs2 = e * 5 + s
                            k.wait_at(sync, "ve", ve_slb[gs2])
                            r0 = e * ECAP + s * P
                            k.dinc(sync.dma_start(
                                out=slotout_h[r0:r0 + P, :],
                                in_=SLOB[:, gs2 % 2, :]), "so")

                PBE.close()
                UPK = bsb("bUPK", [P, 2, 2, D])

                veu0 = k.cnt["ve"]

                @block.gpsimd
                def _(gpsimd):
                    k.wait(gpsimd, "so")
                    for tt in range(NTT):
                        if tt >= 2:
                            k.wait_at(gpsimd, "ve", veu0 + 3 * (tt - 1))
                        for c in range(2):
                            k.dinc(gpsimd.indirect_dma_start(
                                UPK[:, tt % 2, c, :], None,
                                slotout_h[:, :],
                                IndirectOffsetOnAxis(
                                    SLI[:, 2 * tt + c:2 * tt + c + 1], 0)),
                                "pk")

                pk_upk0 = k.cnt["pk"] - 16 * 2 * NTT

                for tt in range(NTT):
                    @block.vector
                    def _(vector, tt=tt):
                        k.wait_at(vector, "pk", pk_upk0 + 32 * (tt + 1))
                        par = tt % 2
                        vector.tensor_scalar(UPK[:, par, 1, :],
                                             UPK[:, par, 1, :],
                                             WF[:, 2 * tt + 1:2 * tt + 2],
                                             None, ALU.mult)
                        vector.tensor_scalar(UPK[:, par, 0, :],
                                             UPK[:, par, 0, :],
                                             WF[:, 2 * tt:2 * tt + 1],
                                             None, ALU.mult)
                        k.inc(vector.engine_nop(), "ve")
                        k.wait(vector, "ve")
                        vector.tensor_tensor(UPK[:, par, 0, :],
                                             UPK[:, par, 0, :],
                                             UPK[:, par, 1, :], ALU.add)
                        k.inc(vector.engine_nop(), "ve")
                        k.wait(vector, "ve")
                        k.inc(vector.tensor_tensor(X[:, tt, :], X[:, tt, :],
                                                   UPK[:, par, 0, :], ALU.add),
                              "ve")
                        assert k.cnt["ve"] == veu0 + 3 * (tt + 1)

        qf_h = dint("qf_h", [H * (DH + 1), QFW], BF16)
        kf_h = dint("kf_h", [H * (DH + 1), T], BF16)

        def emit_attn_b():
            with contextlib.ExitStack() as PC:
                csb = lambda n, s, d=F32: PC.enter_context(
                    nc.sbuf_tensor(n, s, d))

                MK = csb("cMK", [DH, H])
                MK1 = csb("cMK1", [1, H])
                MRP = csb("cMRP", [DH, H])
                MRP1 = csb("cMRP1", [1, H])
                MKS = csb("cMKS", [DH, 1])
                MKSB = csb("cMKSB", [DH, 1], BF16)
                MH = csb("cMH", [1, H + 2])
                NMHB = csb("cNMHB", [P, H])
                UBTB = csb("cUBTB", [DH, H], BF16)
                VBTB = csb("cVBTB", [DH, H], BF16)
                XFC = csb("cXFC", [P, NCH, T], BF16)

                @block.vector
                def _(vector):
                    k.wait(vector, "ld")
                    vector.tensor_copy(UBTB[:, :], UBT[:, :])
                    k.inc(vector.tensor_copy(VBTB[:, :], VBT[:, :]), "ve")

                with contextlib.ExitStack() as PCW:
                    wsb = lambda n, s, d=F32: PCW.enter_context(
                        nc.sbuf_tensor(n, s, d))
                    PSAC = PCW.enter_context(
                        nc.psum_tensor("cPSAC", [P, 2048], BF16))
                    PC1 = PCW.enter_context(
                        nc.psum_tensor("cPC1", [P, 4, 512], F32))
                    PB1 = PCW.enter_context(
                        nc.psum_tensor("cPB1", [1, 2, 512], F32))
                    BNDS = SQ[0:1, :, :].rearrange("p a b -> p (a b)")

                    with contextlib.ExitStack() as PCL:
                        XLTC = PCL.enter_context(
                            nc.sbuf_tensor("cXLT", [P, NTT, D], BF16))
                        emit_ln(XLTC, rms=False)
                        emit_transposes(XLTC, XFC, PSAC, scale_wnc=False)

                    WRAW = wsb("cWRAW", [P, NCH, D], BF16)
                    WT = wsb("cWT", [P, NCH, D], BF16)
                    QFS = wsb("cQFS", [DH + 1, 2, QFW], BF16)
                    KFS = wsb("cKFS", [DH, 2, T], BF16)
                    KRW = wsb("cKRW", [1, 2, T], BF16)
                    RRW = wsb("cRRW", [1, 2, 1024], BF16)
                    ABSQ = wsb("cABSQ", [DH, T], BF16)
                    VS = wsb("cVS", [P, 2, H * (DH + 1)], BF16)
                    MRP2 = wsb("cMRP2", [DH, 1])

                    @block.vector
                    def _(vector):
                        k.wait(vector, "mm")
                        vector.memset(QFS[DH:DH + 1, :, :], 1.0)
                        vector.memset(QFS[:, :, T:QFW], 0.0)
                        k.inc(vector.memset(VS[:, :, :], 1.0), "ve")

                    def load_wt(wmat_d):
                        @block.gpsimd
                        def _(gpsimd):
                            k.wait(gpsimd, "mm")
                            src = AP(wmat_d, 0,
                                     [[D, P], [P * D, NCH], [1, D]])
                            k.dinc(gpsimd.dma_start(out=WRAW[:, :, :],
                                                    in_=src), "wld2")

                        mmw0 = k.cnt["mm"]
                        ev0 = k.cnt["ev2"]

                        @block.tensor
                        def _(tensor):
                            k.wait(tensor, "wld2")
                            k.wait(tensor, "ev2")
                            for n in range(NCH * NCH):
                                oc, ic = divmod(n, NCH)
                                if n >= 16:
                                    k.wait_at(tensor, "ev2",
                                              ev0 + (n - 16) // 8 + 1)
                                slot = n % 16
                                k.inc(tensor.transpose(
                                    PSAC[:, slot * P:(slot + 1) * P],
                                    WRAW[:, oc, ic * P:(ic + 1) * P],
                                    IDB[:, :]), "mm")

                        @block.vector
                        def _(vector):
                            for g in range(NCH * NCH // 8):
                                k.wait_at(vector, "mm", mmw0 + (g + 1) * 8)
                                for j in range(8):
                                    n = g * 8 + j
                                    oc, ic = divmod(n, NCH)
                                    slot = n % 16
                                    vector.tensor_copy(
                                        WT[:, ic, oc * P:(oc + 1) * P],
                                        PSAC[:, slot * P:(slot + 1) * P])
                                k.inc(vector.engine_nop(), "ev2")

                    load_wt(wr_d)
                    mm_rpj = []
                    ve_rpj = []
                    ve_rr = []
                    rp_s0 = k.cnt["sk"]
                    with contextlib.ExitStack() as PCR:
                        REMBF = PCR.enter_context(
                            nc.sbuf_tensor("cREMBF", [P, NCH, 1024], BF16))
                        RPJS = PCR.enter_context(
                            nc.sbuf_tensor("cRPJS", [DH, 2, 1024], BF16))
                        for q in range(4):
                            @block.gpsimd
                            def _(gpsimd, q=q):
                                if q == 0:
                                    k.wait(gpsimd, "mm")
                                else:
                                    k.wait_at(gpsimd, "mm", mm_rpj[-1])
                                src = AP(c_remb_t, q * 1024,
                                         [[RPN, P], [P * RPN, NCH], [1, 1024]])
                                k.dinc(gpsimd.dma_start(out=REMBF[:, :, :],
                                                        in_=src), "wld2")

                            for h in range(H):
                                hpar = h % 2
                                hi = q * H + h

                                @block.tensor
                                def _(tensor, q=q, h=h, hi=hi):
                                    k.wait(tensor, "wld2")
                                    if hi == 0:
                                        k.wait(tensor, "ev2")
                                    if hi >= 2:
                                        k.wait_at(tensor, "sk",
                                                  rp_s0 + 32 * (hi - 1))
                                    for fc in range(2):
                                        for ic in range(NCH):
                                            k.inc(tensor.matmul(
                                                PC1[0:DH, fc, :],
                                                WT[:, ic,
                                                   h * DH:(h + 1) * DH],
                                                REMBF[:, ic,
                                                      fc * 512:(fc + 1) * 512],
                                                start=(ic == 0),
                                                stop=(ic == NCH - 1)), "mm")
                                    mm_rpj.append(k.cnt["mm"])

                                @block.vector
                                def _(vector, hi=hi, hpar=hpar):
                                    k.wait_at(vector, "mm", mm_rpj[hi])
                                    for fc in range(2):
                                        vector.tensor_copy(
                                            RPJS[:, hpar,
                                                 fc * 512:(fc + 1) * 512],
                                            PC1[0:DH, fc, :])
                                    k.inc(vector.engine_nop(), "ve")
                                    ve_rpj.append(k.cnt["ve"])

                                @block.tensor
                                def _(tensor, h=h, hi=hi, hpar=hpar):
                                    k.wait_at(tensor, "ve", ve_rpj[hi])
                                    for fc in range(2):
                                        k.inc(tensor.matmul(
                                            PB1[:, fc, :],
                                            VBTB[:, h:h + 1],
                                            RPJS[:, hpar,
                                                 fc * 512:(fc + 1) * 512],
                                            start=True, stop=True), "mm")

                                mm_rrow = k.cnt["mm"]

                                @block.vector
                                def _(vector, q=q, h=h, hpar=hpar,
                                      mm_rrow=mm_rrow):
                                    k.wait_at(vector, "mm", mm_rrow)
                                    for fc in range(2):
                                        vector.tensor_copy(
                                            RRW[:, hpar,
                                                fc * 512:(fc + 1) * 512],
                                            PB1[:, fc, :])
                                    vector.reduce_max(
                                        MRP2[:, 0:1], RPJS[:, hpar, :], AX,
                                        apply_absolute_value=True)
                                    k.inc(vector.engine_nop(), "ve")
                                    k.wait(vector, "ve")
                                    if q == 0:
                                        vector.tensor_copy(MRP[:, h:h + 1],
                                                           MRP2[:, 0:1])
                                        vector.reduce_max(
                                            MRP1[:, h:h + 1], RRW[:, hpar, :],
                                            AX, apply_absolute_value=True)
                                    else:
                                        vector.tensor_tensor(
                                            MRP[:, h:h + 1], MRP[:, h:h + 1],
                                            MRP2[:, 0:1], ALU.max)
                                        vector.reduce_max(
                                            MH[:, H + 1:H + 2],
                                            RRW[:, hpar, :], AX,
                                            apply_absolute_value=True)
                                        k.inc(vector.engine_nop(), "ve")
                                        k.wait(vector, "ve")
                                        vector.tensor_tensor(
                                            MRP1[:, h:h + 1], MRP1[:, h:h + 1],
                                            MH[:, H + 1:H + 2], ALU.max)
                                    k.inc(vector.engine_nop(), "ve")

                                @block.sync
                                def _(sync, q=q, h=h, hpar=hpar):
                                    k.wait(sync, "ve")
                                    cs = slice(q * 1024, (q + 1) * 1024)
                                    k.dinc(sync.dma_start(
                                        out=rp_hd[h * (DH + 1):
                                                  h * (DH + 1) + DH, cs],
                                        in_=RPJS[:, hpar, :]), "sk")
                                    k.dinc(sync.dma_start(
                                        out=rp_hd[h * (DH + 1) + DH:
                                                  h * (DH + 1) + DH + 1, cs],
                                        in_=RRW[:, hpar, :]), "sk")

                    if _C_STAGES < 2:
                        return
                    load_wt(wk_d)
                    mm_k = []
                    ve_k = []
                    ve_kr = []
                    k_s0 = k.cnt["sk"]
                    for h in range(H):
                        hpar = h % 2

                        @block.tensor
                        def _(tensor, h=h, hpar=hpar):
                            if h == 0:
                                k.wait(tensor, "ev2")
                            if h >= 2:
                                k.wait_at(tensor, "sk", k_s0 + 32 * (h - 1))
                            for fc in range(4):
                                for ic in range(NCH):
                                    k.inc(tensor.matmul(
                                        PC1[0:DH, fc, :],
                                        WT[:, ic, h * DH:(h + 1) * DH],
                                        XFC[:, ic, fc * 512:(fc + 1) * 512],
                                        start=(ic == 0),
                                        stop=(ic == NCH - 1)), "mm")
                            mm_k.append(k.cnt["mm"])

                        @block.vector
                        def _(vector, h=h, hpar=hpar):
                            k.wait_at(vector, "mm", mm_k[h])
                            for fc in range(4):
                                vector.tensor_copy(
                                    KFS[:, hpar, fc * 512:(fc + 1) * 512],
                                    PC1[0:DH, fc, :])
                            k.inc(vector.engine_nop(), "ve")
                            k.wait(vector, "ve")
                            k.inc(vector.reduce_max(
                                MK[:, h:h + 1], KFS[:, hpar, :], AX,
                                apply_absolute_value=True), "ve")
                            ve_k.append(k.cnt["ve"])

                        for rnd in range(2):
                            @block.tensor
                            def _(tensor, h=h, hpar=hpar, rnd=rnd):
                                if rnd == 0:
                                    k.wait_at(tensor, "ve", ve_k[h])
                                else:
                                    k.wait_at(tensor, "ve", ve_kr[-1])
                                for fc2 in range(2):
                                    fc = rnd * 2 + fc2
                                    k.inc(tensor.matmul(
                                        PB1[:, fc2, :], UBTB[:, h:h + 1],
                                        KFS[:, hpar, fc * 512:(fc + 1) * 512],
                                        start=True, stop=True), "mm")

                            mm_krow = k.cnt["mm"]

                            @block.vector
                            def _(vector, hpar=hpar, rnd=rnd,
                                  mm_krow=mm_krow):
                                k.wait_at(vector, "mm", mm_krow)
                                for fc2 in range(2):
                                    fc = rnd * 2 + fc2
                                    vector.tensor_copy(
                                        KRW[:, hpar,
                                            fc * 512:(fc + 1) * 512],
                                        PB1[:, fc2, :])
                                k.inc(vector.engine_nop(), "ve")
                                ve_kr.append(k.cnt["ve"])

                        @block.vector
                        def _(vector, h=h, hpar=hpar):
                            k.wait(vector, "ve")
                            k.inc(vector.reduce_max(
                                MK1[:, h:h + 1], KRW[:, hpar, :], AX,
                                apply_absolute_value=True), "ve")

                        @block.sync
                        def _(sync, h=h, hpar=hpar):
                            k.wait(sync, "ve")
                            k.dinc(sync.dma_start(
                                out=kf_h[h * (DH + 1):h * (DH + 1) + DH, :],
                                in_=KFS[:, hpar, :]), "sk")
                            k.dinc(sync.dma_start(
                                out=kf_h[h * (DH + 1) + DH:
                                         h * (DH + 1) + DH + 1, :],
                                in_=KRW[:, hpar, :]), "sk")

                    load_wt(wq_d)
                    mm_q = []
                    ve_q = []
                    ve_bn = []
                    q_s0 = k.cnt["sk"]
                    for h in range(H):
                        hpar = h % 2

                        @block.tensor
                        def _(tensor, h=h, hpar=hpar):
                            if h == 0:
                                k.wait(tensor, "ev2")
                            if h >= 2:
                                k.wait_at(tensor, "sk", q_s0 + 16 * (h - 1))
                            for fc in range(4):
                                for ic in range(NCH):
                                    k.inc(tensor.matmul(
                                        PC1[0:DH, fc, :],
                                        WT[:, ic, h * DH:(h + 1) * DH],
                                        XFC[:, ic, fc * 512:(fc + 1) * 512],
                                        start=(ic == 0),
                                        stop=(ic == NCH - 1)), "mm")
                            mm_q.append(k.cnt["mm"])

                        @block.vector
                        def _(vector, h=h, hpar=hpar):
                            k.wait_at(vector, "mm", mm_q[h])
                            for fc in range(4):
                                vector.tensor_copy(
                                    QFS[0:DH, hpar, fc * 512:(fc + 1) * 512],
                                    PC1[0:DH, fc, :])
                            vector.tensor_tensor(MKS[:, 0:1], MK[:, h:h + 1],
                                                 MRP[:, h:h + 1], ALU.add)
                            k.inc(vector.engine_nop(), "ve")
                            k.wait(vector, "ve")
                            k.inc(vector.tensor_copy(MKSB[:, 0:1],
                                                     MKS[:, 0:1]), "ve")
                            ve_q.append(k.cnt["ve"])

                        @block.scalar
                        def _(scalar, h=h, hpar=hpar):
                            k.wait_at(scalar, "ve", ve_q[h])
                            k.inc(scalar.activation(
                                ABSQ[:, :], QFS[0:DH, hpar, 0:T],
                                AF.Abs), "se")

                        se_abs = k.cnt["se"]

                        for rnd in range(2):
                            @block.tensor
                            def _(tensor, h=h, hpar=hpar, rnd=rnd,
                                  se_abs=se_abs):
                                if rnd == 0:
                                    k.wait_at(tensor, "se", se_abs)
                                else:
                                    k.wait_at(tensor, "ve", ve_bn[-1])
                                for fc2 in range(2):
                                    fc = rnd * 2 + fc2
                                    k.inc(tensor.matmul(
                                        PB1[:, fc2, :], MKSB[:, 0:1],
                                        ABSQ[:,
                                             fc * 512:(fc + 1) * 512],
                                        start=True, stop=True), "mm")

                            mm_bnd = k.cnt["mm"]

                            @block.vector
                            def _(vector, rnd=rnd, mm_bnd=mm_bnd):
                                k.wait_at(vector, "mm", mm_bnd)
                                vector.tensor_copy(
                                    BNDS[:, rnd * 1024:rnd * 1024 + 512],
                                    PB1[:, 0, :])
                                vector.tensor_copy(
                                    BNDS[:, rnd * 1024 + 512:
                                         (rnd + 1) * 1024],
                                    PB1[:, 1, :])
                                k.inc(vector.engine_nop(), "ve")
                                ve_bn.append(k.cnt["ve"])

                        @block.vector
                        def _(vector, h=h):
                            k.wait(vector, "ve")
                            vector.reduce_max(MH[:, H:H + 1], BNDS[:, :], AX)
                            k.inc(vector.engine_nop(), "ve")
                            k.wait(vector, "ve")
                            vector.tensor_tensor(MH[:, H:H + 1],
                                                 MH[:, H:H + 1],
                                                 MK1[:, h:h + 1], ALU.add)
                            k.inc(vector.engine_nop(), "ve")
                            k.wait(vector, "ve")
                            k.inc(vector.tensor_tensor(
                                MH[:, h:h + 1], MH[:, H:H + 1],
                                MRP1[:, h:h + 1], ALU.add), "ve")

                        @block.sync
                        def _(sync, h=h, hpar=hpar):
                            k.wait(sync, "ve")
                            k.dinc(sync.dma_start(
                                out=qf_h[h * (DH + 1):(h + 1) * (DH + 1), :],
                                in_=QFS[:, hpar, :]), "sk")

                    @block.vector
                    def _(vector):
                        k.wait(vector, "ve")
                        k.inc(vector.tensor_scalar(MH[:, 0:H], MH[:, 0:H],
                                                   -0.125, None, ALU.mult),
                              "ve")

                    @block.sync
                    def _(sync):
                        k.wait(sync, "ve")
                        k.dinc(sync.dma_start(out=mh_hd[:, :],
                                              in_=MH[:, 0:H]), "cc")
                        k.wait(sync, "cc")
                        k.dinc(sync.dma_start(
                            out=NMHB[:, :],
                            in_=mh_hd[:, :].to_broadcast((P, H))), "cc")

                    load_wt(wv_d)
                    mm_v = []
                    ve_v = []
                    v_s0 = k.cnt["sk"]
                    for tt in range(NTT):
                        @block.tensor
                        def _(tensor, tt=tt):
                            if tt == 0:
                                k.wait(tensor, "ev2")
                            if tt >= 2:
                                k.wait_at(tensor, "sk", v_s0 + 16 * (tt - 1))
                            if tt >= 1:
                                k.wait_at(tensor, "ve", ve_v[tt - 1])
                            for nh in range(2):
                                for ic in range(NCH):
                                    k.inc(tensor.matmul(
                                        PC1[:, (tt % 2) * 2 + nh, :],
                                        XFC[:, ic, tt * P:(tt + 1) * P],
                                        WT[:, ic, nh * 512:(nh + 1) * 512],
                                        start=(ic == 0),
                                        stop=(ic == NCH - 1)), "mm")
                            mm_v.append(k.cnt["mm"])

                        @block.vector
                        def _(vector, tt=tt):
                            k.wait_at(vector, "mm", mm_v[tt])
                            for nh in range(2):
                                dst = AP(VS.ap().tensor,
                                         (tt % 2) * H * (DH + 1)
                                         + nh * 8 * (DH + 1),
                                         [[2 * H * (DH + 1), P],
                                          [DH + 1, 8], [1, DH]])
                                k.inc(vector.tensor_copy(
                                    dst, PC1[:, (tt % 2) * 2 + nh, :]), "ve")
                            ve_v.append(k.cnt["ve"])

                        @block.sync
                        def _(sync, tt=tt):
                            k.wait_at(sync, "ve", ve_v[tt])
                            dst = AP(vt_hd, tt * P * H * (DH + 1),
                                     [[H * (DH + 1), P], [1, H * (DH + 1)]])
                            k.dinc(sync.dma_start(
                                out=dst, in_=VS[:, tt % 2, :]), "sk")

                if _C_STAGES < 3:
                    return
                ssb = lambda n, s, d=F32: PC.enter_context(
                    nc.sbuf_tensor(n, s, d))
                QFL = ssb("cQFL", [DH + 1, 2, QFW], BF16)
                KFL = ssb("cKFL", [DH + 1, 2, T], BF16)
                RPJL = ssb("cRPJL", [DH + 1, 2, RPN], BF16)
                VTH = ssb("cVTH", [P, 2, NTT, DH + 1], BF16)
                SKB = ssb("cSKB", [P, 2, SKW], BF16)
                BAND = ssb("cBAND", [P, 2, 2, T], BF16)
                PQ = ssb("cPQ", [P, 2, 2, T], BF16)
                PQT = ssb("cPQT", [P, 2, T], BF16)
                PT = ssb("cPT", [P, NTT, 256], BF16)
                OSB = ssb("cOSB", [DH + 1, 256])
                RCL = ssb("cRCL", [P, 2])

                PSC = PC.enter_context(
                    nc.psum_tensor("cPSC", [P, 4, 512], F32))
                PSPT = PC.enter_context(
                    nc.psum_tensor("cPSPT", [P, 2048], BF16))
                PPV = PC.enter_context(
                    nc.psum_tensor("cPPV", [DH + 1, 256], F32))
                PSE = PC.enter_context(
                    nc.psum_tensor("cPSE", [P, 512], F32))

                hl0 = k.cnt["ld"]
                hd_mm = []
                for h in range(_C_HEADS):
                    hpar = h % 2
                    exp_se = []
                    exp_sub = []
                    exp_sub_all = []
                    ac_mm2 = []
                    pmul_ve = []
                    ptr_mm = []

                    @block.gpsimd
                    def _(sync, h=h, hpar=hpar):
                        if h == 0:
                            k.wait(sync, "sk")
                        if h >= 2:
                            k.wait_at(sync, "mm", hd_mm[h - 2])
                        k.dinc(sync.dma_start(
                            out=QFL[:, hpar, :],
                            in_=qf_h[h * (DH + 1):(h + 1) * (DH + 1), :]),
                            "hl")
                        k.dinc(sync.dma_start(
                            out=KFL[:, hpar, :],
                            in_=kf_h[h * (DH + 1):(h + 1) * (DH + 1), :]),
                            "hl")
                        k.dinc(sync.dma_start(
                            out=RPJL[:, hpar, :],
                            in_=rp_hd[h * (DH + 1):(h + 1) * (DH + 1), :]),
                            "hl")
                        src = AP(vt_hd, h * (DH + 1),
                                 [[H * (DH + 1), P],
                                  [P * H * (DH + 1), NTT], [1, DH + 1]])
                        k.dinc(sync.dma_start(out=VTH[:, hpar, :, :],
                                              in_=src), "hl")

                    hload = k.cnt["hl"]

                    bd_sk0 = k.cnt["sk"]
                    bd_tr0 = k.cnt["tr"]
                    for qt in range(NTT):
                        i0 = qt * P

                        @block.tensor
                        def _(tensor, h=h, hpar=hpar, qt=qt, i0=i0,
                              hload=hload):
                            if qt == 0:
                                k.wait_at(tensor, "hl", hload)
                                k.wait(tensor, "ve")
                                k.wait(tensor, "se")
                            if qt >= 1:
                                k.wait_at(tensor, "tr", bd_tr0 + qt)
                            pend = 128 + i0
                            for uc in range(5):
                                u0 = uc * 512
                                u1 = min(u0 + 512, SKW)
                                segs = []
                                if u0 <= pend:
                                    segs.append((u0, min(u1 - 1, pend),
                                                 3967 - i0, 0))
                                if u1 - 1 >= pend + 1:
                                    segs.append((max(u0, pend + 1), u1 - 1,
                                                 -129 - i0, 1))
                                for (a, b2, roff, fut) in segs:
                                    q0 = i0 + 1 if fut else i0
                                    dst = (PSC[:, uc, a - u0:b2 + 1 - u0]
                                           if uc < 4 else
                                           PSE[:, a - u0:b2 + 1 - u0])
                                    k.inc(tensor.matmul(
                                        dst,
                                        QFL[:, hpar, q0:q0 + P],
                                        RPJL[:, hpar, a + roff:b2 + 1 + roff],
                                        start=True, stop=True), "mm")

                        mm_bd = k.cnt["mm"]

                        @block.scalar
                        def _(scalar, h=h, qt=qt, mm_bd=mm_bd):
                            k.wait_at(scalar, "mm", mm_bd)
                            if qt >= 2:
                                k.wait_at(scalar, "sk",
                                          bd_sk0 + 16 * (qt - 1))
                            par = qt % 2
                            flat = PSC[:, :, :].rearrange("p a b -> p (a b)")
                            scalar.activation(SKB[:, par, 0:1024],
                                              flat[:, 0:1024], AF.Exp,
                                              scale=0.125)
                            scalar.activation(SKB[:, par, 1024:2048],
                                              flat[:, 1024:2048], AF.Exp,
                                              scale=0.125)
                            k.inc(scalar.activation(SKB[:, par, 2048:SKW],
                                                    PSE[:, 0:128], AF.Exp,
                                                    scale=0.125), "tr")

                        @block.sync
                        def _(sync, h=h, qt=qt):
                            k.wait_at(sync, "tr", bd_tr0 + qt + 1)
                            r0 = (h * NTT + qt) * P
                            k.dinc(sync.dma_start(
                                out=skw_h[r0:r0 + P, :],
                                in_=SKB[:, qt % 2, :]), "sk")

                    bd_done = k.cnt["sk"]

                    ac_mm = []
                    pv_ve = []
                    os_ve_l = []
                    tT_ev20 = {}
                    nqp = NTT // 2 if _C_SC >= 2 else 0

                    def emit_band(qp, h=h):
                        @block.gpsimd
                        def _(sync, h=h, qp=qp):
                            k.wait_at(sync, "sk", bd_done)
                            if qp >= 2:
                                k.wait_at(sync, "ve",
                                          pmul_ve[2 * (qp - 2) + 1])
                            for qtl in range(2):
                                qt = 2 * qp + qtl
                                base = (h * NTT + qt) * P * SKW + 127
                                src = AP(skw_h, base, [[SKW - 1, P], [1, T]])
                                k.dinc(sync.dma_start(
                                    out=BAND[:, qp % 2, qtl, :], in_=src),
                                    "hl")
                        return k.cnt["hl"]

                    def emit_half(qp, qtl, band_ld, h=h, hpar=hpar):
                        idx = 2 * qp + qtl

                        @block.tensor
                        def _(tensor, h=h, hpar=hpar, qp=qp, qtl=qtl,
                              idx=idx):
                            if idx >= 1:
                                k.wait_at(tensor, "se", exp_sub_all[idx - 1])
                            qt = 2 * qp + qtl
                            for jc in range(4):
                                js = slice(jc * 512, (jc + 1) * 512)
                                k.inc(tensor.matmul(
                                    PSC[:, jc, :],
                                    QFL[:, hpar, qt * P:(qt + 1) * P],
                                    KFL[:, hpar, js],
                                    start=True, stop=True), "mm")
                            ac_mm2.append(k.cnt["mm"])

                        @block.scalar
                        def _(scalar, h=h, qp=qp, qtl=qtl):
                            k.wait_at(scalar, "mm", ac_mm2[-1])
                            if qp >= 1:
                                k.wait_at(scalar, "ve",
                                          pmul_ve[2 * (qp - 1) + qtl])
                            flat = PSC[:, :, :].rearrange("p a b -> p (a b)")
                            scalar.activation(
                                PQT[:, qtl, 0:1024], flat[:, 0:1024],
                                AF.Exp, scale=0.125,
                                bias=NMHB[:, h:h + 1])
                            k.inc(scalar.activation(
                                PQT[:, qtl, 1024:2048], flat[:, 1024:2048],
                                AF.Exp, scale=0.125,
                                bias=NMHB[:, h:h + 1]), "se")
                            exp_sub_all.append(k.cnt["se"])

                        exp_sub.append(k.cnt["se"])
                        ac_mm.append(k.cnt["mm"])

                        @block.vector
                        def _(vector, qp=qp, qtl=qtl, band_ld=band_ld):
                            k.wait_at(vector, "hl", band_ld)
                            k.wait_at(vector, "se", exp_sub[-1])
                            k.inc(vector.tensor_tensor(
                                PQ[:, qp % 2, qtl, :], PQT[:, qtl, :],
                                BAND[:, qp % 2, qtl, :], ALU.mult), "ve")
                            pmul_ve.append(k.cnt["ve"])

                    def emit_tail_T(qp):
                        ptr0 = k.cnt["mm"]
                        ev20 = k.cnt["ev2"]
                        tT_ev20[qp] = ev20

                        @block.tensor
                        def _(tensor, qp=qp, ptr0=ptr0, ev20=ev20):
                            k.wait_at(tensor, "ve", pmul_ve[2 * qp + 1])
                            k.wait(tensor, "ev2")
                            n = 0
                            for jt in range(NTT):
                                for qtl in range(2):
                                    if n >= 16:
                                        k.wait_at(tensor, "ev2",
                                                  ev20 + (n - 16) // 8 + 1)
                                    slot = n % 16
                                    k.inc(tensor.transpose(
                                        PSPT[:, slot * P:(slot + 1) * P],
                                        PQ[:, qp % 2, qtl,
                                           jt * P:(jt + 1) * P],
                                        IDB[:, :]), "mm")
                                    n += 1

                        @block.vector
                        def _(vector, qp=qp, ptr0=ptr0):
                            for g in range(4):
                                k.wait_at(vector, "mm", ptr0 + (g + 1) * 8)
                                vector.tensor_copy(
                                    PT[:, 4 * g:4 * g + 4, :],
                                    PSPT[:, (g % 2) * 1024:
                                         (g % 2) * 1024 + 1024])
                                k.inc(vector.engine_nop(), "ev2")

                        ptr_mm.append(k.cnt["mm"])

                    def emit_tail_PV(qp, h=h, hpar=hpar):
                        ev20 = tT_ev20[qp]

                        @block.tensor
                        def _(tensor, h=h, hpar=hpar, qp=qp, ev20=ev20):
                            if qp >= 1:
                                k.wait_at(tensor, "ve", os_ve_l[qp - 1])
                            for jt in range(NTT):
                                k.wait_at(tensor, "ev2",
                                          ev20 + min(jt // 4 + 1, 4))
                                k.inc(tensor.matmul(
                                    PPV[:, :], VTH[:, hpar, jt, :],
                                    PT[:, jt, :],
                                    start=(jt == 0), stop=(jt == NTT - 1)),
                                    "mm")

                        pv_mm = k.cnt["mm"]

                        @block.vector
                        def _(vector, qp=qp, pv_mm=pv_mm):
                            k.wait_at(vector, "mm", pv_mm)
                            k.inc(vector.tensor_copy(OSB[:, :], PPV[:, :]),
                                  "ve")

                        os_ve = k.cnt["ve"]
                        os_ve_l.append(os_ve)

                        @block.tensor
                        def _(tensor, qp=qp, os_ve=os_ve):
                            k.wait_at(tensor, "ve", os_ve)
                            for qtl in range(2):
                                k.inc(tensor.transpose(
                                    PSE[:, qtl * P:qtl * P + DH + 1],
                                    OSB[:, qtl * P:(qtl + 1) * P],
                                    IDF[0:DH + 1, 0:DH + 1]), "mm")

                        ep_mm = k.cnt["mm"]

                        @block.vector
                        def _(vector, h=h, qp=qp, ep_mm=ep_mm):
                            k.wait_at(vector, "mm", ep_mm)
                            for qtl in range(2):
                                vector.reciprocal(
                                    RCL[:, qtl:qtl + 1],
                                    PSE[:, qtl * P + DH:qtl * P + DH + 1])
                            for qtl in range(2):
                                qt = 2 * qp + qtl
                                k.inc(vector.scalar_tensor_tensor(
                                    X[:, qt, h * DH:(h + 1) * DH],
                                    PSE[:, qtl * P:qtl * P + DH],
                                    RCL[:, qtl:qtl + 1],
                                    X[:, qt, h * DH:(h + 1) * DH],
                                    ALU.mult, ALU.add), "ve")
                            pv_ve.append(k.cnt["ve"])

                    for qp in range(nqp):
                        band_ld = emit_band(qp)
                        emit_half(qp, 0, band_ld)
                        if _C_SC >= 3 and qp >= 1:
                            emit_tail_T(qp - 1)
                        emit_half(qp, 1, band_ld)
                        if _C_SC >= 3 and qp >= 1:
                            emit_tail_PV(qp - 1)
                        if _C_SC < 3:
                            ptr_mm.append(k.cnt["mm"])
                            pv_ve.append(k.cnt["ve"])
                    if _C_SC >= 3 and nqp:
                        emit_tail_T(nqp - 1)
                        emit_tail_PV(nqp - 1)

                    hd_mm.append(k.cnt["mm"])

        if phases >= 2:
            emit_moe(0)
        if phases >= 3:
            emit_attn_b()
        if phases >= 4:
            emit_moe(1)

        @block.sync
        def _(sync):
            k.wait(sync, "ve")
            k.wait(sync, "se")
            k.wait(sync, "mm")
            dst = AP(out_d, 0, [[D, P], [P * D, NTT], [1, D]])
            k.dinc(sync.dma_start(out=dst, in_=X[:, :, :]), "st")
            k.wait(sync, "st")

    return nc


def _shard_inputs(inputs):
    consts = _host_consts()
    lam = np.ascontiguousarray(
        np.asarray(inputs["L0_attn_a_lambda_log"]).reshape(NCH, P).T)
    wnc = np.ascontiguousarray(
        np.asarray(inputs["L0_attn_a_w_norm"]).reshape(NCH, P).T)
    ubt = np.ascontiguousarray(np.asarray(inputs["L0_attn_b_u_bias"]).T)
    vbt = np.ascontiguousarray(np.asarray(inputs["L0_attn_b_v_bias"]).T)
    common = {
        "w_in": np.asarray(inputs["L0_attn_a_W_in"]),
        "lam_log": lam.astype(np.float32),
        "w_norm_c": wnc.astype(np.float32),
        "w_out": np.asarray(inputs["L0_attn_a_W_out"]),
        "wq": np.asarray(inputs["L0_attn_b_Wq"]),
        "wk": np.asarray(inputs["L0_attn_b_Wk"]),
        "wv": np.asarray(inputs["L0_attn_b_Wv"]),
        "wr": np.asarray(inputs["L0_attn_b_Wr"]),
        "u_bias_t": ubt.astype(np.float32),
        "v_bias_t": vbt.astype(np.float32),
        "gate_w": np.asarray(inputs["L0_ffn_w_gate"]),
        "c_fc": np.asarray(inputs["L0_ffn_c_fc"]),
        "c_proj": np.asarray(inputs["L0_ffn_c_proj"]),
    }
    common.update(consts)
    xfull = np.ascontiguousarray(np.asarray(inputs["L0_attn_a_x"]))
    per_core = []
    for b in range(8):
        m = dict(common)
        m["x_in"] = np.ascontiguousarray(xfull[b])
        pm = np.zeros((NEXP, 1), np.float32)
        pm[:b, 0] = 1.0
        m["prev_mask"] = pm
        per_core.append(m)
    return per_core


def kernel(**inputs):
    in_maps = _shard_inputs(inputs)
    nc = build_program()
    mybir.codegen_inst_isa_subclasses(nc)
    res = run_bass_kernel_spmd(nc, in_maps, core_ids=list(range(8)))
    return np.stack([r["out"] for r in res.results], axis=0)
